# version 01
# baseline (speedup 1.0000x reference)
import math
from contextlib import ExitStack

import numpy as np
import ml_dtypes

import concourse.bass as bass
import concourse.mybir as mybir
from concourse.bass_utils import run_bass_kernel_spmd

F32 = mybir.dt.float32
BF16 = mybir.dt.bfloat16
AF = mybir.ActivationFunctionType
ALU = mybir.AluOpType
AX = mybir.AxisListType.X

BF = ml_dtypes.bfloat16
NCORES = 8
B, T, D = 4, 4096, 1024
TLOC = T // 2
NH, DH, DK = 16, 64, 128
DI = 4096
DN = 2048
DS = 16
DTR = 64
E, HMOE = 8, 2048
PI = float(np.pi)
EPS = 1e-5
NCH = D // 128



class L:

    def __init__(self, name):
        self.name = name
        self.nc = bass.Bass("TRN2", target_bir_lowering=False)
        self.stack = ExitStack()
        self.c = {}
        self.sems = {}

    def din(self, name, shape, dt=F32):
        return self.nc.dram_tensor(name, list(shape), dt, kind="ExternalInput").ap()

    def dout(self, name, shape, dt=F32):
        return self.nc.dram_tensor(name, list(shape), dt, kind="ExternalOutput").ap()

    def dscratch(self, name, shape, dt=F32):
        return self.nc.dram_tensor(name, list(shape), dt, kind="Internal").ap()

    def sb(self, name, shape, dt=F32):
        return self.stack.enter_context(self.nc.sbuf_tensor(name, list(shape), dt))

    def ps(self, name, shape, dt=F32):
        return self.stack.enter_context(self.nc.psum_tensor(name, list(shape), dt))

    def sem(self, *names):
        for n in names:
            self.sems[n] = self.stack.enter_context(self.nc.semaphore(n))
            self.c[n] = 0

    def block(self):
        self.dummy = self.stack.enter_context(self.nc.sbuf_tensor("l_dummy", [1, 16], F32))
        self.blk = self.stack.enter_context(self.nc.Block())

    def mark(self, eng, sname):
        if hasattr(eng, "tensor_copy"):
            i = eng.tensor_copy(self.dummy[:1, 0:8], self.dummy[:1, 8:16])
        else:
            i = eng.copy(self.dummy[:1, 0:8], self.dummy[:1, 8:16])
        self.inc(i, sname)

    def inc(self, inst, sname, n=1):
        inst.then_inc(self.sems[sname], n)
        self.c[sname] += n
        return inst

    def wait(self, eng, sname, cnt=None):
        eng.wait_ge(self.sems[sname], self.c[sname] if cnt is None else cnt)

    def on(self, engine_name, fn):
        getattr(self.blk, engine_name)(fn)

    def close(self):
        self.stack.close()


def _run(lk: L, in_maps):
    lk.close()
    res = run_bass_kernel_spmd(lk.nc, in_maps, core_ids=list(range(NCORES)))
    return res.results


def _bfh(a):
    return np.ascontiguousarray(np.asarray(a).astype(BF))


def _f32h(a):
    return np.ascontiguousarray(np.asarray(a, dtype=np.float32))


def _tile_widths(n, w):
    out = []
    while n > 0:
        out.append(min(w, n))
        n -= out[-1]
    return out



def emit_ln(lk, kit, get_x, put_out, w, sq_full, out_war_sems=(), nchunks=NCH):
    nc = lk.nc

    def s_fn(s):
        if not kit.inited_wait_done:
            lk.wait(s, "gp", 1)
            kit.inited_wait_done = True
        lk.wait(s, "ld")
        lk.wait(s, "ev")
        for c in range(nchunks):
            i = s.activation(sq_full[:, c * w:(c + 1) * w], get_x(c), AF.Square)
        lk.inc(i, "sc")

    lk.on("scalar", s_fn)

    def t_fn(t):
        lk.wait(t, "ev")
        lk.wait(t, "ld")
        ones_c = kit.ones_col_bf if get_x(0).dtype == BF16 else kit.ones_col
        for c in range(nchunks):
            nc.tensor.matmul(kit.p_sum[:1, 0:w], ones_c[:], get_x(c),
                             start=(c == 0), stop=(c == nchunks - 1))
        lk.wait(t, "sc")
        for c in range(nchunks):
            i = nc.tensor.matmul(kit.p_sumsq[:1, 0:w], kit.ones_col[:],
                                 sq_full[:, c * w:(c + 1) * w],
                                 start=(c == 0), stop=(c == nchunks - 1))
        lk.inc(i, "mm")

    lk.on("tensor", t_fn)

    def v_fn(v):
        lk.wait(v, "mm")
        mean = kit.row[:1, 0:w]
        var = kit.row[:1, kit.maxw:kit.maxw + w]
        v.tensor_scalar(mean, kit.p_sum[:1, 0:w], 1.0 / D, 0.0, ALU.mult, ALU.add)
        v.tensor_tensor(var, mean, mean, ALU.mult)
        i = v.scalar_tensor_tensor(var, kit.p_sumsq[:1, 0:w], 1.0 / D, var,
                                   ALU.mult, ALU.subtract)
        lk.inc(i, "ev")

    lk.on("vector", v_fn)

    def s2_fn(s):
        lk.wait(s, "ev")
        var = kit.row[:1, kit.maxw:kit.maxw + w]
        s.activation(var, var, AF.Ln, bias=kit.eps_col[:1])
        i = s.activation(var, var, AF.Exp, scale=kit.half_col[:1])
        lk.inc(i, "sc")

    lk.on("scalar", s2_fn)

    def t2_fn(t):
        lk.wait(t, "sc")
        nc.tensor.matmul(kit.p_rep[:, 0:w], kit.ones_row[:], kit.row[:1, 0:w],
                         start=True, stop=True)
        i = nc.tensor.matmul(kit.p_rep[:, kit.maxw:kit.maxw + w], kit.ones_row[:],
                             kit.row[:1, kit.maxw:kit.maxw + w], start=True, stop=True)
        lk.inc(i, "mm")

    lk.on("tensor", t2_fn)

    def v2_fn(v):
        lk.wait(v, "mm")
        for s_ in out_war_sems:
            lk.wait(v, s_)
        for c in range(nchunks):
            put_out(c, v)
        lk.mark(v, "ev")

    lk.on("vector", v2_fn)


class LNScratch:
    def __init__(self, lk, maxw=512, name="ln"):
        self.lk = lk
        self.maxw = maxw
        self.row = lk.sb(f"{name}_row", [1, 2 * maxw], F32)
        self.p_sum = lk.ps(f"{name}_psum", [1, maxw], F32)
        self.p_sumsq = lk.ps(f"{name}_psumsq", [1, maxw], F32)
        self.p_rep = lk.ps(f"{name}_prep", [128, 2 * maxw], F32)
        self.ones_col = lk.sb(f"{name}_ones_c", [128, 1], F32)
        self.ones_col_bf = lk.sb(f"{name}_ones_cb", [128, 1], BF16)
        self.ones_row = lk.sb(f"{name}_ones_r", [1, 128], F32)
        self.eps_col = lk.sb(f"{name}_eps", [1, 1], F32)
        self.half_col = lk.sb(f"{name}_half", [1, 1], F32)
        self.inited_wait_done = False

    def init_consts(self):
        lk = self.lk

        def g_fn(g):
            g.memset(self.ones_col[:], 1.0)
            g.memset(self.ones_col_bf[:], 1.0)
            g.memset(self.ones_row[:], 1.0)
            g.memset(self.eps_col[:], EPS)
            i = g.memset(self.half_col[:], -0.5)
            lk.inc(i, "gp")

        lk.on("gpsimd", g_fn)



def build_xn():
    NT = TLOC + 1
    lk = L("xn")
    x_in = lk.din("xT", [D, NT])
    g_in = lk.din("ln1g", [128, NCH])
    b_in = lk.din("ln1b", [128, NCH])
    xn_out = lk.dout("xn", [D, NT], BF16)
    x2_out = lk.dout("x2", [D, TLOC], BF16)

    widths = [257] + _tile_widths(TLOC - 256, 512)
    lk.sem("ld", "sc", "mm", "ev", "st", "gp")
    xt = lk.sb("xt", [128, NCH * 512], F32)
    xo = lk.sb("xo", [128, NCH * 512], BF16)
    x2o = lk.sb("x2o", [128, NCH * 512], BF16)
    sq_full = lk.sb("sq", [128, NCH * 512], F32)
    gcol = lk.sb("gcol", [128, NCH], F32)
    bcol = lk.sb("bcol", [128, NCH], F32)
    kit = LNScratch(lk, 512)
    lk.block()
    kit.init_consts()

    def ld0(s):
        i = s.dma_start(out=gcol[:], in_=g_in)
        lk.inc(i, "ld", 16)
        i = s.dma_start(out=bcol[:], in_=b_in)
        lk.inc(i, "ld", 16)

    lk.on("sync", ld0)

    t0 = 0
    for w in widths:
        o = t0

        def ld_fn(s, o=o, w=w):
            lk.wait(s, "ev")
            for c in range(NCH):
                i = s.dma_start(out=xt[:, c * w:(c + 1) * w],
                                in_=x_in[c * 128:(c + 1) * 128, o:o + w])
                lk.inc(i, "ld", 16)

        lk.on("sync", ld_fn)

        def get_x(c, w=w):
            return xt[:, c * w:(c + 1) * w]

        def put_out(c, v, w=w):
            tmp = sq_full[:, c * w:(c + 1) * w]
            v.tensor_tensor(tmp, get_x(c), kit.p_rep[:, 0:w], ALU.subtract)
            v.tensor_tensor(xo[:, c * w:(c + 1) * w], tmp,
                            kit.p_rep[:, 512:512 + w], ALU.mult)

        emit_ln(lk, kit, get_x, put_out, w, sq_full, out_war_sems=("st",))

        def get_x2(c, w=w):
            return xo[:, c * w:(c + 1) * w]

        def put_out2(c, v, w=w):
            tmp = sq_full[:, c * w:(c + 1) * w]
            d = x2o[:, c * w:(c + 1) * w]
            v.tensor_tensor(tmp, get_x2(c), kit.p_rep[:, 0:w], ALU.subtract)
            v.tensor_tensor(tmp, tmp, kit.p_rep[:, 512:512 + w], ALU.mult)
            v.tensor_scalar(d, tmp, gcol[:, c:c + 1], bcol[:, c:c + 1],
                            ALU.mult, ALU.add)

        emit_ln(lk, kit, get_x2, put_out2, w, sq_full, out_war_sems=("st",))

        def st_fn(s, o=o, w=w):
            lk.wait(s, "ev")
            for c in range(NCH):
                i = s.dma_start(out=xn_out[c * 128:(c + 1) * 128, o:o + w],
                                in_=xo[:, c * w:(c + 1) * w])
                lk.inc(i, "st", 16)
            x2src0 = 1 if o == 0 else 0
            for c in range(NCH):
                i = s.dma_start(
                    out=x2_out[c * 128:(c + 1) * 128,
                               max(o - 1, 0):o + w - 1],
                    in_=x2o[:, c * w + x2src0:(c + 1) * w])
                lk.inc(i, "st", 16)

        lk.on("sync", st_fn)
        t0 += w

    def fin(s):
        lk.wait(s, "st")

    lk.on("sync", fin)
    return lk


def host_xn(inputs):
    x = np.asarray(inputs["L0_attn_x"])
    lk = build_xn()
    gi = host_prep_feat_cols(inputs["L0_attn_ln1_g"])
    bi = host_prep_feat_cols(inputs["L0_attn_ln1_b"])
    in_maps = []
    for core in range(NCORES):
        b, j = divmod(core, 2)
        t0 = j * TLOC
        ext = np.zeros((D, TLOC + 1), np.float32)
        ext[:, 1:] = x[b, t0:t0 + TLOC, :].T
        if j == 1:
            ext[:, 0] = x[b, t0 - 1, :]
        in_maps.append({"xT": np.ascontiguousarray(ext), "ln1g": gi, "ln1b": bi})
    res = _run(lk, in_maps)
    return ([np.asarray(r["xn"]) for r in res], [np.asarray(r["x2"]) for r in res])



class PsumPool:

    def __init__(self, lk, nbanks=4, tag="pb"):
        self.lk = lk
        self.n = nbanks
        self.banks = [lk.ps(f"{tag}{i}", [128, 512], F32) for i in range(nbanks)]
        self.free_sems = [f"{tag}f{i}" for i in range(nbanks)]
        lk.sem(*self.free_sems)
        self.uses = [0] * nbanks
        self.rr = 0

    def acquire(self, teng):
        i = self.rr
        self.rr = (self.rr + 1) % self.n
        if self.uses[i] > 0:
            teng.wait_ge(self.lk.sems[self.free_sems[i]], self.uses[i])
        self.uses[i] += 1
        return i, self.banks[i]

    def freed(self, inst, i):
        inst.then_inc(self.lk.sems[self.free_sems[i]], 1)
        self.lk.c[self.free_sems[i]] += 1


def mm_block(lk, pool, get_lhsT, get_rhs, nK, n_list, m_list, evac,
             evac_engine="vector", pre_wait=None):
    nc = lk.nc
    first = True
    for n in n_list:
        for (mo, w) in m_list:
            holder = {}

            def t_fn(t, n=n, mo=mo, w=w, first=first):
                if pre_wait is not None and first:
                    pre_wait(t)
                ib, bank = pool.acquire(t)
                holder["b"] = (ib, bank)
                for kc in range(nK):
                    i = nc.tensor.matmul(bank[:, 0:w], get_lhsT(kc, n),
                                         get_rhs(kc, mo, w),
                                         start=(kc == 0), stop=(kc == nK - 1))
                lk.inc(i, "mm")

            lk.on("tensor", t_fn)
            ib, bank = holder["b"]

            def e_fn(e, n=n, mo=mo, w=w, ib=ib, bank=bank):
                lk.wait(e, "mm")
                last = evac(e, bank, n, mo, w)
                pool.freed(last, ib)

            lk.on(evac_engine, e_fn)
            first = False


def emit_barrier(lk, engine_name, sname="ev"):
    def f(e):
        lk.mark(e, sname)

    lk.on(engine_name, f)


MT512 = [(o, 512) for o in range(0, TLOC, 512)]


def load_weight_bf16(lk, g, dst_sb, hbm_ap, nK, ncols, sem="wt"):
    for kc in range(nK):
        i = g.dma_start(out=dst_sb[:, kc * ncols:(kc + 1) * ncols],
                        in_=hbm_ap[kc * 128:(kc + 1) * 128, :])
        lk.inc(i, sem, 16)



def build_rwkv(dbg=False, tiles=None):
    lk = L("rwkv")
    dbgo = {}
    if dbg:
        for nm, dt in [("xr", BF16), ("r", BF16), ("ka", BF16), ("rw", F32)]:
            dbgo[nm] = lk.dout("dbg_" + nm, [D, 512], dt)
        dbgo["xn"] = lk.dout("dbg_xn", [D, TLOC + 1], BF16)
    xn_in = lk.din("xn", [D, TLOC + 1], BF16)
    wrt = lk.din("wrt", [D, D])
    wkt = lk.din("wkt", [D, D])
    wvt = lk.din("wvt", [D, D])
    mur_in = lk.din("mur", [128, NCH])
    muk_in = lk.din("muk", [128, NCH])
    g_out = lk.dout("g", [D, TLOC], BF16)

    lk.sem("ld", "wt", "mm", "sc", "ev", "st")
    NT = TLOC + 2
    xn = lk.sb("s_xn", [128, NCH * NT], BF16)
    wr = lk.sb("wr", [128, NCH * D], BF16)
    wk = lk.sb("wk", [128, NCH * D], BF16)
    wv = lk.sb("wv", [128, NCH * D], BF16)
    mur = lk.sb("s_mur", [128, NCH], F32)
    muk = lk.sb("s_muk", [128, NCH], F32)
    dsc = lk.sb("dsc", [128, NCH * 512], F32)
    xr = lk.sb("xr", [128, NCH * 512], BF16)
    xk = lk.sb("xk", [128, NCH * 512], BF16)
    rbf = lk.sb("rbf", [128, NCH * 512], BF16)
    kabf = lk.sb("kabf", [128, NCH * 512], BF16)
    ksc = lk.sb("ksc", [128, 512], BF16)
    rwsc = lk.sb("rwsc", [128, NCH * 512], F32)
    gbf = lk.sb("gbf", [128, NCH * 512], BF16)
    pool = PsumPool(lk, 4)
    lk.block()

    def ld_fn(s):
        for c in range(NCH):
            i = s.dma_start(out=xn[:, c * NT:c * NT + (TLOC + 1)],
                            in_=xn_in[c * 128:(c + 1) * 128, :])
            lk.inc(i, "ld", 16)
        i = s.dma_start(out=mur[:], in_=mur_in)
        lk.inc(i, "ld", 16)
        i = s.dma_start(out=muk[:], in_=muk_in)
        lk.inc(i, "ld", 16)

    lk.on("sync", ld_fn)

    def wt_fn(g):
        load_weight_bf16(lk, g, wr, wrt, NCH, D)
        load_weight_bf16(lk, g, wk, wkt, NCH, D)
        load_weight_bf16(lk, g, wv, wvt, NCH, D)

    lk.on("gpsimd", wt_fn)

    for (mo, w) in (tiles or MT512):
        mm_before_mix = lk.c["mm"]

        def v_mix(v, mo=mo, w=w, mmw=mm_before_mix):
            lk.wait(v, "ld")
            lk.wait(v, "mm", mmw)
            for c in range(NCH):
                cur = xn[:, c * NT + mo + 1: c * NT + mo + 1 + w]
                prv = xn[:, c * NT + mo: c * NT + mo + w]
                ds = dsc[:, c * 512:c * 512 + w]
                v.tensor_tensor(ds, cur, prv, ALU.subtract)
                v.scalar_tensor_tensor(xr[:, c * w:(c + 1) * w], ds,
                                       mur[:, c:c + 1], prv, ALU.mult, ALU.add)
                i = v.scalar_tensor_tensor(xk[:, c * w:(c + 1) * w], ds,
                                           muk[:, c:c + 1], prv, ALU.mult, ALU.add)
            lk.inc(i, "ev")

        lk.on("vector", v_mix)
        mix_ev = lk.c["ev"]

        def s_war(s, c=lk.c["ev"]):
            lk.wait(s, "ev", c)

        lk.on("scalar", s_war)

        def evac_r(e, bank, n, mo2, w2):
            return e.activation(rbf[:, n * 512:n * 512 + w2], bank[:, 0:w2], AF.Sigmoid)

        mm_block(lk, pool,
                 get_lhsT=lambda kc, n: wr[:, kc * D + n * 128: kc * D + (n + 1) * 128],
                 get_rhs=lambda kc, mo2, w2, w=w: xr[:, kc * w: kc * w + w2],
                 nK=NCH, n_list=range(NCH), m_list=[(0, w)],
                 evac=evac_r, evac_engine="scalar",
                 pre_wait=lambda t, me=mix_ev: (lk.wait(t, "wt", 16 * 8),
                                                lk.wait(t, "ev", me)))
        r_done_sc = lk.c["sc"]

        def evac_k(e, bank, n, mo2, w2):
            e.tensor_scalar(ksc[:, 0:w2], bank[:, 0:w2], 0.0, 0.0, ALU.max, ALU.add)
            return e.tensor_tensor(kabf[:, n * 512:n * 512 + w2], ksc[:, 0:w2],
                                   bank[:, 0:w2], ALU.mult)

        mm_block(lk, pool,
                 get_lhsT=lambda kc, n: wk[:, kc * D + n * 128: kc * D + (n + 1) * 128],
                 get_rhs=lambda kc, mo2, w2, w=w: xk[:, kc * w: kc * w + w2],
                 nK=NCH, n_list=range(NCH), m_list=[(0, w)],
                 evac=evac_k, evac_engine="vector",
                 pre_wait=lambda t: lk.wait(t, "wt", 16 * 16))
        emit_barrier(lk, "vector", "ev")
        kact_ev = lk.c["ev"]

        def evac_vo(e, bank, n, mo2, w2):
            return e.tensor_tensor(rwsc[:, n * 512:n * 512 + w2],
                                   rbf[:, n * 512:n * 512 + w2],
                                   bank[:, 0:w2], ALU.mult)

        mm_block(lk, pool,
                 get_lhsT=lambda kc, n: wv[:, kc * D + n * 128: kc * D + (n + 1) * 128],
                 get_rhs=lambda kc, mo2, w2, w=w: kabf[:, kc * w: kc * w + w2],
                 nK=NCH, n_list=range(NCH), m_list=[(0, w)],
                 evac=evac_vo, evac_engine="vector",
                 pre_wait=lambda t, ke=kact_ev: (lk.wait(t, "wt", 16 * 24),
                                                 lk.wait(t, "ev", ke)))
        emit_barrier(lk, "vector", "ev")
        vo_ev = lk.c["ev"]

        def s_silu(s, mo=mo, w=w, ve=vo_ev):
            lk.wait(s, "ev", ve)
            lk.wait(s, "st")
            for c in range(NCH):
                i = s.activation(gbf[:, c * 512:c * 512 + w],
                                 rwsc[:, c * 512:c * 512 + w], AF.Silu)
            lk.inc(i, "sc")

        lk.on("scalar", s_silu)

        def st_fn(s, mo=mo, w=w, sc=lk.c["sc"]):
            lk.wait(s, "sc", sc)
            for c in range(NCH):
                i = s.dma_start(out=g_out[c * 128:(c + 1) * 128, mo:mo + w],
                                in_=gbf[:, c * 512:c * 512 + w])
                lk.inc(i, "st", 16)
            if dbg and mo == 0:
                for c in range(NCH):
                    i = s.dma_start(out=dbgo["xn"][c * 128:(c + 1) * 128, :],
                                    in_=xn[:, c * NT:c * NT + (TLOC + 1)])
                    lk.inc(i, "st", 16)
                for nm, tile in [("xr", xr), ("r", rbf), ("ka", kabf), ("rw", rwsc)]:
                    for c in range(NCH):
                        i = s.dma_start(out=dbgo[nm][c * 128:(c + 1) * 128, :],
                                        in_=tile[:, c * 512:(c + 1) * 512])
                        lk.inc(i, "st", 16)

        lk.on("sync", st_fn)

    def fin(s):
        lk.wait(s, "st")

    lk.on("sync", fin)
    return lk


def host_prep_feat_cols(vec):
    v = np.asarray(vec, np.float32)
    return np.ascontiguousarray(v.reshape(-1, 128).T)


def host_rwkv(inputs, xn_list, dbg=False, tiles=None):
    lk = build_rwkv(dbg, tiles)
    wrt = _f32h(np.asarray(inputs["L0_ffn_W_r"]).T)
    wkt = _f32h(np.asarray(inputs["L0_ffn_W_k"]).T)
    wvt = _f32h(np.asarray(inputs["L0_ffn_W_v"]).T)
    mur = host_prep_feat_cols(inputs["L0_ffn_mu_r"])
    muk = host_prep_feat_cols(inputs["L0_ffn_mu_k"])
    in_maps = []
    for core in range(NCORES):
        in_maps.append({"xn": np.ascontiguousarray(xn_list[core]),
                        "wrt": wrt, "wkt": wkt, "wvt": wvt,
                        "mur": mur, "muk": muk})
    res = _run(lk, in_maps)
    if dbg:
        return res
    return [np.asarray(r["g"]) for r in res]



def build_perf_kv(nheads=NH):
    lk = L("perfkv")
    x2_in = lk.din("x2", [D, TLOC], BF16)
    wq_in = lk.din("wq", [D, D]); wk_in = lk.din("wk", [D, D]); wv_in = lk.din("wv", [D, D])
    bq_in = lk.din("bq", [128, NCH]); bk_in = lk.din("bk", [128, NCH])
    bv_in = lk.din("bv", [1, D])
    proj_in = lk.din("proj", [DH, DK])
    qf_out = lk.dout("qf", [D, TLOC], BF16)
    kvp_out = lk.dout("kvp", [NH * DH, 2 * DK])

    lk.sem("ld", "wt", "mm", "sc", "ev", "st", "gp")
    x2 = lk.sb("s_x2", [128, NCH * TLOC], BF16)
    wq = lk.sb("s_wq", [128, NCH * D], BF16)
    wk = lk.sb("s_wk", [128, NCH * D], BF16)
    wv = lk.sb("s_wv", [128, NCH * D], BF16)
    qf = lk.sb("s_qf", [128, NCH * TLOC], BF16)
    kf = lk.sb("s_kf", [128, NCH * TLOC], BF16)
    v_full = lk.sb("s_v", [128, 16 * D], BF16)
    proj = lk.sb("s_proj", [128, 2 * DK], BF16)
    bq = lk.sb("s_bq", [128, NCH], F32)
    bk = lk.sb("s_bk", [128, NCH], F32)
    bvr = lk.sb("s_bvr", [1, D], F32)
    bv_rep = lk.sb("s_bvrep", [128, D], BF16)
    onesr = lk.sb("s_onesr", [1, 128], F32)
    wsc = lk.sb("s_wsc", [128, 128], F32)
    csc = lk.sb("s_csc", [128, 128], F32)
    aw = lk.sb("s_aw", [128, 128], F32)
    kp = lk.sb("s_kp", [128, 2 * DK], BF16)
    kvsb = lk.sb("s_kvsb", [128, 4 * 512], F32)
    cst = lk.sb("s_cst", [128, 2], F32)
    pool = PsumPool(lk, 2)
    p_pk = lk.ps("p_pk", [128, DK], F32)
    kv4 = [lk.ps(f"p_kv{i}", [128, 512], F32) for i in range(4)]
    lk.block()

    def g0(g):
        g.memset(cst[:, 0:1], PI / 2)
        g.memset(cst[:, 1:2], -1.0)
        i = g.memset(onesr[:], 1.0)
        lk.inc(i, "gp")
        load_weight_bf16(lk, g, wq, wq_in, NCH, D)
        load_weight_bf16(lk, g, wk, wk_in, NCH, D)
        load_weight_bf16(lk, g, wv, wv_in, NCH, D)
        g.memset(proj[:], 0.0)
        i = g.dma_start(out=proj[0:64, 0:DK], in_=proj_in)
        lk.inc(i, "wt", 16)
        i = g.dma_start(out=proj[64:128, DK:2 * DK], in_=proj_in)
        lk.inc(i, "wt", 16)

    lk.on("gpsimd", g0)

    def ld0(s):
        for c in range(NCH):
            i = s.dma_start(out=x2[:, c * TLOC:(c + 1) * TLOC],
                            in_=x2_in[c * 128:(c + 1) * 128, :])
            lk.inc(i, "ld", 16)
        for t, srcp in [(bq, bq_in), (bk, bk_in), (bvr, bv_in)]:
            i = s.dma_start(out=t[:], in_=srcp)
            lk.inc(i, "ld", 16)

    lk.on("sync", ld0)

    def t_bv(t):
        lk.wait(t, "ld")
        lk.wait(t, "gp")
        ib0, b0 = pool.acquire(t)
        i = lk.nc.tensor.matmul(b0[:, 0:512], onesr[:], bvr[:1, 0:512],
                                start=True, stop=True)
        lk.inc(i, "mm")
        ib1, b1 = pool.acquire(t)
        i = lk.nc.tensor.matmul(b1[:, 0:512], onesr[:], bvr[:1, 512:1024],
                                start=True, stop=True)
        lk.inc(i, "mm")
        t_bv.banks = [(ib0, b0), (ib1, b1)]

    lk.on("tensor", t_bv)

    def v_bv(v):
        lk.wait(v, "mm")
        (ib0, b0), (ib1, b1) = t_bv.banks
        i = v.tensor_copy(bv_rep[:, 0:512], b0[:, 0:512])
        pool.freed(i, ib0)
        i = v.tensor_copy(bv_rep[:, 512:1024], b1[:, 0:512])
        pool.freed(i, ib1)
        lk.mark(v, "ev")

    lk.on("vector", v_bv)

    for (wt_sb, b_sb, dst, wt_cnt) in [(wq, bq, qf, 16 * 8), (wk, bk, kf, 16 * 16)]:
        def evac_qk(e, bank, n, mo, w, b_sb=b_sb, dst=dst):
            return e.tensor_scalar(dst[:, n * TLOC + mo:n * TLOC + mo + w],
                                   bank[:, 0:w], b_sb[:, n:n + 1], 0.0,
                                   ALU.add, ALU.add)

        mm_block(lk, pool,
                 get_lhsT=lambda kc, n, wt_sb=wt_sb: wt_sb[:, kc * D + n * 128:kc * D + (n + 1) * 128],
                 get_rhs=lambda kc, mo, w: x2[:, kc * TLOC + mo:kc * TLOC + mo + w],
                 nK=NCH, n_list=range(NCH), m_list=MT512,
                 evac=evac_qk, evac_engine="vector",
                 pre_wait=lambda t, wc=wt_cnt: (lk.wait(t, "wt", wc), lk.wait(t, "ld")))
    emit_barrier(lk, "vector", "ev")
    kf_ev = lk.c["ev"]

    def evac_v(e, bank, n, mo, w):
        return e.tensor_tensor(v_full[:, n * D + mo:n * D + mo + w], bank[:, 0:w],
                               bv_rep[:, mo:mo + w], ALU.add)

    mm_block(lk, pool,
             get_lhsT=lambda kc, n: x2[:, kc * TLOC + n * 128:kc * TLOC + n * 128 + 128],
             get_rhs=lambda kc, mo, w: wv[:, kc * D + mo:kc * D + mo + w],
             nK=NCH, n_list=range(16), m_list=[(0, 512), (512, 512)],
             evac=evac_v, evac_engine="vector",
             pre_wait=lambda t: (lk.wait(t, "wt", 16 * 24), lk.wait(t, "ev")))
    emit_barrier(lk, "vector", "ev")

    for h in range(nheads):
        ch, po = h // 2, (h % 2) * 64
        q4, ph, mh = h // 4, (h % 4) // 2, h % 2
        for lc in range(16):
            lo = lc * 128

            def t_pk(t, ch=ch, po=po, lo=lo):
                lk.wait(t, "ev")
                i = lk.nc.tensor.matmul(p_pk[:, 0:DK],
                                        kf[:, ch * TLOC + lo:ch * TLOC + lo + 128],
                                        proj[:, (po // 64) * DK:(po // 64) * DK + DK],
                                        start=True, stop=True)
                lk.inc(i, "mm")

            lk.on("tensor", t_pk)

            def v_wrap(v):
                lk.wait(v, "mm")
                lk.wait(v, "sc")
                v.tensor_scalar(wsc[:], p_pk[:, 0:DK], 0.125, 0.0, ALU.mult, ALU.add)
                v.tensor_scalar(csc[:], wsc[:], PI, -2 * PI, ALU.is_gt, ALU.mult)
                v.tensor_tensor(wsc[:], wsc[:], csc[:], ALU.add)
                v.tensor_scalar(csc[:], wsc[:], -PI, 2 * PI, ALU.is_lt, ALU.mult)
                i = v.tensor_tensor(wsc[:], wsc[:], csc[:], ALU.add)
                lk.inc(i, "ev")

            lk.on("vector", v_wrap)

            def s_sin(s):
                lk.wait(s, "ev")
                s.activation(aw[:], wsc[:], AF.Abs)
                s.activation(kp[:, DK:2 * DK], wsc[:], AF.Sin)
                i = s.activation(kp[:, 0:DK], aw[:], AF.Sin,
                                 bias=cst[:, 0:1], scale=cst[:, 1:2])
                lk.inc(i, "sc")

            lk.on("scalar", s_sin)

            def t_kv(t, h=h, lc=lc, q4=q4, ph=ph, mh=mh):
                lk.wait(t, "sc")
                i = lk.nc.tensor.matmul(
                    kv4[q4][ph * 64:ph * 64 + 64, mh * 256:mh * 256 + 256],
                    v_full[:, lc * D + h * 64:lc * D + (h + 1) * 64], kp[:, 0:256],
                    start=(lc == 0), stop=(lc == 15))
                if lc == 15:
                    lk.inc(i, "mm")

            lk.on("tensor", t_kv)

    def v_kv(v):
        lk.wait(v, "mm")
        for q4 in range(4):
            i = v.tensor_copy(kvsb[:, q4 * 512:(q4 + 1) * 512], kv4[q4][:, 0:512])
        lk.mark(v, "ev")

    lk.on("vector", v_kv)

    def st0(s):
        lk.wait(s, "ev")
        for c in range(NCH):
            i = s.dma_start(out=qf_out[c * 128:(c + 1) * 128, :],
                            in_=qf[:, c * TLOC:(c + 1) * TLOC])
            lk.inc(i, "st", 16)
        for h in range(NH):
            q4, ph, mh = h // 4, (h % 4) // 2, h % 2
            i = s.dma_start(out=kvp_out[h * DH:(h + 1) * DH, :],
                            in_=kvsb[ph * 64:ph * 64 + 64,
                                     q4 * 512 + mh * 256:q4 * 512 + mh * 256 + 256])
            lk.inc(i, "st", 16)
        s.wait_ge(lk.sems["st"], lk.c["st"])

    lk.on("sync", st0)
    return lk


def host_perf_kv(inputs, x2_list, nheads=NH):
    lk = build_perf_kv(nheads)
    wqh = _f32h(inputs["L0_attn_wq"]); wkh = _f32h(inputs["L0_attn_wk"])
    wvh = _f32h(inputs["L0_attn_wv"])
    bqh = host_prep_feat_cols(inputs["L0_attn_bq"])
    bkh = host_prep_feat_cols(inputs["L0_attn_bk"])
    bvh = _f32h(np.asarray(inputs["L0_attn_bv"]).reshape(1, D))
    projh = _f32h(inputs["L0_attn_proj_mat"])
    in_maps = [{"x2": np.ascontiguousarray(x2_list[c]), "wq": wqh, "wk": wkh,
                "wv": wvh, "bq": bqh, "bk": bkh, "bv": bvh, "proj": projh}
               for c in range(NCORES)]
    res = _run(lk, in_maps)
    return ([np.asarray(r["qf"]) for r in res], [np.asarray(r["kvp"]) for r in res])



def build_attn():
    lk = L("attn")
    qf_in = lk.din("qf", [D, TLOC], BF16)
    kvt_in = lk.din("kvT2", [2 * 2 * DK, D])
    xn_in = lk.din("xnr", [D, TLOC + 1], BF16)
    wo_in = lk.din("wo", [D, D])
    bo_in = lk.din("bo", [128, NCH])
    proj_in = lk.din("proj", [DH, DK])
    xs_out = lk.dout("xs", [D, TLOC], BF16)

    lk.sem("ld", "wt", "mm", "sc", "ev", "st", "gp")
    qf = lk.sb("s_qf", [128, NCH * TLOC], BF16)
    xn = lk.sb("s_xn", [128, NCH * TLOC], BF16)
    kvt2 = lk.sb("s_kvt2", [128, 4 * D], F32)
    kvt = lk.sb("s_kvt", [128, 2 * D], BF16)
    wo = lk.sb("s_wo", [128, NCH * D], BF16)
    bo = lk.sb("s_bo", [128, NCH], F32)
    proj = lk.sb("s_proj", [128, 2 * DK], BF16)
    wsc = lk.sb("s_wsc", [128, 512], F32)
    csc = lk.sb("s_csc", [128, 512], F32)
    aw = lk.sb("s_aw", [128, 512], F32)
    qp = lk.sb("s_qp", [128, 2 * 512], BF16)
    attn_sb = lk.sb("s_attn", [128, NCH * 512], BF16)
    xs_sb = lk.sb("s_xs", [128, NCH * TLOC], BF16)
    cst = lk.sb("s_cst", [128, 2], F32)
    pool = PsumPool(lk, 2)
    p_pq = lk.ps("p_pq", [128, 512], F32)
    p_at = lk.ps("p_at", [64, 512], F32)
    lk.block()

    def g0(g):
        g.memset(cst[:, 0:1], PI / 2)
        g.memset(cst[:, 1:2], -1.0)
        g.memset(proj[:], 0.0)
        i = g.dma_start(out=proj[0:64, 0:DK], in_=proj_in)
        lk.inc(i, "wt", 16)
        i = g.dma_start(out=proj[64:128, DK:2 * DK], in_=proj_in)
        lk.inc(i, "wt", 16)
        load_weight_bf16(lk, g, wo, wo_in, NCH, D)

    lk.on("gpsimd", g0)

    def ld0(s):
        for c in range(NCH):
            i = s.dma_start(out=qf[:, c * TLOC:(c + 1) * TLOC],
                            in_=qf_in[c * 128:(c + 1) * 128, :])
            lk.inc(i, "ld", 16)
        for c in range(NCH):
            i = s.dma_start(out=xn[:, c * TLOC:(c + 1) * TLOC],
                            in_=xn_in[c * 128:(c + 1) * 128, 1:TLOC + 1])
            lk.inc(i, "ld", 16)
        for c in range(4):
            i = s.dma_start(out=kvt2[:, c * D:(c + 1) * D],
                            in_=kvt_in[c * 128:(c + 1) * 128, :])
            lk.inc(i, "ld", 16)
        i = s.dma_start(out=bo[:], in_=bo_in)
        lk.inc(i, "ld", 16)

    lk.on("sync", ld0)

    def v_kv(v):
        lk.wait(v, "ld")
        for mc in range(2):
            a = kvt2[:, mc * D:(mc + 1) * D]
            b = kvt2[:, (2 + mc) * D:(3 + mc) * D]
            v.tensor_tensor(a, a, b, ALU.add)
            i = v.tensor_scalar(kvt[:, mc * D:(mc + 1) * D], a, 1.0 / 128.0, 0.0,
                                ALU.mult, ALU.add)
        lk.mark(v, "ev")

    lk.on("vector", v_kv)

    for (mo, w) in MT512:
        for h in range(NH):
            ch, po = h // 2, (h % 2) * 64

            def t_pq(t, ch=ch, po=po, mo=mo, w=w):
                lk.wait(t, "ev")
                lk.wait(t, "wt", 16 * 2)
                i = lk.nc.tensor.matmul(p_pq[:, 0:w],
                                        proj[:, (po // 64) * DK:(po // 64) * DK + DK],
                                        qf[:, ch * TLOC + mo:ch * TLOC + mo + w],
                                        start=True, stop=True)
                lk.inc(i, "mm")

            lk.on("tensor", t_pq)

            def v_wrap(v, w=w):
                lk.wait(v, "mm")
                lk.wait(v, "sc")
                v.tensor_scalar(wsc[:, 0:w], p_pq[:, 0:w], 0.125, 0.0, ALU.mult, ALU.add)
                v.tensor_scalar(csc[:, 0:w], wsc[:, 0:w], PI, -2 * PI, ALU.is_gt, ALU.mult)
                v.tensor_tensor(wsc[:, 0:w], wsc[:, 0:w], csc[:, 0:w], ALU.add)
                v.tensor_scalar(csc[:, 0:w], wsc[:, 0:w], -PI, 2 * PI, ALU.is_lt, ALU.mult)
                i = v.tensor_tensor(wsc[:, 0:w], wsc[:, 0:w], csc[:, 0:w], ALU.add)
                lk.inc(i, "ev")

            lk.on("vector", v_wrap)

            def s_sin(s, w=w):
                lk.wait(s, "ev")
                s.activation(aw[:, 0:w], wsc[:, 0:w], AF.Abs)
                s.activation(qp[:, 512:512 + w], wsc[:, 0:w], AF.Sin)
                i = s.activation(qp[:, 0:w], aw[:, 0:w], AF.Sin,
                                 bias=cst[:, 0:1], scale=cst[:, 1:2])
                lk.inc(i, "sc")

            lk.on("scalar", s_sin)

            def t_at(t, h=h, w=w):
                lk.wait(t, "sc")
                lk.nc.tensor.matmul(p_at[:, 0:w], kvt[:, 0 * D + h * 64:0 * D + h * 64 + 64],
                                    qp[:, 0:w], start=True, stop=False)
                i = lk.nc.tensor.matmul(p_at[:, 0:w], kvt[:, 1 * D + h * 64:1 * D + h * 64 + 64],
                                        qp[:, 512:512 + w], start=False, stop=True)
                lk.inc(i, "mm")

            lk.on("tensor", t_at)

            def v_at(v, h=h, ch=ch, po=po, w=w):
                lk.wait(v, "mm")
                i = v.tensor_copy(attn_sb[po:po + 64, ch * 512:ch * 512 + w], p_at[:, 0:w])
                lk.inc(i, "ev")

            lk.on("vector", v_at)

        def evac_o(e, bank, n, mo2, w2, mo=mo):
            return e.scalar_tensor_tensor(xs_sb[:, n * TLOC + mo:n * TLOC + mo + w2],
                                          bank[:, 0:w2], bo[:, n:n + 1],
                                          xn[:, n * TLOC + mo:n * TLOC + mo + w2],
                                          ALU.add, ALU.add)

        mm_block(lk, pool,
                 get_lhsT=lambda kc, n: wo[:, kc * D + n * 128:kc * D + (n + 1) * 128],
                 get_rhs=lambda kc, mo2, w2: attn_sb[:, kc * 512:kc * 512 + w2],
                 nK=NCH, n_list=range(NCH), m_list=[(0, w)],
                 evac=evac_o, evac_engine="vector",
                 pre_wait=lambda t, c=lk.c["ev"]: (lk.wait(t, "wt"), lk.wait(t, "ev", c)))
        emit_barrier(lk, "vector", "ev")

    def st0(s):
        lk.wait(s, "ev")
        for c in range(NCH):
            i = s.dma_start(out=xs_out[c * 128:(c + 1) * 128, :],
                            in_=xs_sb[:, c * TLOC:(c + 1) * TLOC])
            lk.inc(i, "st", 16)
        s.wait_ge(lk.sems["st"], lk.c["st"])

    lk.on("sync", st0)
    return lk


def host_attn(inputs, qf_list, kvp_list, xn_list):
    lk = build_attn()
    woh = _f32h(inputs["L0_attn_wo"])
    boh = host_prep_feat_cols(inputs["L0_attn_bo"])
    projh = _f32h(inputs["L0_attn_proj_mat"])
    in_maps = []
    for core in range(NCORES):
        pair = core ^ 1
        kvt2 = np.concatenate([np.asarray(kvp_list[core]).T,
                               np.asarray(kvp_list[pair]).T], axis=0)
        in_maps.append({"qf": np.ascontiguousarray(qf_list[core]),
                        "kvT2": _f32h(kvt2),
                        "xnr": np.ascontiguousarray(xn_list[core]),
                        "wo": woh, "bo": boh, "proj": projh})
    res = _run(lk, in_maps)
    return [np.asarray(r["xs"]) for r in res]



def build_ln2():
    lk = L("ln2")
    xs_in = lk.din("xs", [D, TLOC], BF16)
    g_in = lk.din("g2", [128, NCH])
    b_in = lk.din("b2", [128, NCH])
    out = lk.dout("x2b", [D, TLOC], BF16)
    lk.sem("ld", "sc", "mm", "ev", "st", "gp")
    xt = lk.sb("s_xt", [128, NCH * 512], BF16)
    xo = lk.sb("s_xo", [128, NCH * 512], BF16)
    sq_full = lk.sb("s_sq", [128, NCH * 512], F32)
    tsc = lk.sb("s_tsc", [128, 512], F32)
    gcol = lk.sb("s_gcol", [128, NCH], F32)
    bcol = lk.sb("s_bcol", [128, NCH], F32)
    kit = LNScratch(lk, 512)
    lk.block()
    kit.init_consts()

    def ld0(s):
        i = s.dma_start(out=gcol[:], in_=g_in)
        lk.inc(i, "ld", 16)
        i = s.dma_start(out=bcol[:], in_=b_in)
        lk.inc(i, "ld", 16)

    lk.on("sync", ld0)

    for (mo, w) in MT512:
        def ld_fn(s, mo=mo, w=w):
            lk.wait(s, "ev")
            for c in range(NCH):
                i = s.dma_start(out=xt[:, c * w:(c + 1) * w],
                                in_=xs_in[c * 128:(c + 1) * 128, mo:mo + w])
                lk.inc(i, "ld", 16)

        lk.on("sync", ld_fn)

        def get_x(c, w=w):
            return xt[:, c * w:(c + 1) * w]

        def put_out(c, v, w=w):
            v.tensor_tensor(tsc[:, 0:w], get_x(c), kit.p_rep[:, 0:w], ALU.subtract)
            v.tensor_tensor(tsc[:, 0:w], tsc[:, 0:w], kit.p_rep[:, 512:512 + w], ALU.mult)
            v.tensor_scalar(xo[:, c * w:(c + 1) * w], tsc[:, 0:w],
                            gcol[:, c:c + 1], bcol[:, c:c + 1], ALU.mult, ALU.add)

        emit_ln(lk, kit, get_x, put_out, w, sq_full, out_war_sems=("st",))

        def st_fn(s, mo=mo, w=w):
            lk.wait(s, "ev")
            for c in range(NCH):
                i = s.dma_start(out=out[c * 128:(c + 1) * 128, mo:mo + w],
                                in_=xo[:, c * w:(c + 1) * w])
                lk.inc(i, "st", 16)

        lk.on("sync", st_fn)

    def fin(s):
        lk.wait(s, "st")

    lk.on("sync", fin)
    return lk


def host_ln2(inputs, xs_list):
    lk = build_ln2()
    gi = host_prep_feat_cols(inputs["L0_attn_ln2_g"])
    bi = host_prep_feat_cols(inputs["L0_attn_ln2_b"])
    res = _run(lk, [{"xs": np.ascontiguousarray(xs_list[c]), "g2": gi, "b2": bi}
                    for c in range(NCORES)])
    return [np.asarray(r["x2b"]) for r in res]



def build_ffn(nmt=4, np2=2):
    NI = DI // 128
    lk = L("ffn")
    x2b_in = lk.din("x2b", [D, TLOC], BF16)
    xs_in = lk.din("xs", [D, TLOC], BF16)
    w1_in = lk.din("w1", [D, DI])
    b1_in = lk.din("b1", [128, NI])
    w2_in = lk.din("w2", [DI, D])
    b2_in = lk.din("b2", [128, NCH])
    s_out = lk.dout("sout", [D, TLOC], BF16)

    lk.sem("ld", "wt", "w2t", "rf", "mm", "sc", "ev", "st", "gp")
    x2b = lk.sb("s_x2b", [128, NCH * TLOC], BF16)
    xs = lk.sb("s_xs", [128, NCH * TLOC], BF16)
    w1 = lk.sb("s_w1", [128, NCH * DI], BF16)
    ring = lk.sb("s_ring", [128, 2 * D], BF16)
    b1c = lk.sb("s_b1c", [128, NI], F32)
    b2c = lk.sb("s_b2c", [128, NCH], F32)
    h = lk.sb("s_h", [128, NI * 512], BF16)
    sbuf_s = lk.sb("s_s", [128, NCH * TLOC], BF16)
    pool = PsumPool(lk, 2)
    pw2 = [lk.ps(f"p_w2{i}", [128, 512], F32) for i in range(4)]
    lk.block()

    def g0(g):
        load_weight_bf16(lk, g, w1, w1_in, NCH, DI)

    lk.on("gpsimd", g0)

    def ld0(s):
        for c in range(NCH):
            i = s.dma_start(out=x2b[:, c * TLOC:(c + 1) * TLOC],
                            in_=x2b_in[c * 128:(c + 1) * 128, :])
            lk.inc(i, "ld", 16)
        for c in range(NCH):
            i = s.dma_start(out=xs[:, c * TLOC:(c + 1) * TLOC],
                            in_=xs_in[c * 128:(c + 1) * 128, :])
            lk.inc(i, "ld", 16)
        i = s.dma_start(out=b1c[:], in_=b1_in)
        lk.inc(i, "ld", 16)
        i = s.dma_start(out=b2c[:], in_=b2_in)
        lk.inc(i, "ld", 16)

    lk.on("sync", ld0)

    ring_loads = [0]
    for (mo, w) in MT512[:nmt]:
        def evac_h(e, bank, n, mo2, w2, w=w):
            return e.tensor_scalar(h[:, n * 512:n * 512 + w2], bank[:, 0:w2],
                                   b1c[:, n:n + 1], 0.0, ALU.add, ALU.max)

        hw_ev = lk.c["ev"]
        mm_block(lk, pool,
                 get_lhsT=lambda kc, n: w1[:, kc * DI + n * 128:kc * DI + (n + 1) * 128],
                 get_rhs=lambda kc, mo2, w2, mo=mo: x2b[:, kc * TLOC + mo:kc * TLOC + mo + w2],
                 nK=NCH, n_list=range(NI), m_list=[(0, w)],
                 evac=evac_h, evac_engine="vector",
                 pre_wait=lambda t, c=lk.c["mm"]: (lk.wait(t, "wt"), lk.wait(t, "ld"),
                                                   lk.wait(t, "mm", c)))
        emit_barrier(lk, "vector", "ev")
        h_ev = lk.c["ev"]

        for p2 in range(np2):
            ngrp = [p2 * 4 + i for i in range(4)]

            def g_ring(g, mo=mo, p2=p2):
                for kc in range(NI):
                    slot = kc % 2
                    uses = ring_loads[0]
                    if uses >= 2:
                        need = uses - 1 - (uses - 1) // NI
                        g.wait_ge(lk.sems["rf"], need)
                    i = g.dma_start(out=ring[:, slot * D:(slot + 1) * D],
                                    in_=w2_in[kc * 128:(kc + 1) * 128, :])
                    lk.inc(i, "w2t", 16)
                    ring_loads[0] += 1

            lk.on("gpsimd", g_ring)

            def t_w2(t, mo=mo, w=w, p2=p2, ec=lk.c["ev"], base=ring_loads[0] - NI,
                     ngrp=tuple(ngrp)):
                lk.wait(t, "ev", ec)
                for kc in range(NI):
                    slot = kc % 2
                    t.wait_ge(lk.sems["w2t"], 16 * (base + kc + 1))
                    for ni, n in enumerate(ngrp):
                        i = lk.nc.tensor.matmul(
                            pw2[ni][:, 0:w],
                            ring[:, slot * D + n * 128:slot * D + (n + 1) * 128],
                            h[:, kc * 512:kc * 512 + w],
                            start=(kc == 0), stop=(kc == NI - 1))
                        if ni == 3 and kc < NI - 1:
                            i.then_inc(lk.sems["rf"], 1)
                            lk.c["rf"] += 1
                lk.inc(i, "mm")

            lk.on("tensor", t_w2)

            def v_w2(v, mo=mo, w=w, p2=p2, ngrp=tuple(ngrp)):
                lk.wait(v, "mm")
                for ni, n in enumerate(ngrp):
                    i = v.scalar_tensor_tensor(
                        sbuf_s[:, n * TLOC + mo:n * TLOC + mo + w],
                        pw2[ni][:, 0:w], b2c[:, n:n + 1],
                        xs[:, n * TLOC + mo:n * TLOC + mo + w], ALU.add, ALU.add)
                lk.mark(v, "ev")

            lk.on("vector", v_w2)

    def st0(s):
        lk.wait(s, "ev")
        for c in range(NCH):
            i = s.dma_start(out=s_out[c * 128:(c + 1) * 128, :],
                            in_=sbuf_s[:, c * TLOC:(c + 1) * TLOC])
            lk.inc(i, "st", 16)
        s.wait_ge(lk.sems["st"], lk.c["st"])

    lk.on("sync", st0)
    return lk


def host_ffn(inputs, x2b_list, xs_list, nmt=4, np2=2):
    lk = build_ffn(nmt, np2)
    w1h = _f32h(inputs["L0_attn_w1"])
    w2h = _f32h(inputs["L0_attn_w2"])
    b1h = host_prep_feat_cols(inputs["L0_attn_b1"])
    b2h = host_prep_feat_cols(inputs["L0_attn_b2"])
    res = _run(lk, [{"x2b": np.ascontiguousarray(x2b_list[c]),
                     "xs": np.ascontiguousarray(xs_list[c]),
                     "w1": w1h, "b1": b1h, "w2": w2h, "b2": b2h}
                    for c in range(NCORES)])
    return [np.asarray(r["sout"]) for r in res]



def build_combine():
    lk = L("comb")
    x_in = lk.din("xT", [D, TLOC])
    s_in = lk.din("sg", [D, TLOC], BF16)
    g_in = lk.din("gg", [D, TLOC], BF16)
    out = lk.dout("x1", [D, TLOC])
    lk.sem("ld", "ev", "st")
    xt = lk.sb("s_xt", [128, NCH * 512], F32)
    st_ = lk.sb("s_st", [128, NCH * 512], BF16)
    gt = lk.sb("s_gt", [128, NCH * 512], BF16)
    tf = lk.sb("s_tf", [128, 512], F32)
    x1 = lk.sb("s_x1", [128, NCH * 512], F32)
    lk.block()

    for (mo, w) in MT512:
        def ld_fn(s, mo=mo, w=w):
            lk.wait(s, "ev")
            for c in range(NCH):
                i = s.dma_start(out=xt[:, c * w:(c + 1) * w],
                                in_=x_in[c * 128:(c + 1) * 128, mo:mo + w])
                lk.inc(i, "ld", 16)
                i = s.dma_start(out=st_[:, c * w:(c + 1) * w],
                                in_=s_in[c * 128:(c + 1) * 128, mo:mo + w])
                lk.inc(i, "ld", 16)
                i = s.dma_start(out=gt[:, c * w:(c + 1) * w],
                                in_=g_in[c * 128:(c + 1) * 128, mo:mo + w])
                lk.inc(i, "ld", 16)

        lk.on("sync", ld_fn)

        def v_fn(v, mo=mo, w=w, stc=lk.c["st"]):
            lk.wait(v, "ld")
            lk.wait(v, "st", stc)
            for c in range(NCH):
                v.tensor_tensor(tf[:, 0:w], st_[:, c * w:(c + 1) * w],
                                gt[:, c * w:(c + 1) * w], ALU.mult)
                v.tensor_tensor(x1[:, c * w:(c + 1) * w], tf[:, 0:w],
                                xt[:, c * w:(c + 1) * w], ALU.add)
            lk.mark(v, "ev")

        lk.on("vector", v_fn)

        def st_fn(s, mo=mo, w=w):
            lk.wait(s, "ev")
            for c in range(NCH):
                i = s.dma_start(out=out[c * 128:(c + 1) * 128, mo:mo + w],
                                in_=x1[:, c * w:(c + 1) * w])
                lk.inc(i, "st", 16)

        lk.on("sync", st_fn)

    def fin(s):
        lk.wait(s, "st")

    lk.on("sync", fin)
    return lk


def host_combine(inputs, s_list, g_list, x_list=None):
    lk = build_combine()
    if x_list is None:
        x = np.asarray(inputs["L0_attn_x"])
        x_list = []
        for core in range(NCORES):
            b, j = divmod(core, 2)
            x_list.append(np.ascontiguousarray(x[b, j * TLOC:(j + 1) * TLOC, :].T))
    in_maps = []
    for core in range(NCORES):
        in_maps.append({"xT": _f32h(x_list[core]), "sg": np.ascontiguousarray(s_list[core]),
                        "gg": np.ascontiguousarray(g_list[core])})
    res = _run(lk, in_maps)
    return [np.asarray(r["x1"]) for r in res]



NDC = DN // 128


def build_l1front(ph=4):
    EXT = TLOC + 3
    lk = L("l1f")
    x1_in = lk.din("x1e", [D, EXT])
    win_in = lk.din("winT", [D, 2 * DN])
    cw_in = lk.din("convw", [128, NDC * 4])
    cb_in = lk.din("convb", [128, NDC])
    xp_in = lk.din("xprojT", [DN, 96])
    wdt_in = lk.din("wdtT", [DTR, DN])
    bdt_in = lk.din("bdt", [128, NDC])
    xn1_out = lk.dout("xn1", [D, TLOC], BF16)
    u_out = lk.dout("u", [DN, TLOC], BF16)
    dt_out = lk.dout("dt", [DN, TLOC], BF16)
    z_out = lk.dout("z", [DN, TLOC], BF16)
    B_out = lk.dout("Bs", [DS, TLOC])
    C_out = lk.dout("Cs", [DS, TLOC])

    lk.sem("ld", "wt", "nt", "nf", "mm", "sc", "ev", "st", "gp")
    NTE = EXT + 1
    xt = lk.sb("s_xt", [128, NCH * 512], F32)
    xn1 = lk.sb("s_xn1", [128, NCH * NTE], BF16)
    sq_full = lk.sb("s_sq", [128, NCH * 512], F32)
    nring = lk.sb("s_nring", [128, 2 * NCH * 128], BF16)
    xi = lk.sb("s_xi", [128, NDC * NTE], BF16)
    u_t = lk.sb("s_ut", [128, NDC * 512], BF16)
    z_t = lk.sb("s_zt", [128, NDC * 512], BF16)
    dt_t = lk.sb("s_dtt", [128, NDC * 512], BF16)
    cw = lk.sb("s_cw", [128, NDC * 4], F32)
    cb = lk.sb("s_cb", [128, NDC], F32)
    bdt = lk.sb("s_bdt", [128, NDC], F32)
    xp = lk.sb("s_xp", [128, NDC * 96], BF16)
    wdt = lk.sb("s_wdt", [DTR, DN], BF16)
    dtr_bf = lk.sb("s_dtrbf", [DTR, 512], BF16)
    bc_sb = lk.sb("s_bc", [DS, 2 * 512], F32)
    asc = lk.sb("s_asc", [128, 512], F32)
    csc = lk.sb("s_csc", [128, 512], F32)
    esc = lk.sb("s_esc", [128, 512], F32)
    cst = lk.sb("s_cstf", [128, 2], F32)
    kit = LNScratch(lk, 512)
    pool = PsumPool(lk, 2)
    p_dtr = lk.ps("p_dtr", [DTR, 512], F32)
    p_bc = lk.ps("p_bc", [128, 512], F32)
    lk.block()
    kit.init_consts()

    def g0(g):
        g.memset(cst[:, 0:1], -1.0)
        g.memset(cst[:, 1:2], 1.0)
        load_weight_bf16(lk, g, xp, xp_in, NDC, 96)
        i = g.dma_start(out=wdt[:], in_=wdt_in)
        lk.inc(i, "wt", 16)

    lk.on("gpsimd", g0)

    def ld0(s):
        for t, srcp in [(cw, cw_in), (cb, cb_in), (bdt, bdt_in)]:
            i = s.dma_start(out=t[:], in_=srcp)
            lk.inc(i, "ld", 16)

    lk.on("sync", ld0)

    widths = [259] + _tile_widths(EXT - 259, 512)
    t0 = 0
    for w in widths:
        o = t0

        def ld_fn(s, o=o, w=w):
            lk.wait(s, "ev")
            for c in range(NCH):
                i = s.dma_start(out=xt[:, c * w:(c + 1) * w],
                                in_=x1_in[c * 128:(c + 1) * 128, o:o + w])
                lk.inc(i, "ld", 16)

        lk.on("sync", ld_fn)

        def get_x(c, w=w):
            return xt[:, c * w:(c + 1) * w]

        def put_out(c, v, o=o, w=w):
            tmp = sq_full[:, c * w:(c + 1) * w]
            v.tensor_tensor(tmp, get_x(c), kit.p_rep[:, 0:w], ALU.subtract)
            v.tensor_tensor(xn1[:, c * NTE + o:c * NTE + o + w], tmp,
                            kit.p_rep[:, 512:512 + w], ALU.mult)

        emit_ln(lk, kit, get_x, put_out, w, sq_full, out_war_sems=("st",))

        def st_fn(s, o=o, w=w):
            if o + w > 3:
                lk.wait(s, "ev")
                so = max(o, 3)
                for c in range(NCH):
                    i = s.dma_start(out=xn1_out[c * 128:(c + 1) * 128, so - 3:o + w - 3],
                                    in_=xn1[:, c * NTE + so:c * NTE + o + w])
                    lk.inc(i, "st", 16)

        lk.on("sync", st_fn)
        t0 += w
    xn1_ev = lk.c["ev"]

    n_loads = [0]

    def stream_n(n):
        def g_n(g, n=n):
            uses = n_loads[0]
            if uses >= 2:
                g.wait_ge(lk.sems["nf"], uses - 1)
            for kc in range(NCH):
                i = g.dma_start(out=nring[:, (n % 2) * (NCH * 128) + kc * 128:
                                          (n % 2) * (NCH * 128) + (kc + 1) * 128],
                                in_=win_in[kc * 128:(kc + 1) * 128, n * 128:(n + 1) * 128])
                lk.inc(i, "nt", 16)
            n_loads[0] += 1

        lk.on("gpsimd", g_n)

    XI_M = [(0, 259), (259, 512), (771, 512), (1283, 512), (1795, 256)]
    Z_M = [(3, 512), (515, 512), (1027, 512), (1539, 512)]

    for n in (range(32) if ph >= 2 else []):
        stream_n(n)
        is_xi = n < 16
        mlist = XI_M if is_xi else Z_M

        def t_xz(t, n=n, mlist=mlist, nl=n_loads[0], ev_need=xn1_ev):
            lk.wait(t, "ev", ev_need)
            t.wait_ge(lk.sems["nt"], 16 * 8 * nl)
            for gi_, (mo, w) in enumerate(mlist):
                ib, bank = pool.acquire(t)
                for kc in range(NCH):
                    i = lk.nc.tensor.matmul(
                        bank[:, 0:w],
                        nring[:, (n % 2) * (NCH * 128) + kc * 128:(n % 2) * (NCH * 128) + (kc + 1) * 128],
                        xn1[:, kc * NTE + mo:kc * NTE + mo + w],
                        start=(kc == 0), stop=(kc == NCH - 1))
                lk.inc(i, "mm")
                t_xz.banks.append((ib, bank, mo, w, lk.c["mm"]))

        t_xz.banks = []
        lk.on("tensor", t_xz)

        def v_xz(v, n=n, is_xi=is_xi, banks=t_xz.banks, stc=lk.c["st"]):
            if not is_xi:
                lk.wait(v, "st", stc)
            for (ib, bank, mo, w, mmc) in banks:
                lk.wait(v, "mm", mmc)
                if is_xi:
                    i = v.tensor_copy(xi[:, n * NTE + mo:n * NTE + mo + w], bank[:, 0:w])
                else:
                    i = v.tensor_copy(z_t[:, mo - 3:mo - 3 + w], bank[:, 0:w])
                pool.freed(i, ib)
            i = v.tensor_copy(lk.dummy[:1, 0:8], lk.dummy[:1, 8:16])
            i.then_inc(lk.sems["nf"], 1)
            lk.c["nf"] += 1
            lk.mark(v, "ev")

        lk.on("vector", v_xz)

        if not is_xi:
            def st_z(s, n=n):
                lk.wait(s, "ev")
                i = s.dma_start(out=z_out[(n - 16) * 128:(n - 15) * 128, :],
                                in_=z_t[:, 0:TLOC])
                lk.inc(i, "st", 16)

            lk.on("sync", st_z)

    xi_ev = lk.c["ev"]

    for mt, (mo, w) in enumerate(MT512 if ph >= 3 else []):
        for ci in range(NDC):
            def v_conv(v, ci=ci, mo=mo, w=w, xe=xi_ev, stc=lk.c["st"], scc=lk.c["sc"]):
                lk.wait(v, "ev", xe)
                lk.wait(v, "st", stc)
                lk.wait(v, "sc", scc)
                base = ci * NTE + mo
                v.tensor_scalar(csc[:, 0:w], xi[:, base:base + w],
                                cw[:, ci * 4:ci * 4 + 1], 0.0, ALU.mult, ALU.add)
                v.scalar_tensor_tensor(esc[:, 0:w], xi[:, base + 1:base + 1 + w],
                                       cw[:, ci * 4 + 1:ci * 4 + 2], csc[:, 0:w],
                                       ALU.mult, ALU.add)
                v.scalar_tensor_tensor(csc[:, 0:w], xi[:, base + 2:base + 2 + w],
                                       cw[:, ci * 4 + 2:ci * 4 + 3], esc[:, 0:w],
                                       ALU.mult, ALU.add)
                i = v.scalar_tensor_tensor(esc[:, 0:w], xi[:, base + 3:base + 3 + w],
                                           cw[:, ci * 4 + 3:ci * 4 + 4], csc[:, 0:w],
                                           ALU.mult, ALU.add)
                lk.inc(i, "ev")

            lk.on("vector", v_conv)

            def s_u(s, ci=ci, w=w):
                lk.wait(s, "ev")
                i = s.activation(u_t[:, ci * 512:ci * 512 + w], esc[:, 0:w],
                                 AF.Silu, bias=cb[:, ci:ci + 1])
                lk.inc(i, "sc")

            lk.on("scalar", s_u)

        u_sc = lk.c["sc"]
        if ph < 4:
            continue

        def t_xp(t, mo=mo, w=w, us=u_sc):
            lk.wait(t, "sc", us)
            lk.wait(t, "ev")
            for kc in range(NDC):
                lk.nc.tensor.matmul(p_dtr[:, 0:w], xp[:, kc * 96:kc * 96 + 64],
                                    u_t[:, kc * 512:kc * 512 + w],
                                    start=(kc == 0), stop=(kc == NDC - 1))
            for kc in range(NDC):
                lk.nc.tensor.matmul(p_bc[0:16, 0:w], xp[:, kc * 96 + 64:kc * 96 + 80],
                                    u_t[:, kc * 512:kc * 512 + w],
                                    start=(kc == 0), stop=(kc == NDC - 1))
            for kc in range(NDC):
                i = lk.nc.tensor.matmul(p_bc[64:80, 0:w], xp[:, kc * 96 + 80:kc * 96 + 96],
                                        u_t[:, kc * 512:kc * 512 + w],
                                        start=(kc == 0), stop=(kc == NDC - 1))
            lk.inc(i, "mm")

        lk.on("tensor", t_xp)

        def v_xp(v, mo=mo, w=w):
            lk.wait(v, "mm")
            v.tensor_copy(dtr_bf[:, 0:w], p_dtr[:, 0:w])
            v.tensor_copy(bc_sb[:, 0:w], p_bc[0:16, 0:w])
            i = v.tensor_copy(bc_sb[:, 512:512 + w], p_bc[64:80, 0:w])
            lk.mark(v, "ev")

        lk.on("vector", v_xp)

        def st_bc(s, mo=mo, w=w):
            lk.wait(s, "ev")
            i = s.dma_start(out=B_out[:, mo:mo + w], in_=bc_sb[:, 0:w])
            lk.inc(i, "st", 16)
            i = s.dma_start(out=C_out[:, mo:mo + w], in_=bc_sb[:, 512:512 + w])
            lk.inc(i, "st", 16)
            for ci in range(NDC):
                i = s.dma_start(out=u_out[ci * 128:(ci + 1) * 128, mo:mo + w],
                                in_=u_t[:, ci * 512:ci * 512 + w])
                lk.inc(i, "st", 16)

        lk.on("sync", st_bc)

        for ci in range(NDC):
            def t_dt(t, ci=ci, w=w, ec=lk.c["ev"]):
                lk.wait(t, "ev", ec)
                ib, bank = pool.acquire(t)
                i = lk.nc.tensor.matmul(bank[:, 0:w], wdt[:, ci * 128:(ci + 1) * 128],
                                        dtr_bf[:, 0:w], start=True, stop=True)
                lk.inc(i, "mm")
                t_dt.bk = (ib, bank)

            lk.on("tensor", t_dt)
            ib, bank = t_dt.bk

            def v_dt1(v, ci=ci, w=w, ib=ib, bank=bank, scc=lk.c["sc"]):
                lk.wait(v, "mm")
                lk.wait(v, "sc", scc)
                i = v.tensor_scalar(asc[:, 0:w], bank[:, 0:w], bdt[:, ci:ci + 1], 0.0,
                                    ALU.add, ALU.add)
                pool.freed(i, ib)
                lk.mark(v, "ev")

            lk.on("vector", v_dt1)

            def s_dt(s, ci=ci, w=w):
                lk.wait(s, "ev")
                s.activation(csc[:, 0:w], asc[:, 0:w], AF.Abs)
                s.activation(csc[:, 0:w], csc[:, 0:w], AF.Exp, scale=cst[:, 0:1])
                i = s.activation(csc[:, 0:w], csc[:, 0:w], AF.Ln, bias=cst[:, 1:2])
                lk.inc(i, "sc")

            lk.on("scalar", s_dt)

            def v_dt2(v, ci=ci, w=w):
                lk.wait(v, "sc")
                v.tensor_scalar(esc[:, 0:w], asc[:, 0:w], 0.0, 0.0, ALU.max, ALU.add)
                i = v.tensor_tensor(dt_t[:, ci * 512:ci * 512 + w], esc[:, 0:w],
                                    csc[:, 0:w], ALU.add)
                lk.inc(i, "ev")

            lk.on("vector", v_dt2)

        dt_ev = lk.c["ev"]

        def st_dt(s, mo=mo, w=w, de=dt_ev):
            lk.wait(s, "ev", de)
            for ci in range(NDC):
                i = s.dma_start(out=dt_out[ci * 128:(ci + 1) * 128, mo:mo + w],
                                in_=dt_t[:, ci * 512:ci * 512 + w])
                lk.inc(i, "st", 16)

        lk.on("sync", st_dt)

    def fin(s):
        lk.wait(s, "st")

    lk.on("sync", fin)
    return lk


def build_l1front_psums(lk):
    pass


def host_l1front(inputs, x1_list, ph=4):
    lk = build_l1front(ph)
    winh = _f32h(np.asarray(inputs["L1_attn_W_in"]).T)
    cwh = _f32h(np.transpose(np.asarray(inputs["L1_attn_conv_w"]).reshape(NDC, 128, 4),
                             (1, 0, 2)).reshape(128, NDC * 4))
    cbh = host_prep_feat_cols(inputs["L1_attn_conv_b"])
    xph = _f32h(np.asarray(inputs["L1_attn_W_xproj"]).T)
    wdth = _f32h(np.asarray(inputs["L1_attn_W_dt"]).T)
    bdth = host_prep_feat_cols(inputs["L1_attn_b_dt"])
    in_maps = []
    for core in range(NCORES):
        b, j = divmod(core, 2)
        ext = np.zeros((D, TLOC + 3), np.float32)
        x1c = np.asarray(x1_list[core])
        ext[:, 3:] = x1c
        if j == 1:
            ext[:, 0:3] = np.asarray(x1_list[core - 1])[:, TLOC - 3:TLOC]
        in_maps.append({"x1e": np.ascontiguousarray(ext), "winT": winh, "convw": cwh,
                        "convb": cbh, "xprojT": xph, "wdtT": wdth, "bdt": bdth})
    res = _run(lk, in_maps)
    return {k: [np.asarray(r[k]) for r in res]
            for k in ["xn1", "u", "dt", "z", "Bs", "Cs"]}



def build_scan(nseg=8):
    NCB = 8
    NSEG = 8
    SG = T // NSEG
    lk = L("scan")
    dt_in = lk.din("dt", [1024, T], BF16)
    u_in = lk.din("u", [1024, T], BF16)
    b_in = lk.din("Bs", [DS, T])
    c_in = lk.din("Cs", [DS, T])
    al_in = lk.din("Alog", [128, NCB * DS])
    y_out = lk.dout("y", [1024, T], BF16)
    hst_out = lk.dout("hstd", [128, NCB * DS * 2])

    lk.sem("ld", "mm", "sc", "ev", "st", "gp")
    dt_s = lk.sb("s_dt", [128, NCB * SG], BF16)
    u_s = lk.sb("s_u", [128, NCB * SG], BF16)
    dtu = lk.sb("s_dtu", [128, NCB * SG], F32)
    brow = lk.sb("s_brow", [1, DS * 512], F32)
    crow = lk.sb("s_crow", [1, DS * 512], F32)
    asb = lk.sb("s_asb", [128, NCB * DS], F32)
    onesr = lk.sb("s_onesr", [1, 128], F32)
    abuf = lk.sb("s_abuf", [128, 2 * SG], F32)
    bbuf = lk.sb("s_bbuf", [128, 2 * SG], F32)
    hbuf = lk.sb("s_hbuf", [128, 2 * SG], F32)
    tmp = lk.sb("s_tmp", [128, SG], F32)
    brep = lk.sb("s_brep", [128, SG], F32)
    crep = lk.sb("s_crep", [128, SG], F32)
    hst = lk.sb("s_hst", [128, NCB * DS * 2], F32)
    yacc = lk.sb("s_yacc", [128, NCB * SG], F32)
    ybf = lk.sb("s_ybf", [128, NCB * SG], BF16)
    pool = PsumPool(lk, 2)
    lk.block()

    def g0(g):
        g.memset(onesr[:], 1.0)
        i = g.memset(hst[:], 0.0)
        lk.inc(i, "gp")

    lk.on("gpsimd", g0)

    def ld0(s):
        i = s.dma_start(out=asb[:], in_=al_in)
        lk.inc(i, "ld", 16)

    lk.on("sync", ld0)

    def s_a(s):
        lk.wait(s, "ld", 16)
        i = s.activation(asb[:], asb[:], AF.Exp)
        lk.inc(i, "sc")

    lk.on("scalar", s_a)

    def v_a(v):
        lk.wait(v, "sc")
        i = v.tensor_scalar(asb[:], asb[:], -1.0, 0.0, ALU.mult, ALU.add)
        lk.mark(v, "ev")

    lk.on("vector", v_a)
    a_ev = lk.c["ev"]

    it = 0
    ev_after_iter = []

    for seg in range(nseg):
        so = seg * SG

        def ld_seg(s, so=so, evn=lk.c["ev"]):
            lk.wait(s, "ev", evn)
            for cb in range(NCB):
                i = s.dma_start(out=dt_s[:, cb * SG:(cb + 1) * SG],
                                in_=dt_in[cb * 128:(cb + 1) * 128, so:so + SG])
                lk.inc(i, "ld", 16)
                i = s.dma_start(out=u_s[:, cb * SG:(cb + 1) * SG],
                                in_=u_in[cb * 128:(cb + 1) * 128, so:so + SG])
                lk.inc(i, "ld", 16)
            for si in range(DS):
                i = s.dma_start(out=brow[:1, si * SG:(si + 1) * SG],
                                in_=b_in[si:si + 1, so:so + SG])
                lk.inc(i, "ld", 16)
                i = s.dma_start(out=crow[:1, si * SG:(si + 1) * SG],
                                in_=c_in[si:si + 1, so:so + SG])
                lk.inc(i, "ld", 16)

        lk.on("sync", ld_seg)

        def v_dtu(v, stc=lk.c["st"]):
            lk.wait(v, "ld")
            lk.wait(v, "gp", 1)
            lk.wait(v, "st", stc)
            for cb in range(NCB):
                i = v.tensor_tensor(dtu[:, cb * SG:(cb + 1) * SG],
                                    dt_s[:, cb * SG:(cb + 1) * SG],
                                    u_s[:, cb * SG:(cb + 1) * SG], ALU.mult)
            lk.mark(v, "ev")

        lk.on("vector", v_dtu)

        for si in range(DS):
            def t_rep(t, so=so, si=si):
                lk.wait(t, "ld")
                ib0, b0 = pool.acquire(t)
                i = lk.nc.tensor.matmul(b0[:, 0:SG], onesr[:],
                                        brow[:1, si * SG:(si + 1) * SG],
                                        start=True, stop=True)
                lk.inc(i, "mm")
                ib1, b1 = pool.acquire(t)
                i = lk.nc.tensor.matmul(b1[:, 0:SG], onesr[:],
                                        crow[:1, si * SG:(si + 1) * SG],
                                        start=True, stop=True)
                lk.inc(i, "mm")
                t_rep.banks = [(ib0, b0), (ib1, b1)]

            lk.on("tensor", t_rep)
            (ib0, b0), (ib1, b1) = t_rep.banks

            def v_rep(v, ib0=ib0, b0=b0, ib1=ib1, b1=b1):
                lk.wait(v, "mm")
                i = v.tensor_copy(brep[:], b0[:, 0:SG])
                pool.freed(i, ib0)
                i = v.tensor_copy(crep[:], b1[:, 0:SG])
                pool.freed(i, ib1)

            lk.on("vector", v_rep)

            for cb in range(NCB):
                slot = it % 2

                def s_agen(s, cb=cb, si=si, slot=slot, it=it):
                    if it >= 2:
                        lk.wait(s, "ev", ev_after_iter[it - 2])
                    else:
                        lk.wait(s, "ev", a_ev)
                    lk.wait(s, "ld")
                    i = s.activation(abuf[:, slot * SG:(slot + 1) * SG],
                                     dt_s[:, cb * SG:(cb + 1) * SG], AF.Exp,
                                     scale=asb[:, cb * DS + si:cb * DS + si + 1])
                    lk.inc(i, "sc")

                lk.on("scalar", s_agen)

                def v_scan(v, cb=cb, si=si, slot=slot, scc=lk.c["sc"]):
                    lk.wait(v, "sc", scc)
                    a = abuf[:, slot * SG:(slot + 1) * SG]
                    b = bbuf[:, slot * SG:(slot + 1) * SG]
                    hh = hbuf[:, slot * SG:(slot + 1) * SG]
                    v.tensor_tensor(b, dtu[:, cb * SG:(cb + 1) * SG], brep[:], ALU.mult)
                    v.tensor_tensor_scan(hh, a, b,
                                         hst[:, 2 * (cb * DS + si) + 1:2 * (cb * DS + si) + 2],
                                         ALU.mult, ALU.add)
                    if si == 0:
                        v.tensor_tensor(yacc[:, cb * SG:(cb + 1) * SG], hh, crep[:],
                                        ALU.mult)
                    else:
                        v.tensor_tensor(tmp[:], hh, crep[:], ALU.mult)
                        v.tensor_tensor(yacc[:, cb * SG:(cb + 1) * SG],
                                        yacc[:, cb * SG:(cb + 1) * SG], tmp[:],
                                        ALU.add)
                    i = v.tensor_copy(hst[:, 2 * (cb * DS + si):2 * (cb * DS + si) + 2],
                                      hbuf[:, slot * SG + SG - 2:slot * SG + SG])
                    lk.inc(i, "ev")

                lk.on("vector", v_scan)
                ev_after_iter.append(lk.c["ev"])
                it += 1

        def v_y(v, stc=lk.c["st"]):
            lk.wait(v, "st", stc)
            for cb in range(NCB):
                i = v.tensor_copy(ybf[:, cb * SG:(cb + 1) * SG],
                                  yacc[:, cb * SG:(cb + 1) * SG])
            lk.mark(v, "ev")

        lk.on("vector", v_y)

        def st_y(s, so=so):
            lk.wait(s, "ev")
            for cb in range(NCB):
                i = s.dma_start(out=y_out[cb * 128:(cb + 1) * 128, so:so + SG],
                                in_=ybf[:, cb * SG:(cb + 1) * SG])
                lk.inc(i, "st", 16)

        lk.on("sync", st_y)


    def st_h(s):
        lk.wait(s, "ev")
        i = s.dma_start(out=hst_out, in_=hst[:])
        lk.inc(i, "st", 16)

    lk.on("sync", st_h)

    def fin(s):
        lk.wait(s, "st")

    lk.on("sync", fin)
    return lk


def host_scan(inputs, l1f, nseg=8):
    lk = build_scan(nseg)
    al = np.asarray(inputs["L1_attn_A_log"], np.float32)
    in_maps = []
    for core in range(NCORES):
        b, ch = divmod(core, 2)
        r0, r1 = ch * 1024, (ch + 1) * 1024
        dt_cs = np.concatenate([np.asarray(l1f["dt"][2 * b])[r0:r1],
                                np.asarray(l1f["dt"][2 * b + 1])[r0:r1]], axis=1)
        u_cs = np.concatenate([np.asarray(l1f["u"][2 * b])[r0:r1],
                               np.asarray(l1f["u"][2 * b + 1])[r0:r1]], axis=1)
        Bf = np.concatenate([np.asarray(l1f["Bs"][2 * b]),
                             np.asarray(l1f["Bs"][2 * b + 1])], axis=1)
        Cf = np.concatenate([np.asarray(l1f["Cs"][2 * b]),
                             np.asarray(l1f["Cs"][2 * b + 1])], axis=1)
        alc = np.ascontiguousarray(
            np.transpose(al[r0:r1].reshape(8, 128, DS), (1, 0, 2)).reshape(128, 8 * DS))
        in_maps.append({"dt": np.ascontiguousarray(dt_cs), "u": np.ascontiguousarray(u_cs),
                        "Bs": _f32h(Bf), "Cs": _f32h(Cf), "Alog": alc})
    res = _run(lk, in_maps)
    return [np.asarray(r["y"]) for r in res], [np.asarray(r["hstd"]) for r in res]



def build_gate_s1():
    lk = L("gates1")
    y_in = lk.din("y", [DN, TLOC], BF16)
    u_in = lk.din("u", [DN, TLOC], BF16)
    z_in = lk.din("z", [DN, TLOC], BF16)
    dsk_in = lk.din("dsk", [128, NDC])
    wout_in = lk.din("woutT", [DN, D])
    xn1_in = lk.din("xn1", [D, TLOC], BF16)
    gw_in = lk.din("gateWp", [128, NCH * E])
    gb_in = lk.din("gateb", [1, E])
    id_in = lk.din("ident", [128, 128])
    s1_out = lk.dout("s1", [D, TLOC], BF16)
    oh1_out = lk.dout("oh1f", [E, TLOC])
    oh2_out = lk.dout("oh2f", [E, TLOC])
    v1_out = lk.dout("v1f", [1, TLOC])
    v2_out = lk.dout("v2f", [1, TLOC])

    lk.sem("ld", "wt", "mm", "sc", "ev", "st", "gp")
    yt = lk.sb("s_yt", [128, NDC * 512], BF16)
    ut = lk.sb("s_ut", [128, NDC * 512], BF16)
    zt = lk.sb("s_zt", [128, NDC * 512], BF16)
    zs = lk.sb("s_zs", [128, 512], BF16)
    ydt = lk.sb("s_ydt", [128, 512], F32)
    yg = lk.sb("s_yg", [128, NDC * 512], BF16)
    wout = lk.sb("s_wout", [128, NDC * D], BF16)
    dsk = lk.sb("s_dsk", [128, NDC], F32)
    s1t = lk.sb("s_s1t", [128, NCH * 512], BF16)
    xn1 = lk.sb("s_xn1g", [128, NCH * TLOC], BF16)
    gwsb = lk.sb("s_gwp", [128, NCH * E], BF16)
    gbrep = lk.sb("s_gbrep", [128, E], F32)
    gbrow = lk.sb("s_gbrow", [1, E], F32)
    ident = lk.sb("s_ident", [128, 128], F32)
    onesr = lk.sb("s_onesr1", [1, 128], F32)
    gla = lk.sb("s_gla", [128, 16 * E], F32)
    oh1a = lk.sb("s_oh1a", [128, 16 * E], F32)
    oh2a = lk.sb("s_oh2a", [128, 16 * E], F32)
    mska = lk.sb("s_mska", [128, 16 * E], F32)
    glm16 = lk.sb("s_glm16", [128, 16], F32)
    va = lk.sb("s_va", [128, 32], F32)
    ohT1 = lk.sb("s_ohT1", [8, TLOC], F32)
    ohT2 = lk.sb("s_ohT2", [8, TLOC], F32)
    vT = lk.sb("s_vT", [2, TLOC], F32)
    pool = PsumPool(lk, 2)
    p_g = lk.ps("p_g", [128, 8], F32)
    p_t1 = lk.ps("p_t1", [8, 128], F32)
    p_t2 = lk.ps("p_t2", [8, 128], F32)
    p_t3 = lk.ps("p_t3", [2, 128], F32)
    p_gb = lk.ps("p_gb", [128, E], F32)
    lk.block()

    def g0(g):
        i = g.memset(onesr[:], 1.0)
        lk.inc(i, "gp")
        load_weight_bf16(lk, g, wout, wout_in, NDC, D)
        i = g.dma_start(out=gwsb[:], in_=gw_in)
        lk.inc(i, "wt", 16)

    lk.on("gpsimd", g0)

    def ld0(s):
        for t, srcp in [(dsk, dsk_in), (gbrow, gb_in), (ident, id_in)]:
            i = s.dma_start(out=t[:], in_=srcp)
            lk.inc(i, "ld", 16)
        for c in range(NCH):
            i = s.dma_start(out=xn1[:, c * TLOC:(c + 1) * TLOC],
                            in_=xn1_in[c * 128:(c + 1) * 128, :])
            lk.inc(i, "ld", 16)

    lk.on("sync", ld0)

    def t_gb(t):
        lk.wait(t, "ld", 48)
        lk.wait(t, "gp", 1)
        i = lk.nc.tensor.matmul(p_gb[:, 0:E], onesr[:], gbrow[:1, 0:E],
                                start=True, stop=True)
        lk.inc(i, "mm")

    lk.on("tensor", t_gb)

    def v_gb(v):
        lk.wait(v, "mm")
        i = v.tensor_copy(gbrep[:], p_gb[:, 0:E])
        lk.mark(v, "ev")

    lk.on("vector", v_gb)

    for (mo, w) in MT512:
        def ld_t(s, mo=mo, w=w, evc=lk.c["ev"]):
            lk.wait(s, "ev", evc)
            for ci in range(NDC):
                i = s.dma_start(out=yt[:, ci * 512:ci * 512 + w],
                                in_=y_in[ci * 128:(ci + 1) * 128, mo:mo + w])
                lk.inc(i, "ld", 16)
                i = s.dma_start(out=ut[:, ci * 512:ci * 512 + w],
                                in_=u_in[ci * 128:(ci + 1) * 128, mo:mo + w])
                lk.inc(i, "ld", 16)
                i = s.dma_start(out=zt[:, ci * 512:ci * 512 + w],
                                in_=z_in[ci * 128:(ci + 1) * 128, mo:mo + w])
                lk.inc(i, "ld", 16)

        lk.on("sync", ld_t)

        for ci in range(NDC):
            def v_yd(v, ci=ci, w=w, scc=lk.c["sc"]):
                lk.wait(v, "ld")
                lk.wait(v, "sc", scc)
                i = v.scalar_tensor_tensor(ydt[:, 0:w], ut[:, ci * 512:ci * 512 + w],
                                           dsk[:, ci:ci + 1], yt[:, ci * 512:ci * 512 + w],
                                           ALU.mult, ALU.add)
                lk.inc(i, "ev")

            lk.on("vector", v_yd)

            def s_z(s, ci=ci, w=w):
                lk.wait(s, "ev")
                i = s.activation(zs[:, 0:w], zt[:, ci * 512:ci * 512 + w], AF.Silu)
                lk.inc(i, "sc")

            lk.on("scalar", s_z)

            def v_yg(v, ci=ci, w=w):
                lk.wait(v, "sc")
                i = v.tensor_tensor(yg[:, ci * 512:ci * 512 + w], ydt[:, 0:w],
                                    zs[:, 0:w], ALU.mult)
                lk.inc(i, "ev")

            lk.on("vector", v_yg)

        def evac_s1(e, bank, n, mo2, w2):
            return e.tensor_copy(s1t[:, n * 512:n * 512 + w2], bank[:, 0:w2])

        mm_block(lk, pool,
                 get_lhsT=lambda kc, n: wout[:, kc * D + n * 128:kc * D + (n + 1) * 128],
                 get_rhs=lambda kc, mo2, w2: yg[:, kc * 512:kc * 512 + w2],
                 nK=NDC, n_list=range(NCH), m_list=[(0, w)],
                 evac=evac_s1, evac_engine="vector",
                 pre_wait=lambda t, c=lk.c["ev"]: (lk.wait(t, "wt", 16 * 16),
                                                   lk.wait(t, "ev", c)))
        emit_barrier(lk, "vector", "ev")

        def st_s1(s, mo=mo, w=w):
            lk.wait(s, "ev")
            for c in range(NCH):
                i = s.dma_start(out=s1_out[c * 128:(c + 1) * 128, mo:mo + w],
                                in_=s1t[:, c * 512:c * 512 + w])
                lk.inc(i, "st", 16)

        lk.on("sync", st_s1)

    for lc in range(16):
        lo = lc * 128

        def t_gl(t, lo=lo, evc=lk.c["ev"]):
            lk.wait(t, "ev", evc)
            for kc in range(NCH):
                i = lk.nc.tensor.matmul(p_g[:, 0:E],
                                        xn1[:, kc * TLOC + lo:kc * TLOC + lo + 128],
                                        gwsb[:, kc * E:(kc + 1) * E],
                                        start=(kc == 0), stop=(kc == NCH - 1))
            lk.inc(i, "mm")

        lk.on("tensor", t_gl)

        def v_ev(v, lc=lc):
            lk.wait(v, "mm")
            i = v.tensor_tensor(gla[:, lc * E:(lc + 1) * E], p_g[:, 0:E], gbrep[:],
                                ALU.add)
            lk.inc(i, "ev")

        lk.on("vector", v_ev)

    gl3 = gla.ap().rearrange("p (l e) -> p l e", e=E)
    oh13 = oh1a.ap().rearrange("p (l e) -> p l e", e=E)
    oh23 = oh2a.ap().rearrange("p (l e) -> p l e", e=E)
    msk3 = mska.ap().rearrange("p (l e) -> p l e", e=E)
    va3 = va.ap().rearrange("p (l k) -> p l k", k=2)
    m1b = va3[:, :, 0:1].broadcast_to([128, 16, E])
    m2b = va3[:, :, 1:2].broadcast_to([128, 16, E])
    g16b = glm16.ap().unsqueeze(2).broadcast_to([128, 16, E])

    def v_b1(v):
        sp = lambda: v.tensor_copy(ydt[:, 0:512], ydt[:, 0:512])
        v.reduce_max(glm16[:], gl3, axis=AX)
        sp()
        i = v.tensor_tensor(gl3, gl3, g16b, ALU.subtract)
        lk.inc(i, "ev")

    lk.on("vector", v_b1)

    def s_b1(s):
        lk.wait(s, "ev")
        i = s.activation(gla[:], gla[:], AF.Exp)
        lk.inc(i, "sc")

    lk.on("scalar", s_b1)

    def v_b2(v):
        lk.wait(v, "sc")
        sp = lambda: v.tensor_copy(ydt[:, 0:512], ydt[:, 0:512])
        v.reduce_sum(glm16[:], gl3, axis=AX)
        sp()
        v.reciprocal(glm16[:], glm16[:])
        sp()
        v.tensor_tensor(gl3, gl3, g16b, ALU.mult)
        v.reduce_max(va3[:, :, 0:1], gl3, axis=AX)
        sp()
        v.tensor_tensor(oh13, gl3, m1b, ALU.is_ge)
        v.tensor_scalar(mska[:], oh1a[:], -1.0, 1.0, ALU.mult, ALU.add)
        v.tensor_tensor(mska[:], gla[:], mska[:], ALU.mult)
        v.reduce_max(va3[:, :, 1:2], msk3, axis=AX)
        sp()
        i = v.tensor_tensor(oh23, msk3, m2b, ALU.is_ge)
        lk.inc(i, "ev")

    lk.on("vector", v_b2)

    for lc in range(16):
        lo = lc * 128

        def t_tr(t, lc=lc, evc=lk.c["ev"]):
            lk.wait(t, "ev", evc)
            lk.nc.tensor.transpose(p_t1[:, 0:128], oh1a[:, lc * E:(lc + 1) * E], ident[:])
            lk.nc.tensor.transpose(p_t2[:, 0:128], oh2a[:, lc * E:(lc + 1) * E], ident[:])
            i = lk.nc.tensor.transpose(p_t3[:, 0:128], va[:, 2 * lc:2 * lc + 2], ident[:])
            lk.inc(i, "mm")

        lk.on("tensor", t_tr)

        def v_tr(v, lo=lo):
            lk.wait(v, "mm")
            v.tensor_copy(ohT1[:, lo:lo + 128], p_t1[:, 0:128])
            v.tensor_copy(ohT2[:, lo:lo + 128], p_t2[:, 0:128])
            i = v.tensor_copy(vT[:, lo:lo + 128], p_t3[:, 0:128])
            lk.mark(v, "ev")

        lk.on("vector", v_tr)

    def st1(s):
        lk.wait(s, "ev")
        i = s.dma_start(out=oh1_out, in_=ohT1[:])
        lk.inc(i, "st", 16)
        i = s.dma_start(out=oh2_out, in_=ohT2[:])
        lk.inc(i, "st", 16)
        i = s.dma_start(out=v1_out, in_=vT[0:1, :])
        lk.inc(i, "st", 16)
        i = s.dma_start(out=v2_out, in_=vT[1:2, :])
        lk.inc(i, "st", 16)
        s.wait_ge(lk.sems["st"], lk.c["st"])

    lk.on("sync", st1)
    return lk


def host_gate_s1(inputs, l1f, y_scan):
    lk = build_gate_s1()
    dskh = host_prep_feat_cols(inputs["L1_attn_D_skip"])
    wouth = _f32h(np.asarray(inputs["L1_attn_W_out"]).T)
    gwp = _f32h(np.transpose(np.asarray(inputs["L1_ffn_gate_W"]).reshape(NCH, 128, E),
                             (1, 0, 2)).reshape(128, NCH * E))
    gbh = _f32h(np.asarray(inputs["L1_ffn_gate_b"]).reshape(1, E))
    idh = np.eye(128, dtype=np.float32)
    in_maps = []
    for core in range(NCORES):
        b, j = divmod(core, 2)
        yo = np.concatenate([np.asarray(y_scan[2 * b])[:, j * TLOC:(j + 1) * TLOC],
                             np.asarray(y_scan[2 * b + 1])[:, j * TLOC:(j + 1) * TLOC]],
                            axis=0)
        in_maps.append({"y": np.ascontiguousarray(yo),
                        "u": np.ascontiguousarray(l1f["u"][core]),
                        "z": np.ascontiguousarray(l1f["z"][core]),
                        "dsk": dskh, "woutT": wouth,
                        "xn1": np.ascontiguousarray(l1f["xn1"][core]),
                        "gateWp": gwp, "gateb": gbh, "ident": idh})
    res = _run(lk, in_maps)
    return {k: [np.asarray(r[k]) for r in res]
            for k in ["s1", "oh1f", "oh2f", "v1f", "v2f"]}



def build_moe():
    lk = L("moe")
    xn1_in = lk.din("xn1", [D, TLOC], BF16)
    oh1_in = lk.din("oh1f", [E, TLOC])
    oh2_in = lk.din("oh2f", [E, TLOC])
    v1_in = lk.din("v1f", [1, TLOC])
    v2_in = lk.din("v2f", [1, TLOC])
    w1_in = lk.din("W1", [E * D, HMOE])
    w2_in = lk.din("W2", [E * HMOE, D])
    b1_in = lk.din("b1c", [128, E * 16])
    b2_in = lk.din("b2c", [128, E * 8])
    sel_in = lk.din("sel8", [E, E * 128])
    mg_in = lk.din("mlng", [128, NCH])
    mb_in = lk.din("mlnb", [128, NCH])
    g1_out = lk.dout("g1", [D, TLOC], BF16)

    lk.sem("ld", "wt", "mm", "sc", "ev", "st", "gp")
    xn1 = lk.sb("m_xn1", [128, NCH * TLOC], BF16)
    boost = lk.sb("m_boost", [128, NCH * TLOC], BF16)
    g1acc = lk.sb("m_g1acc", [128, NCH * TLOC], BF16)
    w1e = lk.sb("m_w1e", [128, NCH * HMOE], BF16)
    w2e = lk.sb("m_w2e", [128, NDC * D], BF16)
    h_sb = lk.sb("m_h", [128, NDC * 512], BF16)
    ohf = lk.sb("m_ohf", [E, 2 * TLOC], BF16)
    vf = lk.sb("m_vf", [1, 2 * TLOC], BF16)
    selc = lk.sb("m_sel", [E, E * 128], BF16)
    mrep = lk.sb("m_mrep", [128, 512], BF16)
    vrep = lk.sb("m_vrep", [128, 512], F32)
    eot = lk.sb("m_eot", [128, 512], BF16)
    tmpf = lk.sb("m_tmpf", [128, 512], F32)
    b1c = lk.sb("m_b1c", [128, E * 16], F32)
    b2c = lk.sb("m_b2c", [128, E * 8], F32)
    mg = lk.sb("m_mg", [128, NCH], F32)
    mb = lk.sb("m_mb", [128, NCH], F32)
    onesr = lk.sb("m_onesr", [1, 128], BF16)
    kit = LNScratch(lk, 512, name="mln")
    pool = PsumPool(lk, 2)
    lk.block()
    kit.init_consts()

    def g0(g):
        g.memset(onesr[:], 1.0)
        i = g.dma_start(out=ohf[:, 0:TLOC], in_=oh1_in)
        lk.inc(i, "wt", 16)
        i = g.dma_start(out=ohf[:, TLOC:2 * TLOC], in_=oh2_in)
        lk.inc(i, "wt", 16)
        i = g.dma_start(out=vf[:, 0:TLOC], in_=v1_in)
        lk.inc(i, "wt", 16)
        i = g.dma_start(out=vf[:, TLOC:2 * TLOC], in_=v2_in)
        lk.inc(i, "wt", 16)
        i = g.dma_start(out=selc[:], in_=sel_in)
        lk.inc(i, "wt", 16)
        i = g.nop()
        lk.inc(i, "gp")

    lk.on("gpsimd", g0)

    def ld0(s):
        for c in range(NCH):
            i = s.dma_start(out=xn1[:, c * TLOC:(c + 1) * TLOC],
                            in_=xn1_in[c * 128:(c + 1) * 128, :])
            lk.inc(i, "ld", 16)
        i = s.dma_start(out=b1c[:], in_=b1_in)
        lk.inc(i, "ld", 16)
        i = s.dma_start(out=b2c[:], in_=b2_in)
        lk.inc(i, "ld", 16)
        i = s.dma_start(out=mg[:], in_=mg_in)
        lk.inc(i, "ld", 16)
        i = s.dma_start(out=mb[:], in_=mb_in)
        lk.inc(i, "ld", 16)

    lk.on("sync", ld0)

    def v_init(v):
        lk.wait(v, "ld", 16 * NCH)
        for c in range(NCH):
            v.tensor_copy(boost[:, c * TLOC:(c + 1) * TLOC], xn1[:, c * TLOC:(c + 1) * TLOC])
            i = v.tensor_copy(g1acc[:, c * TLOC:(c + 1) * TLOC], xn1[:, c * TLOC:(c + 1) * TLOC])
        lk.mark(v, "ev")

    lk.on("vector", v_init)

    for p2 in range(2):
        rhs_t = xn1 if p2 == 0 else boost
        for e in range(E):
            def g_we(g, e=e, wtc=lk.c["rf"] if False else None, evc=lk.c["ev"]):
                g.wait_ge(lk.sems["mm"], lk.c["mm"])
                for kc in range(NCH):
                    i = g.dma_start(out=w1e[:, kc * HMOE:(kc + 1) * HMOE],
                                    in_=w1_in[e * D + kc * 128:e * D + (kc + 1) * 128, :])
                    lk.inc(i, "wt", 16)
                for kc in range(NDC):
                    i = g.dma_start(out=w2e[:, kc * D:(kc + 1) * D],
                                    in_=w2_in[e * HMOE + kc * 128:e * HMOE + (kc + 1) * 128, :])
                    lk.inc(i, "wt", 16)

            lk.on("gpsimd", g_we)
            wt_done = lk.c["wt"]

            for (mo, w) in MT512:
                def t_rep(t, e=e, p2=p2, mo=mo, w=w, evc=lk.c["ev"]):
                    lk.wait(t, "ev", evc)
                    lk.wait(t, "ld")
                    lk.wait(t, "wt", 16 * 5)
                    lk.wait(t, "gp", 1)
                    ib0, b0 = pool.acquire(t)
                    i = lk.nc.tensor.matmul(b0[:, 0:w], selc[:, e * 128:(e + 1) * 128],
                                            ohf[:, p2 * TLOC + mo:p2 * TLOC + mo + w],
                                            start=True, stop=True)
                    lk.inc(i, "mm")
                    ib1, b1 = pool.acquire(t)
                    i = lk.nc.tensor.matmul(b1[:, 0:w], onesr[:],
                                            vf[:1, p2 * TLOC + mo:p2 * TLOC + mo + w],
                                            start=True, stop=True)
                    lk.inc(i, "mm")
                    t_rep.banks = [(ib0, b0), (ib1, b1)]

                lk.on("tensor", t_rep)
                (ib0, b0), (ib1, b1) = t_rep.banks

                def v_rep(v, ib0=ib0, b0=b0, ib1=ib1, b1=b1, w=w):
                    lk.wait(v, "mm")
                    i = v.tensor_copy(mrep[:, 0:w], b0[:, 0:w])
                    pool.freed(i, ib0)
                    i = v.tensor_copy(vrep[:, 0:w], b1[:, 0:w])
                    pool.freed(i, ib1)
                    lk.mark(v, "ev")

                lk.on("vector", v_rep)

                def evac_h(e_, bank, n, mo2, w2, e=e):
                    return e_.tensor_scalar(h_sb[:, n * 512:n * 512 + w2], bank[:, 0:w2],
                                            b1c[:, e * 16 + n:e * 16 + n + 1], 0.0,
                                            ALU.add, ALU.max)

                mm_block(lk, pool,
                         get_lhsT=lambda kc, n: w1e[:, kc * HMOE + n * 128:kc * HMOE + (n + 1) * 128],
                         get_rhs=lambda kc, mo2, w2, rhs_t=rhs_t, mo=mo: rhs_t[:, kc * TLOC + mo:kc * TLOC + mo + w2],
                         nK=NCH, n_list=range(NDC), m_list=[(0, w)],
                         evac=evac_h, evac_engine="vector",
                         pre_wait=lambda t, wd=wt_done, c=lk.c["ev"]: (lk.wait(t, "wt", wd),
                                                                       lk.wait(t, "ev", c)))

                def v_mask(v, w=w):
                    for n in range(NDC):
                        i = v.tensor_tensor(h_sb[:, n * 512:n * 512 + w],
                                            h_sb[:, n * 512:n * 512 + w],
                                            mrep[:, 0:w], ALU.mult)
                    lk.mark(v, "ev")

                lk.on("vector", v_mask)

                def evac_o(e_, bank, n, mo2, w2, e=e, p2=p2, mo=mo):
                    e_.tensor_scalar(eot[:, 0:w2], bank[:, 0:w2],
                                     b2c[:, e * 8 + n:e * 8 + n + 1], 0.0, ALU.add, ALU.add)
                    e_.tensor_tensor(tmpf[:, 0:w2], eot[:, 0:w2], vrep[:, 0:w2], ALU.mult)
                    i = e_.tensor_tensor(g1acc[:, n * TLOC + mo:n * TLOC + mo + w2],
                                         g1acc[:, n * TLOC + mo:n * TLOC + mo + w2],
                                         tmpf[:, 0:w2], ALU.add)
                    if p2 == 0:
                        i = e_.scalar_tensor_tensor(boost[:, n * TLOC + mo:n * TLOC + mo + w2],
                                                    eot[:, 0:w2], 0.5,
                                                    boost[:, n * TLOC + mo:n * TLOC + mo + w2],
                                                    ALU.mult, ALU.add)
                    return i

                mm_block(lk, pool,
                         get_lhsT=lambda kc, n: w2e[:, kc * D + n * 128:kc * D + (n + 1) * 128],
                         get_rhs=lambda kc, mo2, w2: h_sb[:, kc * 512:kc * 512 + w2],
                         nK=NDC, n_list=range(NCH), m_list=[(0, w)],
                         evac=evac_o, evac_engine="vector",
                         pre_wait=lambda t, c=lk.c["ev"]: lk.wait(t, "ev", c))
                emit_barrier(lk, "vector", "ev")

    sq_full = h_sb.ap().bitcast(F32)
    tsc = tmpf
    g1o = w1e
    for (mo, w) in MT512:
        def get_x(c, mo=mo, w=w):
            return g1acc[:, c * TLOC + mo:c * TLOC + mo + w]

        def put_out(c, v, w=w):
            v.tensor_tensor(tsc[:, 0:w], get_x(c), kit.p_rep[:, 0:w], ALU.subtract)
            v.tensor_tensor(tsc[:, 0:w], tsc[:, 0:w], kit.p_rep[:, 512:512 + w], ALU.mult)
            v.tensor_scalar(g1o[:, c * 512:c * 512 + w], tsc[:, 0:w],
                            mg[:, c:c + 1], mb[:, c:c + 1], ALU.mult, ALU.add)

        emit_ln(lk, kit, get_x, put_out, w, sq_full, out_war_sems=("st",))

        def s_silu(s, mo=mo, w=w):
            lk.wait(s, "ev")
            for c in range(NCH):
                i = s.activation(g1o[:, c * 512:c * 512 + w], g1o[:, c * 512:c * 512 + w],
                                 AF.Silu)
            lk.inc(i, "sc")

        lk.on("scalar", s_silu)

        def st_g(s, mo=mo, w=w):
            lk.wait(s, "sc")
            for c in range(NCH):
                i = s.dma_start(out=g1_out[c * 128:(c + 1) * 128, mo:mo + w],
                                in_=g1o[:, c * 512:c * 512 + w])
                lk.inc(i, "st", 16)

        lk.on("sync", st_g)

    def fin(s):
        lk.wait(s, "st")

    lk.on("sync", fin)
    return lk


def host_moe(inputs, l1f, g1r):
    lk = build_moe()
    w1h = _f32h(np.asarray(inputs["L1_ffn_W1"]).reshape(E * D, HMOE))
    w2h = _f32h(np.asarray(inputs["L1_ffn_W2"]).reshape(E * HMOE, D))
    b1h = _f32h(np.concatenate([np.asarray(inputs["L1_ffn_b1"])[e].reshape(16, 128).T
                                for e in range(E)], axis=1))
    b2h = _f32h(np.concatenate([np.asarray(inputs["L1_ffn_b2"])[e].reshape(8, 128).T
                                for e in range(E)], axis=1))
    selh = _f32h(np.kron(np.eye(E, dtype=np.float32), np.ones((1, 128), np.float32)))
    mgh = host_prep_feat_cols(inputs["L1_ffn_ln_gamma"])
    mbh = host_prep_feat_cols(inputs["L1_ffn_ln_beta"])
    in_maps = []
    for core in range(NCORES):
        in_maps.append({"xn1": np.ascontiguousarray(l1f["xn1"][core]),
                        "oh1f": _f32h(g1r["oh1f"][core]), "oh2f": _f32h(g1r["oh2f"][core]),
                        "v1f": _f32h(g1r["v1f"][core]), "v2f": _f32h(g1r["v2f"][core]),
                        "W1": w1h, "W2": w2h, "b1c": b1h, "b2c": b2h, "sel8": selh,
                        "mlng": mgh, "mlnb": mbh})
    res = _run(lk, in_maps)
    return [np.asarray(r["g1"]) for r in res]



def kernel(**inputs):
    xn, x2 = host_xn(inputs)
    qf, kvp = host_perf_kv(inputs, x2)
    xs = host_attn(inputs, qf, kvp, xn)
    x2b = host_ln2(inputs, xs)
    s0 = host_ffn(inputs, x2b, xs)
    g0 = host_rwkv(inputs, xn)
    x1 = host_combine(inputs, s0, g0)
    l1f = host_l1front(inputs, x1)
    y, _hst = host_scan(inputs, l1f)
    g1r = host_gate_s1(inputs, l1f, y)
    g1 = host_moe(inputs, l1f, g1r)
    out_sh = host_combine(inputs, g1r["s1"], g1, x_list=x1)
    out = np.zeros((B, T, D), np.float32)
    for core in range(NCORES):
        b, j = divmod(core, 2)
        out[b, j * TLOC:(j + 1) * TLOC, :] = np.asarray(out_sh[core]).T
    return out


# version 02
# speedup vs baseline: 1.1714x; 1.1714x over previous
import math
from contextlib import ExitStack

import numpy as np
import ml_dtypes

import concourse.bass as bass
import concourse.mybir as mybir
from concourse.bass_utils import run_bass_kernel_spmd

F32 = mybir.dt.float32
BF16 = mybir.dt.bfloat16
AF = mybir.ActivationFunctionType
ALU = mybir.AluOpType
AX = mybir.AxisListType.X

BF = ml_dtypes.bfloat16
NCORES = 8
B, T, D = 4, 4096, 1024
TLOC = T // 2
NH, DH, DK = 16, 64, 128
DI = 4096
DN = 2048
DS = 16
DTR = 64
E, HMOE = 8, 2048
PI = float(np.pi)
EPS = 1e-5
NCH = D // 128



class L:

    def __init__(self, name):
        self.name = name
        self.nc = bass.Bass("TRN2", target_bir_lowering=False)
        self.stack = ExitStack()
        self.c = {}
        self.sems = {}

    def din(self, name, shape, dt=F32):
        return self.nc.dram_tensor(name, list(shape), dt, kind="ExternalInput").ap()

    def dout(self, name, shape, dt=F32):
        return self.nc.dram_tensor(name, list(shape), dt, kind="ExternalOutput").ap()

    def dscratch(self, name, shape, dt=F32):
        return self.nc.dram_tensor(name, list(shape), dt, kind="Internal").ap()

    def sb(self, name, shape, dt=F32):
        return self.stack.enter_context(self.nc.sbuf_tensor(name, list(shape), dt))

    def ps(self, name, shape, dt=F32):
        return self.stack.enter_context(self.nc.psum_tensor(name, list(shape), dt))

    def sem(self, *names):
        for n in names:
            self.sems[n] = self.stack.enter_context(self.nc.semaphore(n))
            self.c[n] = 0

    def block(self):
        self.dummy = self.stack.enter_context(self.nc.sbuf_tensor("l_dummy", [1, 16], F32))
        self.blk = self.stack.enter_context(self.nc.Block())

    def mark(self, eng, sname):
        if hasattr(eng, "tensor_copy"):
            i = eng.tensor_copy(self.dummy[:1, 0:8], self.dummy[:1, 8:16])
        else:
            i = eng.copy(self.dummy[:1, 0:8], self.dummy[:1, 8:16])
        self.inc(i, sname)

    def inc(self, inst, sname, n=1):
        inst.then_inc(self.sems[sname], n)
        self.c[sname] += n
        return inst

    def wait(self, eng, sname, cnt=None):
        eng.wait_ge(self.sems[sname], self.c[sname] if cnt is None else cnt)

    def on(self, engine_name, fn):
        getattr(self.blk, engine_name)(fn)

    def close(self):
        self.stack.close()


def _run(lk: L, in_maps):
    lk.close()
    res = run_bass_kernel_spmd(lk.nc, in_maps, core_ids=list(range(NCORES)))
    return res.results


def _bfh(a):
    return np.ascontiguousarray(np.asarray(a).astype(BF))


def _f32h(a):
    return np.ascontiguousarray(np.asarray(a, dtype=np.float32))


def _tile_widths(n, w):
    out = []
    while n > 0:
        out.append(min(w, n))
        n -= out[-1]
    return out



def emit_ln(lk, kit, get_x, put_out, w, sq_full, out_war_sems=(), nchunks=NCH):
    nc = lk.nc

    def s_fn(s):
        if not kit.inited_wait_done:
            lk.wait(s, "gp", 1)
            kit.inited_wait_done = True
        lk.wait(s, "ld")
        lk.wait(s, "ev")
        for c in range(nchunks):
            i = s.activation(sq_full[:, c * w:(c + 1) * w], get_x(c), AF.Square)
        lk.inc(i, "sc")

    lk.on("scalar", s_fn)

    def t_fn(t):
        lk.wait(t, "gp", kit.gp_done)
        lk.wait(t, "ev")
        lk.wait(t, "ld")
        ones_c = kit.ones_col_bf if get_x(0).dtype == BF16 else kit.ones_col
        for c in range(nchunks):
            nc.tensor.matmul(kit.p_sum[:1, 0:w], ones_c[:], get_x(c),
                             start=(c == 0), stop=(c == nchunks - 1))
        lk.wait(t, "sc")
        for c in range(nchunks):
            i = nc.tensor.matmul(kit.p_sumsq[:1, 0:w], kit.ones_col[:],
                                 sq_full[:, c * w:(c + 1) * w],
                                 start=(c == 0), stop=(c == nchunks - 1))
        lk.inc(i, "mm")

    lk.on("tensor", t_fn)

    def v_fn(v):
        lk.wait(v, "mm")
        mean = kit.row[:1, 0:w]
        var = kit.row[:1, kit.maxw:kit.maxw + w]
        v.tensor_scalar(mean, kit.p_sum[:1, 0:w], 1.0 / D, 0.0, ALU.mult, ALU.add)
        v.tensor_tensor(var, mean, mean, ALU.mult)
        i = v.scalar_tensor_tensor(var, kit.p_sumsq[:1, 0:w], 1.0 / D, var,
                                   ALU.mult, ALU.subtract)
        lk.inc(i, "ev")

    lk.on("vector", v_fn)

    def s2_fn(s):
        lk.wait(s, "gp", kit.gp_done)
        lk.wait(s, "ev")
        var = kit.row[:1, kit.maxw:kit.maxw + w]
        s.activation(var, var, AF.Ln, bias=kit.eps_col[:1])
        i = s.activation(var, var, AF.Exp, scale=kit.half_col[:1])
        lk.inc(i, "sc")

    lk.on("scalar", s2_fn)

    def t2_fn(t):
        lk.wait(t, "sc")
        nc.tensor.matmul(kit.p_rep[:, 0:w], kit.ones_row[:], kit.row[:1, 0:w],
                         start=True, stop=True)
        i = nc.tensor.matmul(kit.p_rep[:, kit.maxw:kit.maxw + w], kit.ones_row[:],
                             kit.row[:1, kit.maxw:kit.maxw + w], start=True, stop=True)
        lk.inc(i, "mm")

    lk.on("tensor", t2_fn)

    def v2_fn(v):
        lk.wait(v, "mm")
        for s_ in out_war_sems:
            lk.wait(v, s_)
        for c in range(nchunks):
            put_out(c, v)
        lk.mark(v, "ev")

    lk.on("vector", v2_fn)


class LNScratch:
    def __init__(self, lk, maxw=512, name="ln"):
        self.lk = lk
        self.maxw = maxw
        self.row = lk.sb(f"{name}_row", [1, 2 * maxw], F32)
        self.p_sum = lk.ps(f"{name}_psum", [1, maxw], F32)
        self.p_sumsq = lk.ps(f"{name}_psumsq", [1, maxw], F32)
        self.p_rep = lk.ps(f"{name}_prep", [128, 2 * maxw], F32)
        self.ones_col = lk.sb(f"{name}_ones_c", [128, 1], F32)
        self.ones_col_bf = lk.sb(f"{name}_ones_cb", [128, 1], BF16)
        self.ones_row = lk.sb(f"{name}_ones_r", [1, 128], F32)
        self.eps_col = lk.sb(f"{name}_eps", [1, 1], F32)
        self.half_col = lk.sb(f"{name}_half", [1, 1], F32)
        self.inited_wait_done = False

    def init_consts(self):
        lk = self.lk

        def g_fn(g):
            g.memset(self.ones_col[:], 1.0)
            g.memset(self.ones_col_bf[:], 1.0)
            g.memset(self.ones_row[:], 1.0)
            g.memset(self.eps_col[:], EPS)
            i = g.memset(self.half_col[:], -0.5)
            lk.inc(i, "gp")

        lk.on("gpsimd", g_fn)
        self.gp_done = lk.c["gp"]



def build_xn():
    NT = TLOC + 1
    lk = L("xn")
    x_in = lk.din("xT", [D, NT])
    g_in = lk.din("ln1g", [128, NCH])
    b_in = lk.din("ln1b", [128, NCH])
    xn_out = lk.dout("xn", [D, NT], BF16)
    x2_out = lk.dout("x2", [D, TLOC], BF16)

    widths = [257] + _tile_widths(TLOC - 256, 512)
    lk.sem("ld", "sc", "mm", "ev", "st", "gp")
    xt = lk.sb("xt", [128, NCH * 512], F32)
    xo = lk.sb("xo", [128, NCH * 512], BF16)
    x2o = lk.sb("x2o", [128, NCH * 512], BF16)
    sq_full = lk.sb("sq", [128, NCH * 512], F32)
    gcol = lk.sb("gcol", [128, NCH], F32)
    bcol = lk.sb("bcol", [128, NCH], F32)
    kit = LNScratch(lk, 512)
    lk.block()
    kit.init_consts()

    def ld0(s):
        i = s.dma_start(out=gcol[:], in_=g_in)
        lk.inc(i, "ld", 16)
        i = s.dma_start(out=bcol[:], in_=b_in)
        lk.inc(i, "ld", 16)

    lk.on("sync", ld0)

    t0 = 0
    for w in widths:
        o = t0

        def ld_fn(s, o=o, w=w):
            lk.wait(s, "ev")
            for c in range(NCH):
                i = s.dma_start(out=xt[:, c * w:(c + 1) * w],
                                in_=x_in[c * 128:(c + 1) * 128, o:o + w])
                lk.inc(i, "ld", 16)

        lk.on("sync", ld_fn)

        def get_x(c, w=w):
            return xt[:, c * w:(c + 1) * w]

        def put_out(c, v, w=w):
            tmp = sq_full[:, c * w:(c + 1) * w]
            v.tensor_tensor(tmp, get_x(c), kit.p_rep[:, 0:w], ALU.subtract)
            v.tensor_tensor(xo[:, c * w:(c + 1) * w], tmp,
                            kit.p_rep[:, 512:512 + w], ALU.mult)

        emit_ln(lk, kit, get_x, put_out, w, sq_full, out_war_sems=("st",))

        def get_x2(c, w=w):
            return xo[:, c * w:(c + 1) * w]

        def put_out2(c, v, w=w):
            tmp = sq_full[:, c * w:(c + 1) * w]
            d = x2o[:, c * w:(c + 1) * w]
            v.tensor_tensor(tmp, get_x2(c), kit.p_rep[:, 0:w], ALU.subtract)
            v.tensor_tensor(tmp, tmp, kit.p_rep[:, 512:512 + w], ALU.mult)
            v.tensor_scalar(d, tmp, gcol[:, c:c + 1], bcol[:, c:c + 1],
                            ALU.mult, ALU.add)

        emit_ln(lk, kit, get_x2, put_out2, w, sq_full, out_war_sems=("st",))

        def st_fn(s, o=o, w=w):
            lk.wait(s, "ev")
            for c in range(NCH):
                i = s.dma_start(out=xn_out[c * 128:(c + 1) * 128, o:o + w],
                                in_=xo[:, c * w:(c + 1) * w])
                lk.inc(i, "st", 16)
            x2src0 = 1 if o == 0 else 0
            for c in range(NCH):
                i = s.dma_start(
                    out=x2_out[c * 128:(c + 1) * 128,
                               max(o - 1, 0):o + w - 1],
                    in_=x2o[:, c * w + x2src0:(c + 1) * w])
                lk.inc(i, "st", 16)

        lk.on("sync", st_fn)
        t0 += w

    def fin(s):
        lk.wait(s, "st")

    lk.on("sync", fin)
    return lk


def host_xn(inputs):
    x = np.asarray(inputs["L0_attn_x"])
    lk = build_xn()
    gi = host_prep_feat_cols(inputs["L0_attn_ln1_g"])
    bi = host_prep_feat_cols(inputs["L0_attn_ln1_b"])
    in_maps = []
    for core in range(NCORES):
        b, j = divmod(core, 2)
        t0 = j * TLOC
        ext = np.zeros((D, TLOC + 1), np.float32)
        ext[:, 1:] = x[b, t0:t0 + TLOC, :].T
        if j == 1:
            ext[:, 0] = x[b, t0 - 1, :]
        in_maps.append({"xT": np.ascontiguousarray(ext), "ln1g": gi, "ln1b": bi})
    res = _run(lk, in_maps)
    return ([np.asarray(r["xn"]) for r in res], [np.asarray(r["x2"]) for r in res])



class PsumPool:

    def __init__(self, lk, nbanks=4, tag="pb"):
        self.lk = lk
        self.n = nbanks
        self.banks = [lk.ps(f"{tag}{i}", [128, 512], F32) for i in range(nbanks)]
        self.free_sems = [f"{tag}f{i}" for i in range(nbanks)]
        lk.sem(*self.free_sems)
        self.uses = [0] * nbanks
        self.rr = 0

    def acquire(self, teng):
        i = self.rr
        self.rr = (self.rr + 1) % self.n
        if self.uses[i] > 0:
            teng.wait_ge(self.lk.sems[self.free_sems[i]], self.uses[i])
        self.uses[i] += 1
        return i, self.banks[i]

    def freed(self, inst, i):
        inst.then_inc(self.lk.sems[self.free_sems[i]], 1)
        self.lk.c[self.free_sems[i]] += 1


def mm_block(lk, pool, get_lhsT, get_rhs, nK, n_list, m_list, evac,
             evac_engine="vector", pre_wait=None):
    nc = lk.nc
    first = True
    for n in n_list:
        for (mo, w) in m_list:
            holder = {}

            def t_fn(t, n=n, mo=mo, w=w, first=first):
                if pre_wait is not None and first:
                    pre_wait(t)
                ib, bank = pool.acquire(t)
                holder["b"] = (ib, bank)
                for kc in range(nK):
                    i = nc.tensor.matmul(bank[:, 0:w], get_lhsT(kc, n),
                                         get_rhs(kc, mo, w),
                                         start=(kc == 0), stop=(kc == nK - 1))
                lk.inc(i, "mm")

            lk.on("tensor", t_fn)
            ib, bank = holder["b"]

            def e_fn(e, n=n, mo=mo, w=w, ib=ib, bank=bank):
                lk.wait(e, "mm")
                last = evac(e, bank, n, mo, w)
                pool.freed(last, ib)

            lk.on(evac_engine, e_fn)
            first = False


def emit_barrier(lk, engine_name, sname="ev"):
    def f(e):
        lk.mark(e, sname)

    lk.on(engine_name, f)


MT512 = [(o, 512) for o in range(0, TLOC, 512)]


def load_weight_bf16(lk, g, dst_sb, hbm_ap, nK, ncols, sem="wt"):
    for kc in range(nK):
        i = g.dma_start(out=dst_sb[:, kc * ncols:(kc + 1) * ncols],
                        in_=hbm_ap[kc * 128:(kc + 1) * 128, :])
        lk.inc(i, sem, 16)



def build_rwkv(dbg=False, tiles=None):
    lk = L("rwkv")
    dbgo = {}
    if dbg:
        for nm, dt in [("xr", BF16), ("r", BF16), ("ka", BF16), ("rw", F32)]:
            dbgo[nm] = lk.dout("dbg_" + nm, [D, 512], dt)
        dbgo["xn"] = lk.dout("dbg_xn", [D, TLOC + 1], BF16)
    xn_in = lk.din("xn", [D, TLOC + 1], BF16)
    wrt = lk.din("wrt", [D, D])
    wkt = lk.din("wkt", [D, D])
    wvt = lk.din("wvt", [D, D])
    mur_in = lk.din("mur", [128, NCH])
    muk_in = lk.din("muk", [128, NCH])
    g_out = lk.dout("g", [D, TLOC], BF16)

    lk.sem("ld", "wt", "mm", "sc", "ev", "st")
    NT = TLOC + 2
    xn = lk.sb("s_xn", [128, NCH * NT], BF16)
    wr = lk.sb("wr", [128, NCH * D], BF16)
    wk = lk.sb("wk", [128, NCH * D], BF16)
    wv = lk.sb("wv", [128, NCH * D], BF16)
    mur = lk.sb("s_mur", [128, NCH], F32)
    muk = lk.sb("s_muk", [128, NCH], F32)
    dsc = lk.sb("dsc", [128, NCH * 512], F32)
    xr = lk.sb("xr", [128, NCH * 512], BF16)
    xk = lk.sb("xk", [128, NCH * 512], BF16)
    rbf = lk.sb("rbf", [128, NCH * 512], BF16)
    kabf = lk.sb("kabf", [128, NCH * 512], BF16)
    ksc = lk.sb("ksc", [128, 512], BF16)
    rwsc = lk.sb("rwsc", [128, NCH * 512], F32)
    gbf = lk.sb("gbf", [128, NCH * 512], BF16)
    pool = PsumPool(lk, 4)
    lk.block()

    def ld_fn(s):
        for c in range(NCH):
            i = s.dma_start(out=xn[:, c * NT:c * NT + (TLOC + 1)],
                            in_=xn_in[c * 128:(c + 1) * 128, :])
            lk.inc(i, "ld", 16)
        i = s.dma_start(out=mur[:], in_=mur_in)
        lk.inc(i, "ld", 16)
        i = s.dma_start(out=muk[:], in_=muk_in)
        lk.inc(i, "ld", 16)

    lk.on("sync", ld_fn)

    def wt_fn(g):
        load_weight_bf16(lk, g, wr, wrt, NCH, D)
        load_weight_bf16(lk, g, wk, wkt, NCH, D)
        load_weight_bf16(lk, g, wv, wvt, NCH, D)

    lk.on("gpsimd", wt_fn)

    for (mo, w) in (tiles or MT512):
        mm_before_mix = lk.c["mm"]

        def v_mix(v, mo=mo, w=w, mmw=mm_before_mix):
            lk.wait(v, "ld")
            lk.wait(v, "mm", mmw)
            for c in range(NCH):
                cur = xn[:, c * NT + mo + 1: c * NT + mo + 1 + w]
                prv = xn[:, c * NT + mo: c * NT + mo + w]
                ds = dsc[:, c * 512:c * 512 + w]
                v.tensor_tensor(ds, cur, prv, ALU.subtract)
                v.scalar_tensor_tensor(xr[:, c * w:(c + 1) * w], ds,
                                       mur[:, c:c + 1], prv, ALU.mult, ALU.add)
                i = v.scalar_tensor_tensor(xk[:, c * w:(c + 1) * w], ds,
                                           muk[:, c:c + 1], prv, ALU.mult, ALU.add)
            lk.inc(i, "ev")

        lk.on("vector", v_mix)
        mix_ev = lk.c["ev"]

        def s_war(s, c=lk.c["ev"]):
            lk.wait(s, "ev", c)

        lk.on("scalar", s_war)

        def evac_r(e, bank, n, mo2, w2):
            return e.activation(rbf[:, n * 512:n * 512 + w2], bank[:, 0:w2], AF.Sigmoid)

        mm_block(lk, pool,
                 get_lhsT=lambda kc, n: wr[:, kc * D + n * 128: kc * D + (n + 1) * 128],
                 get_rhs=lambda kc, mo2, w2, w=w: xr[:, kc * w: kc * w + w2],
                 nK=NCH, n_list=range(NCH), m_list=[(0, w)],
                 evac=evac_r, evac_engine="scalar",
                 pre_wait=lambda t, me=mix_ev: (lk.wait(t, "wt", 16 * 8),
                                                lk.wait(t, "ev", me)))
        r_done_sc = lk.c["sc"]

        def evac_k(e, bank, n, mo2, w2):
            e.tensor_scalar(ksc[:, 0:w2], bank[:, 0:w2], 0.0, 0.0, ALU.max, ALU.add)
            return e.tensor_tensor(kabf[:, n * 512:n * 512 + w2], ksc[:, 0:w2],
                                   bank[:, 0:w2], ALU.mult)

        mm_block(lk, pool,
                 get_lhsT=lambda kc, n: wk[:, kc * D + n * 128: kc * D + (n + 1) * 128],
                 get_rhs=lambda kc, mo2, w2, w=w: xk[:, kc * w: kc * w + w2],
                 nK=NCH, n_list=range(NCH), m_list=[(0, w)],
                 evac=evac_k, evac_engine="vector",
                 pre_wait=lambda t: lk.wait(t, "wt", 16 * 16))
        emit_barrier(lk, "vector", "ev")
        kact_ev = lk.c["ev"]

        def evac_vo(e, bank, n, mo2, w2):
            return e.tensor_tensor(rwsc[:, n * 512:n * 512 + w2],
                                   rbf[:, n * 512:n * 512 + w2],
                                   bank[:, 0:w2], ALU.mult)

        mm_block(lk, pool,
                 get_lhsT=lambda kc, n: wv[:, kc * D + n * 128: kc * D + (n + 1) * 128],
                 get_rhs=lambda kc, mo2, w2, w=w: kabf[:, kc * w: kc * w + w2],
                 nK=NCH, n_list=range(NCH), m_list=[(0, w)],
                 evac=evac_vo, evac_engine="vector",
                 pre_wait=lambda t, ke=kact_ev: (lk.wait(t, "wt", 16 * 24),
                                                 lk.wait(t, "ev", ke)))
        emit_barrier(lk, "vector", "ev")
        vo_ev = lk.c["ev"]

        def s_silu(s, mo=mo, w=w, ve=vo_ev):
            lk.wait(s, "ev", ve)
            lk.wait(s, "st")
            for c in range(NCH):
                i = s.activation(gbf[:, c * 512:c * 512 + w],
                                 rwsc[:, c * 512:c * 512 + w], AF.Silu)
            lk.inc(i, "sc")

        lk.on("scalar", s_silu)

        def st_fn(s, mo=mo, w=w, sc=lk.c["sc"]):
            lk.wait(s, "sc", sc)
            for c in range(NCH):
                i = s.dma_start(out=g_out[c * 128:(c + 1) * 128, mo:mo + w],
                                in_=gbf[:, c * 512:c * 512 + w])
                lk.inc(i, "st", 16)
            if dbg and mo == 0:
                for c in range(NCH):
                    i = s.dma_start(out=dbgo["xn"][c * 128:(c + 1) * 128, :],
                                    in_=xn[:, c * NT:c * NT + (TLOC + 1)])
                    lk.inc(i, "st", 16)
                for nm, tile in [("xr", xr), ("r", rbf), ("ka", kabf), ("rw", rwsc)]:
                    for c in range(NCH):
                        i = s.dma_start(out=dbgo[nm][c * 128:(c + 1) * 128, :],
                                        in_=tile[:, c * 512:(c + 1) * 512])
                        lk.inc(i, "st", 16)

        lk.on("sync", st_fn)

    def fin(s):
        lk.wait(s, "st")

    lk.on("sync", fin)
    return lk


def host_prep_feat_cols(vec):
    v = np.asarray(vec, np.float32)
    return np.ascontiguousarray(v.reshape(-1, 128).T)


def host_rwkv(inputs, xn_list, dbg=False, tiles=None):
    lk = build_rwkv(dbg, tiles)
    wrt = _f32h(np.asarray(inputs["L0_ffn_W_r"]).T)
    wkt = _f32h(np.asarray(inputs["L0_ffn_W_k"]).T)
    wvt = _f32h(np.asarray(inputs["L0_ffn_W_v"]).T)
    mur = host_prep_feat_cols(inputs["L0_ffn_mu_r"])
    muk = host_prep_feat_cols(inputs["L0_ffn_mu_k"])
    in_maps = []
    for core in range(NCORES):
        in_maps.append({"xn": np.ascontiguousarray(xn_list[core]),
                        "wrt": wrt, "wkt": wkt, "wvt": wvt,
                        "mur": mur, "muk": muk})
    res = _run(lk, in_maps)
    if dbg:
        return res
    return [np.asarray(r["g"]) for r in res]



def build_perf_kv(nheads=NH):
    lk = L("perfkv")
    x2_in = lk.din("x2", [D, TLOC], BF16)
    wq_in = lk.din("wq", [D, D]); wk_in = lk.din("wk", [D, D]); wv_in = lk.din("wv", [D, D])
    bq_in = lk.din("bq", [128, NCH]); bk_in = lk.din("bk", [128, NCH])
    bv_in = lk.din("bv", [1, D])
    proj_in = lk.din("proj", [DH, DK])
    qf_out = lk.dout("qf", [D, TLOC], BF16)
    kvp_out = lk.dout("kvp", [NH * DH, 2 * DK])

    lk.sem("ld", "wt", "mm", "sc", "ev", "st", "gp")
    x2 = lk.sb("s_x2", [128, NCH * TLOC], BF16)
    wq = lk.sb("s_wq", [128, NCH * D], BF16)
    wk = lk.sb("s_wk", [128, NCH * D], BF16)
    wv = lk.sb("s_wv", [128, NCH * D], BF16)
    qf = lk.sb("s_qf", [128, NCH * TLOC], BF16)
    kf = lk.sb("s_kf", [128, NCH * TLOC], BF16)
    v_full = lk.sb("s_v", [128, 16 * D], BF16)
    proj = lk.sb("s_proj", [128, 2 * DK], BF16)
    bq = lk.sb("s_bq", [128, NCH], F32)
    bk = lk.sb("s_bk", [128, NCH], F32)
    bvr = lk.sb("s_bvr", [1, D], F32)
    bv_rep = lk.sb("s_bvrep", [128, D], BF16)
    onesr = lk.sb("s_onesr", [1, 128], F32)
    wsc = lk.sb("s_wsc", [128, 128], F32)
    csc = lk.sb("s_csc", [128, 128], F32)
    aw = lk.sb("s_aw", [128, 128], F32)
    kp = lk.sb("s_kp", [128, 2 * DK], BF16)
    kvsb = lk.sb("s_kvsb", [128, 4 * 512], F32)
    cst = lk.sb("s_cst", [128, 2], F32)
    pool = PsumPool(lk, 2)
    p_pk = lk.ps("p_pk", [128, DK], F32)
    kv4 = [lk.ps(f"p_kv{i}", [128, 512], F32) for i in range(4)]
    lk.block()

    def g0(g):
        g.memset(cst[:, 0:1], PI / 2)
        g.memset(cst[:, 1:2], -1.0)
        i = g.memset(onesr[:], 1.0)
        lk.inc(i, "gp")
        load_weight_bf16(lk, g, wq, wq_in, NCH, D)
        load_weight_bf16(lk, g, wk, wk_in, NCH, D)
        load_weight_bf16(lk, g, wv, wv_in, NCH, D)
        g.memset(proj[:], 0.0)
        i = g.dma_start(out=proj[0:64, 0:DK], in_=proj_in)
        lk.inc(i, "wt", 16)
        i = g.dma_start(out=proj[64:128, DK:2 * DK], in_=proj_in)
        lk.inc(i, "wt", 16)

    lk.on("gpsimd", g0)

    def ld0(s):
        for c in range(NCH):
            i = s.dma_start(out=x2[:, c * TLOC:(c + 1) * TLOC],
                            in_=x2_in[c * 128:(c + 1) * 128, :])
            lk.inc(i, "ld", 16)
        for t, srcp in [(bq, bq_in), (bk, bk_in), (bvr, bv_in)]:
            i = s.dma_start(out=t[:], in_=srcp)
            lk.inc(i, "ld", 16)

    lk.on("sync", ld0)

    def t_bv(t):
        lk.wait(t, "ld")
        lk.wait(t, "gp")
        ib0, b0 = pool.acquire(t)
        i = lk.nc.tensor.matmul(b0[:, 0:512], onesr[:], bvr[:1, 0:512],
                                start=True, stop=True)
        lk.inc(i, "mm")
        ib1, b1 = pool.acquire(t)
        i = lk.nc.tensor.matmul(b1[:, 0:512], onesr[:], bvr[:1, 512:1024],
                                start=True, stop=True)
        lk.inc(i, "mm")
        t_bv.banks = [(ib0, b0), (ib1, b1)]

    lk.on("tensor", t_bv)

    def v_bv(v):
        lk.wait(v, "mm")
        (ib0, b0), (ib1, b1) = t_bv.banks
        i = v.tensor_copy(bv_rep[:, 0:512], b0[:, 0:512])
        pool.freed(i, ib0)
        i = v.tensor_copy(bv_rep[:, 512:1024], b1[:, 0:512])
        pool.freed(i, ib1)
        lk.mark(v, "ev")

    lk.on("vector", v_bv)

    for (wt_sb, b_sb, dst, wt_cnt) in [(wq, bq, qf, 16 * 8), (wk, bk, kf, 16 * 16)]:
        def evac_qk(e, bank, n, mo, w, b_sb=b_sb, dst=dst):
            return e.tensor_scalar(dst[:, n * TLOC + mo:n * TLOC + mo + w],
                                   bank[:, 0:w], b_sb[:, n:n + 1], 0.0,
                                   ALU.add, ALU.add)

        mm_block(lk, pool,
                 get_lhsT=lambda kc, n, wt_sb=wt_sb: wt_sb[:, kc * D + n * 128:kc * D + (n + 1) * 128],
                 get_rhs=lambda kc, mo, w: x2[:, kc * TLOC + mo:kc * TLOC + mo + w],
                 nK=NCH, n_list=range(NCH), m_list=MT512,
                 evac=evac_qk, evac_engine="vector",
                 pre_wait=lambda t, wc=wt_cnt: (lk.wait(t, "wt", wc), lk.wait(t, "ld")))
    emit_barrier(lk, "vector", "ev")
    kf_ev = lk.c["ev"]

    def evac_v(e, bank, n, mo, w):
        return e.tensor_tensor(v_full[:, n * D + mo:n * D + mo + w], bank[:, 0:w],
                               bv_rep[:, mo:mo + w], ALU.add)

    mm_block(lk, pool,
             get_lhsT=lambda kc, n: x2[:, kc * TLOC + n * 128:kc * TLOC + n * 128 + 128],
             get_rhs=lambda kc, mo, w: wv[:, kc * D + mo:kc * D + mo + w],
             nK=NCH, n_list=range(16), m_list=[(0, 512), (512, 512)],
             evac=evac_v, evac_engine="vector",
             pre_wait=lambda t: (lk.wait(t, "wt", 16 * 24), lk.wait(t, "ev")))
    emit_barrier(lk, "vector", "ev")

    for h in range(nheads):
        ch, po = h // 2, (h % 2) * 64
        q4, ph, mh = h // 4, (h % 4) // 2, h % 2
        for lc in range(16):
            lo = lc * 128

            def t_pk(t, ch=ch, po=po, lo=lo):
                lk.wait(t, "ev")
                i = lk.nc.tensor.matmul(p_pk[:, 0:DK],
                                        kf[:, ch * TLOC + lo:ch * TLOC + lo + 128],
                                        proj[:, (po // 64) * DK:(po // 64) * DK + DK],
                                        start=True, stop=True)
                lk.inc(i, "mm")

            lk.on("tensor", t_pk)

            def v_wrap(v):
                lk.wait(v, "mm")
                lk.wait(v, "sc")
                v.tensor_scalar(wsc[:], p_pk[:, 0:DK], 0.125, 0.0, ALU.mult, ALU.add)
                v.tensor_scalar(csc[:], wsc[:], PI, -2 * PI, ALU.is_gt, ALU.mult)
                v.tensor_tensor(wsc[:], wsc[:], csc[:], ALU.add)
                v.tensor_scalar(csc[:], wsc[:], -PI, 2 * PI, ALU.is_lt, ALU.mult)
                i = v.tensor_tensor(wsc[:], wsc[:], csc[:], ALU.add)
                lk.inc(i, "ev")

            lk.on("vector", v_wrap)

            def s_sin(s):
                lk.wait(s, "ev")
                s.activation(aw[:], wsc[:], AF.Abs)
                s.activation(kp[:, DK:2 * DK], wsc[:], AF.Sin)
                i = s.activation(kp[:, 0:DK], aw[:], AF.Sin,
                                 bias=cst[:, 0:1], scale=cst[:, 1:2])
                lk.inc(i, "sc")

            lk.on("scalar", s_sin)

            def t_kv(t, h=h, lc=lc, q4=q4, ph=ph, mh=mh):
                lk.wait(t, "sc")
                i = lk.nc.tensor.matmul(
                    kv4[q4][ph * 64:ph * 64 + 64, mh * 256:mh * 256 + 256],
                    v_full[:, lc * D + h * 64:lc * D + (h + 1) * 64], kp[:, 0:256],
                    start=(lc == 0), stop=(lc == 15))
                if lc == 15:
                    lk.inc(i, "mm")

            lk.on("tensor", t_kv)

    def v_kv(v):
        lk.wait(v, "mm")
        for q4 in range(4):
            i = v.tensor_copy(kvsb[:, q4 * 512:(q4 + 1) * 512], kv4[q4][:, 0:512])
        lk.mark(v, "ev")

    lk.on("vector", v_kv)

    def st0(s):
        lk.wait(s, "ev")
        for c in range(NCH):
            i = s.dma_start(out=qf_out[c * 128:(c + 1) * 128, :],
                            in_=qf[:, c * TLOC:(c + 1) * TLOC])
            lk.inc(i, "st", 16)
        for h in range(NH):
            q4, ph, mh = h // 4, (h % 4) // 2, h % 2
            i = s.dma_start(out=kvp_out[h * DH:(h + 1) * DH, :],
                            in_=kvsb[ph * 64:ph * 64 + 64,
                                     q4 * 512 + mh * 256:q4 * 512 + mh * 256 + 256])
            lk.inc(i, "st", 16)
        s.wait_ge(lk.sems["st"], lk.c["st"])

    lk.on("sync", st0)
    return lk


def host_perf_kv(inputs, x2_list, nheads=NH):
    lk = build_perf_kv(nheads)
    wqh = _f32h(inputs["L0_attn_wq"]); wkh = _f32h(inputs["L0_attn_wk"])
    wvh = _f32h(inputs["L0_attn_wv"])
    bqh = host_prep_feat_cols(inputs["L0_attn_bq"])
    bkh = host_prep_feat_cols(inputs["L0_attn_bk"])
    bvh = _f32h(np.asarray(inputs["L0_attn_bv"]).reshape(1, D))
    projh = _f32h(inputs["L0_attn_proj_mat"])
    in_maps = [{"x2": np.ascontiguousarray(x2_list[c]), "wq": wqh, "wk": wkh,
                "wv": wvh, "bq": bqh, "bk": bkh, "bv": bvh, "proj": projh}
               for c in range(NCORES)]
    res = _run(lk, in_maps)
    return ([np.asarray(r["qf"]) for r in res], [np.asarray(r["kvp"]) for r in res])



def build_attn():
    lk = L("attn")
    qf_in = lk.din("qf", [D, TLOC], BF16)
    kvt_in = lk.din("kvT2", [2 * 2 * DK, D])
    xn_in = lk.din("xnr", [D, TLOC + 1], BF16)
    wo_in = lk.din("wo", [D, D])
    bo_in = lk.din("bo", [128, NCH])
    proj_in = lk.din("proj", [DH, DK])
    xs_out = lk.dout("xs", [D, TLOC], BF16)

    lk.sem("ld", "wt", "mm", "sc", "ev", "st", "gp")
    qf = lk.sb("s_qf", [128, NCH * TLOC], BF16)
    xn = lk.sb("s_xn", [128, NCH * TLOC], BF16)
    kvt2 = lk.sb("s_kvt2", [128, 4 * D], F32)
    kvt = lk.sb("s_kvt", [128, 2 * D], BF16)
    wo = lk.sb("s_wo", [128, NCH * D], BF16)
    bo = lk.sb("s_bo", [128, NCH], F32)
    proj = lk.sb("s_proj", [128, 2 * DK], BF16)
    wsc = lk.sb("s_wsc", [128, 512], F32)
    csc = lk.sb("s_csc", [128, 512], F32)
    aw = lk.sb("s_aw", [128, 512], F32)
    qp = lk.sb("s_qp", [128, 2 * 512], BF16)
    attn_sb = lk.sb("s_attn", [128, NCH * 512], BF16)
    xs_sb = lk.sb("s_xs", [128, NCH * TLOC], BF16)
    cst = lk.sb("s_cst", [128, 2], F32)
    pool = PsumPool(lk, 2)
    p_pq = lk.ps("p_pq", [128, 512], F32)
    p_at = lk.ps("p_at", [64, 512], F32)
    lk.block()

    def g0(g):
        g.memset(cst[:, 0:1], PI / 2)
        g.memset(cst[:, 1:2], -1.0)
        g.memset(proj[:], 0.0)
        i = g.dma_start(out=proj[0:64, 0:DK], in_=proj_in)
        lk.inc(i, "wt", 16)
        i = g.dma_start(out=proj[64:128, DK:2 * DK], in_=proj_in)
        lk.inc(i, "wt", 16)
        load_weight_bf16(lk, g, wo, wo_in, NCH, D)

    lk.on("gpsimd", g0)

    def ld0(s):
        for c in range(NCH):
            i = s.dma_start(out=qf[:, c * TLOC:(c + 1) * TLOC],
                            in_=qf_in[c * 128:(c + 1) * 128, :])
            lk.inc(i, "ld", 16)
        for c in range(NCH):
            i = s.dma_start(out=xn[:, c * TLOC:(c + 1) * TLOC],
                            in_=xn_in[c * 128:(c + 1) * 128, 1:TLOC + 1])
            lk.inc(i, "ld", 16)
        for c in range(4):
            i = s.dma_start(out=kvt2[:, c * D:(c + 1) * D],
                            in_=kvt_in[c * 128:(c + 1) * 128, :])
            lk.inc(i, "ld", 16)
        i = s.dma_start(out=bo[:], in_=bo_in)
        lk.inc(i, "ld", 16)

    lk.on("sync", ld0)

    def v_kv(v):
        lk.wait(v, "ld")
        for mc in range(2):
            a = kvt2[:, mc * D:(mc + 1) * D]
            b = kvt2[:, (2 + mc) * D:(3 + mc) * D]
            v.tensor_tensor(a, a, b, ALU.add)
            i = v.tensor_scalar(kvt[:, mc * D:(mc + 1) * D], a, 1.0 / 128.0, 0.0,
                                ALU.mult, ALU.add)
        lk.mark(v, "ev")

    lk.on("vector", v_kv)

    for (mo, w) in MT512:
        for h in range(NH):
            ch, po = h // 2, (h % 2) * 64

            def t_pq(t, ch=ch, po=po, mo=mo, w=w):
                lk.wait(t, "ev")
                lk.wait(t, "wt", 16 * 2)
                i = lk.nc.tensor.matmul(p_pq[:, 0:w],
                                        proj[:, (po // 64) * DK:(po // 64) * DK + DK],
                                        qf[:, ch * TLOC + mo:ch * TLOC + mo + w],
                                        start=True, stop=True)
                lk.inc(i, "mm")

            lk.on("tensor", t_pq)

            def v_wrap(v, w=w):
                lk.wait(v, "mm")
                lk.wait(v, "sc")
                v.tensor_scalar(wsc[:, 0:w], p_pq[:, 0:w], 0.125, 0.0, ALU.mult, ALU.add)
                v.tensor_scalar(csc[:, 0:w], wsc[:, 0:w], PI, -2 * PI, ALU.is_gt, ALU.mult)
                v.tensor_tensor(wsc[:, 0:w], wsc[:, 0:w], csc[:, 0:w], ALU.add)
                v.tensor_scalar(csc[:, 0:w], wsc[:, 0:w], -PI, 2 * PI, ALU.is_lt, ALU.mult)
                i = v.tensor_tensor(wsc[:, 0:w], wsc[:, 0:w], csc[:, 0:w], ALU.add)
                lk.inc(i, "ev")

            lk.on("vector", v_wrap)

            def s_sin(s, w=w):
                lk.wait(s, "ev")
                s.activation(aw[:, 0:w], wsc[:, 0:w], AF.Abs)
                s.activation(qp[:, 512:512 + w], wsc[:, 0:w], AF.Sin)
                i = s.activation(qp[:, 0:w], aw[:, 0:w], AF.Sin,
                                 bias=cst[:, 0:1], scale=cst[:, 1:2])
                lk.inc(i, "sc")

            lk.on("scalar", s_sin)

            def t_at(t, h=h, w=w):
                lk.wait(t, "sc")
                lk.nc.tensor.matmul(p_at[:, 0:w], kvt[:, 0 * D + h * 64:0 * D + h * 64 + 64],
                                    qp[:, 0:w], start=True, stop=False)
                i = lk.nc.tensor.matmul(p_at[:, 0:w], kvt[:, 1 * D + h * 64:1 * D + h * 64 + 64],
                                        qp[:, 512:512 + w], start=False, stop=True)
                lk.inc(i, "mm")

            lk.on("tensor", t_at)

            def v_at(v, h=h, ch=ch, po=po, w=w):
                lk.wait(v, "mm")
                i = v.tensor_copy(attn_sb[po:po + 64, ch * 512:ch * 512 + w], p_at[:, 0:w])
                lk.inc(i, "ev")

            lk.on("vector", v_at)

        def evac_o(e, bank, n, mo2, w2, mo=mo):
            return e.scalar_tensor_tensor(xs_sb[:, n * TLOC + mo:n * TLOC + mo + w2],
                                          bank[:, 0:w2], bo[:, n:n + 1],
                                          xn[:, n * TLOC + mo:n * TLOC + mo + w2],
                                          ALU.add, ALU.add)

        mm_block(lk, pool,
                 get_lhsT=lambda kc, n: wo[:, kc * D + n * 128:kc * D + (n + 1) * 128],
                 get_rhs=lambda kc, mo2, w2: attn_sb[:, kc * 512:kc * 512 + w2],
                 nK=NCH, n_list=range(NCH), m_list=[(0, w)],
                 evac=evac_o, evac_engine="vector",
                 pre_wait=lambda t, c=lk.c["ev"]: (lk.wait(t, "wt"), lk.wait(t, "ev", c)))
        emit_barrier(lk, "vector", "ev")

    def st0(s):
        lk.wait(s, "ev")
        for c in range(NCH):
            i = s.dma_start(out=xs_out[c * 128:(c + 1) * 128, :],
                            in_=xs_sb[:, c * TLOC:(c + 1) * TLOC])
            lk.inc(i, "st", 16)
        s.wait_ge(lk.sems["st"], lk.c["st"])

    lk.on("sync", st0)
    return lk


def host_attn(inputs, qf_list, kvp_list, xn_list):
    lk = build_attn()
    woh = _f32h(inputs["L0_attn_wo"])
    boh = host_prep_feat_cols(inputs["L0_attn_bo"])
    projh = _f32h(inputs["L0_attn_proj_mat"])
    in_maps = []
    for core in range(NCORES):
        pair = core ^ 1
        kvt2 = np.concatenate([np.asarray(kvp_list[core]).T,
                               np.asarray(kvp_list[pair]).T], axis=0)
        in_maps.append({"qf": np.ascontiguousarray(qf_list[core]),
                        "kvT2": _f32h(kvt2),
                        "xnr": np.ascontiguousarray(xn_list[core]),
                        "wo": woh, "bo": boh, "proj": projh})
    res = _run(lk, in_maps)
    return [np.asarray(r["xs"]) for r in res]



def build_ln2():
    lk = L("ln2")
    xs_in = lk.din("xs", [D, TLOC], BF16)
    g_in = lk.din("g2", [128, NCH])
    b_in = lk.din("b2", [128, NCH])
    out = lk.dout("x2b", [D, TLOC], BF16)
    lk.sem("ld", "sc", "mm", "ev", "st", "gp")
    xt = lk.sb("s_xt", [128, NCH * 512], BF16)
    xo = lk.sb("s_xo", [128, NCH * 512], BF16)
    sq_full = lk.sb("s_sq", [128, NCH * 512], F32)
    tsc = lk.sb("s_tsc", [128, 512], F32)
    gcol = lk.sb("s_gcol", [128, NCH], F32)
    bcol = lk.sb("s_bcol", [128, NCH], F32)
    kit = LNScratch(lk, 512)
    lk.block()
    kit.init_consts()

    def ld0(s):
        i = s.dma_start(out=gcol[:], in_=g_in)
        lk.inc(i, "ld", 16)
        i = s.dma_start(out=bcol[:], in_=b_in)
        lk.inc(i, "ld", 16)

    lk.on("sync", ld0)

    for (mo, w) in MT512:
        def ld_fn(s, mo=mo, w=w):
            lk.wait(s, "ev")
            for c in range(NCH):
                i = s.dma_start(out=xt[:, c * w:(c + 1) * w],
                                in_=xs_in[c * 128:(c + 1) * 128, mo:mo + w])
                lk.inc(i, "ld", 16)

        lk.on("sync", ld_fn)

        def get_x(c, w=w):
            return xt[:, c * w:(c + 1) * w]

        def put_out(c, v, w=w):
            v.tensor_tensor(tsc[:, 0:w], get_x(c), kit.p_rep[:, 0:w], ALU.subtract)
            v.tensor_tensor(tsc[:, 0:w], tsc[:, 0:w], kit.p_rep[:, 512:512 + w], ALU.mult)
            v.tensor_scalar(xo[:, c * w:(c + 1) * w], tsc[:, 0:w],
                            gcol[:, c:c + 1], bcol[:, c:c + 1], ALU.mult, ALU.add)

        emit_ln(lk, kit, get_x, put_out, w, sq_full, out_war_sems=("st",))

        def st_fn(s, mo=mo, w=w):
            lk.wait(s, "ev")
            for c in range(NCH):
                i = s.dma_start(out=out[c * 128:(c + 1) * 128, mo:mo + w],
                                in_=xo[:, c * w:(c + 1) * w])
                lk.inc(i, "st", 16)

        lk.on("sync", st_fn)

    def fin(s):
        lk.wait(s, "st")

    lk.on("sync", fin)
    return lk


def host_ln2(inputs, xs_list):
    lk = build_ln2()
    gi = host_prep_feat_cols(inputs["L0_attn_ln2_g"])
    bi = host_prep_feat_cols(inputs["L0_attn_ln2_b"])
    res = _run(lk, [{"xs": np.ascontiguousarray(xs_list[c]), "g2": gi, "b2": bi}
                    for c in range(NCORES)])
    return [np.asarray(r["x2b"]) for r in res]



def build_ffn(nmt=4, np2=2):
    NI = DI // 128
    lk = L("ffn")
    x2b_in = lk.din("x2b", [D, TLOC], BF16)
    xs_in = lk.din("xs", [D, TLOC], BF16)
    w1_in = lk.din("w1", [D, DI])
    b1_in = lk.din("b1", [128, NI])
    w2_in = lk.din("w2", [DI, D])
    b2_in = lk.din("b2", [128, NCH])
    s_out = lk.dout("sout", [D, TLOC], BF16)

    lk.sem("ld", "wt", "w2t", "rf", "mm", "sc", "ev", "st", "gp")
    x2b = lk.sb("s_x2b", [128, NCH * TLOC], BF16)
    xs = lk.sb("s_xs", [128, NCH * TLOC], BF16)
    w1 = lk.sb("s_w1", [128, NCH * DI], BF16)
    ring = lk.sb("s_ring", [128, 2 * D], BF16)
    b1c = lk.sb("s_b1c", [128, NI], F32)
    b2c = lk.sb("s_b2c", [128, NCH], F32)
    h = lk.sb("s_h", [128, NI * 512], BF16)
    sbuf_s = lk.sb("s_s", [128, NCH * TLOC], BF16)
    pool = PsumPool(lk, 2)
    pw2 = [lk.ps(f"p_w2{i}", [128, 512], F32) for i in range(4)]
    lk.block()

    def g0(g):
        load_weight_bf16(lk, g, w1, w1_in, NCH, DI)

    lk.on("gpsimd", g0)

    def ld0(s):
        for c in range(NCH):
            i = s.dma_start(out=x2b[:, c * TLOC:(c + 1) * TLOC],
                            in_=x2b_in[c * 128:(c + 1) * 128, :])
            lk.inc(i, "ld", 16)
        for c in range(NCH):
            i = s.dma_start(out=xs[:, c * TLOC:(c + 1) * TLOC],
                            in_=xs_in[c * 128:(c + 1) * 128, :])
            lk.inc(i, "ld", 16)
        i = s.dma_start(out=b1c[:], in_=b1_in)
        lk.inc(i, "ld", 16)
        i = s.dma_start(out=b2c[:], in_=b2_in)
        lk.inc(i, "ld", 16)

    lk.on("sync", ld0)

    ring_loads = [0]
    for (mo, w) in MT512[:nmt]:
        def evac_h(e, bank, n, mo2, w2, w=w):
            return e.tensor_scalar(h[:, n * 512:n * 512 + w2], bank[:, 0:w2],
                                   b1c[:, n:n + 1], 0.0, ALU.add, ALU.max)

        hw_ev = lk.c["ev"]
        mm_block(lk, pool,
                 get_lhsT=lambda kc, n: w1[:, kc * DI + n * 128:kc * DI + (n + 1) * 128],
                 get_rhs=lambda kc, mo2, w2, mo=mo: x2b[:, kc * TLOC + mo:kc * TLOC + mo + w2],
                 nK=NCH, n_list=range(NI), m_list=[(0, w)],
                 evac=evac_h, evac_engine="vector",
                 pre_wait=lambda t, c=lk.c["mm"]: (lk.wait(t, "wt"), lk.wait(t, "ld"),
                                                   lk.wait(t, "mm", c)))
        emit_barrier(lk, "vector", "ev")
        h_ev = lk.c["ev"]

        for p2 in range(np2):
            ngrp = [p2 * 4 + i for i in range(4)]

            def g_ring(g, mo=mo, p2=p2):
                for kc in range(NI):
                    slot = kc % 2
                    uses = ring_loads[0]
                    if uses >= 2:
                        need = uses - 1 - (uses - 1) // NI
                        g.wait_ge(lk.sems["rf"], need)
                    i = g.dma_start(out=ring[:, slot * D:(slot + 1) * D],
                                    in_=w2_in[kc * 128:(kc + 1) * 128, :])
                    lk.inc(i, "w2t", 16)
                    ring_loads[0] += 1

            lk.on("gpsimd", g_ring)

            def t_w2(t, mo=mo, w=w, p2=p2, ec=lk.c["ev"], base=ring_loads[0] - NI,
                     ngrp=tuple(ngrp)):
                lk.wait(t, "ev", ec)
                for kc in range(NI):
                    slot = kc % 2
                    t.wait_ge(lk.sems["w2t"], 16 * (base + kc + 1))
                    for ni, n in enumerate(ngrp):
                        i = lk.nc.tensor.matmul(
                            pw2[ni][:, 0:w],
                            ring[:, slot * D + n * 128:slot * D + (n + 1) * 128],
                            h[:, kc * 512:kc * 512 + w],
                            start=(kc == 0), stop=(kc == NI - 1))
                        if ni == 3 and kc < NI - 1:
                            i.then_inc(lk.sems["rf"], 1)
                            lk.c["rf"] += 1
                lk.inc(i, "mm")

            lk.on("tensor", t_w2)

            def v_w2(v, mo=mo, w=w, p2=p2, ngrp=tuple(ngrp)):
                lk.wait(v, "mm")
                for ni, n in enumerate(ngrp):
                    i = v.scalar_tensor_tensor(
                        sbuf_s[:, n * TLOC + mo:n * TLOC + mo + w],
                        pw2[ni][:, 0:w], b2c[:, n:n + 1],
                        xs[:, n * TLOC + mo:n * TLOC + mo + w], ALU.add, ALU.add)
                lk.mark(v, "ev")

            lk.on("vector", v_w2)

    def st0(s):
        lk.wait(s, "ev")
        for c in range(NCH):
            i = s.dma_start(out=s_out[c * 128:(c + 1) * 128, :],
                            in_=sbuf_s[:, c * TLOC:(c + 1) * TLOC])
            lk.inc(i, "st", 16)
        s.wait_ge(lk.sems["st"], lk.c["st"])

    lk.on("sync", st0)
    return lk


def host_ffn(inputs, x2b_list, xs_list, nmt=4, np2=2):
    lk = build_ffn(nmt, np2)
    w1h = _f32h(inputs["L0_attn_w1"])
    w2h = _f32h(inputs["L0_attn_w2"])
    b1h = host_prep_feat_cols(inputs["L0_attn_b1"])
    b2h = host_prep_feat_cols(inputs["L0_attn_b2"])
    res = _run(lk, [{"x2b": np.ascontiguousarray(x2b_list[c]),
                     "xs": np.ascontiguousarray(xs_list[c]),
                     "w1": w1h, "b1": b1h, "w2": w2h, "b2": b2h}
                    for c in range(NCORES)])
    return [np.asarray(r["sout"]) for r in res]



def build_combine():
    lk = L("comb")
    x_in = lk.din("xT", [D, TLOC])
    s_in = lk.din("sg", [D, TLOC], BF16)
    g_in = lk.din("gg", [D, TLOC], BF16)
    out = lk.dout("x1", [D, TLOC])
    lk.sem("ld", "ev", "st")
    xt = lk.sb("s_xt", [128, NCH * 512], F32)
    st_ = lk.sb("s_st", [128, NCH * 512], BF16)
    gt = lk.sb("s_gt", [128, NCH * 512], BF16)
    tf = lk.sb("s_tf", [128, 512], F32)
    x1 = lk.sb("s_x1", [128, NCH * 512], F32)
    lk.block()

    for (mo, w) in MT512:
        def ld_fn(s, mo=mo, w=w):
            lk.wait(s, "ev")
            for c in range(NCH):
                i = s.dma_start(out=xt[:, c * w:(c + 1) * w],
                                in_=x_in[c * 128:(c + 1) * 128, mo:mo + w])
                lk.inc(i, "ld", 16)
                i = s.dma_start(out=st_[:, c * w:(c + 1) * w],
                                in_=s_in[c * 128:(c + 1) * 128, mo:mo + w])
                lk.inc(i, "ld", 16)
                i = s.dma_start(out=gt[:, c * w:(c + 1) * w],
                                in_=g_in[c * 128:(c + 1) * 128, mo:mo + w])
                lk.inc(i, "ld", 16)

        lk.on("sync", ld_fn)

        def v_fn(v, mo=mo, w=w, stc=lk.c["st"]):
            lk.wait(v, "ld")
            lk.wait(v, "st", stc)
            for c in range(NCH):
                v.tensor_tensor(tf[:, 0:w], st_[:, c * w:(c + 1) * w],
                                gt[:, c * w:(c + 1) * w], ALU.mult)
                v.tensor_tensor(x1[:, c * w:(c + 1) * w], tf[:, 0:w],
                                xt[:, c * w:(c + 1) * w], ALU.add)
            lk.mark(v, "ev")

        lk.on("vector", v_fn)

        def st_fn(s, mo=mo, w=w):
            lk.wait(s, "ev")
            for c in range(NCH):
                i = s.dma_start(out=out[c * 128:(c + 1) * 128, mo:mo + w],
                                in_=x1[:, c * w:(c + 1) * w])
                lk.inc(i, "st", 16)

        lk.on("sync", st_fn)

    def fin(s):
        lk.wait(s, "st")

    lk.on("sync", fin)
    return lk


def host_combine(inputs, s_list, g_list, x_list=None):
    lk = build_combine()
    if x_list is None:
        x = np.asarray(inputs["L0_attn_x"])
        x_list = []
        for core in range(NCORES):
            b, j = divmod(core, 2)
            x_list.append(np.ascontiguousarray(x[b, j * TLOC:(j + 1) * TLOC, :].T))
    in_maps = []
    for core in range(NCORES):
        in_maps.append({"xT": _f32h(x_list[core]), "sg": np.ascontiguousarray(s_list[core]),
                        "gg": np.ascontiguousarray(g_list[core])})
    res = _run(lk, in_maps)
    return [np.asarray(r["x1"]) for r in res]



NDC = DN // 128


def build_l1front(ph=4):
    EXT = TLOC + 3
    lk = L("l1f")
    x1_in = lk.din("x1e", [D, EXT])
    win_in = lk.din("winT", [D, 2 * DN])
    cw_in = lk.din("convw", [128, NDC * 4])
    cb_in = lk.din("convb", [128, NDC])
    xp_in = lk.din("xprojT", [DN, 96])
    wdt_in = lk.din("wdtT", [DTR, DN])
    bdt_in = lk.din("bdt", [128, NDC])
    xn1_out = lk.dout("xn1", [D, TLOC], BF16)
    u_out = lk.dout("u", [DN, TLOC], BF16)
    dt_out = lk.dout("dt", [DN, TLOC], BF16)
    z_out = lk.dout("z", [DN, TLOC], BF16)
    B_out = lk.dout("Bs", [DS, TLOC])
    C_out = lk.dout("Cs", [DS, TLOC])

    lk.sem("ld", "wt", "nt", "nf", "mm", "sc", "ev", "st", "gp")
    NTE = EXT + 1
    xt = lk.sb("s_xt", [128, NCH * 512], F32)
    xn1 = lk.sb("s_xn1", [128, NCH * NTE], BF16)
    sq_full = lk.sb("s_sq", [128, NCH * 512], F32)
    nring = lk.sb("s_nring", [128, 2 * NCH * 128], BF16)
    xi = lk.sb("s_xi", [128, NDC * NTE], BF16)
    u_t = lk.sb("s_ut", [128, NDC * 512], BF16)
    z_t = lk.sb("s_zt", [128, NDC * 512], BF16)
    dt_t = lk.sb("s_dtt", [128, NDC * 512], BF16)
    cw = lk.sb("s_cw", [128, NDC * 4], F32)
    cb = lk.sb("s_cb", [128, NDC], F32)
    bdt = lk.sb("s_bdt", [128, NDC], F32)
    xp = lk.sb("s_xp", [128, NDC * 96], BF16)
    wdt = lk.sb("s_wdt", [DTR, DN], BF16)
    dtr_bf = lk.sb("s_dtrbf", [DTR, 512], BF16)
    bc_sb = lk.sb("s_bc", [DS, 2 * 512], F32)
    asc = lk.sb("s_asc", [128, 512], F32)
    csc = lk.sb("s_csc", [128, 512], F32)
    esc = lk.sb("s_esc", [128, 512], F32)
    cst = lk.sb("s_cstf", [128, 2], F32)
    kit = LNScratch(lk, 512)
    pool = PsumPool(lk, 2)
    p_dtr = lk.ps("p_dtr", [DTR, 512], F32)
    p_bc = lk.ps("p_bc", [128, 512], F32)
    lk.block()
    kit.init_consts()

    def g0(g):
        g.memset(cst[:, 0:1], -1.0)
        g.memset(cst[:, 1:2], 1.0)
        load_weight_bf16(lk, g, xp, xp_in, NDC, 96)
        i = g.dma_start(out=wdt[:], in_=wdt_in)
        lk.inc(i, "wt", 16)

    lk.on("gpsimd", g0)

    def ld0(s):
        for t, srcp in [(cw, cw_in), (cb, cb_in), (bdt, bdt_in)]:
            i = s.dma_start(out=t[:], in_=srcp)
            lk.inc(i, "ld", 16)

    lk.on("sync", ld0)

    widths = [259] + _tile_widths(EXT - 259, 512)
    t0 = 0
    for w in widths:
        o = t0

        def ld_fn(s, o=o, w=w):
            lk.wait(s, "ev")
            for c in range(NCH):
                i = s.dma_start(out=xt[:, c * w:(c + 1) * w],
                                in_=x1_in[c * 128:(c + 1) * 128, o:o + w])
                lk.inc(i, "ld", 16)

        lk.on("sync", ld_fn)

        def get_x(c, w=w):
            return xt[:, c * w:(c + 1) * w]

        def put_out(c, v, o=o, w=w):
            tmp = sq_full[:, c * w:(c + 1) * w]
            v.tensor_tensor(tmp, get_x(c), kit.p_rep[:, 0:w], ALU.subtract)
            v.tensor_tensor(xn1[:, c * NTE + o:c * NTE + o + w], tmp,
                            kit.p_rep[:, 512:512 + w], ALU.mult)

        emit_ln(lk, kit, get_x, put_out, w, sq_full, out_war_sems=("st",))

        def st_fn(s, o=o, w=w):
            if o + w > 3:
                lk.wait(s, "ev")
                so = max(o, 3)
                for c in range(NCH):
                    i = s.dma_start(out=xn1_out[c * 128:(c + 1) * 128, so - 3:o + w - 3],
                                    in_=xn1[:, c * NTE + so:c * NTE + o + w])
                    lk.inc(i, "st", 16)

        lk.on("sync", st_fn)
        t0 += w
    xn1_ev = lk.c["ev"]

    n_loads = [0]

    def stream_n(n):
        def g_n(g, n=n):
            uses = n_loads[0]
            if uses >= 2:
                g.wait_ge(lk.sems["nf"], uses - 1)
            for kc in range(NCH):
                i = g.dma_start(out=nring[:, (n % 2) * (NCH * 128) + kc * 128:
                                          (n % 2) * (NCH * 128) + (kc + 1) * 128],
                                in_=win_in[kc * 128:(kc + 1) * 128, n * 128:(n + 1) * 128])
                lk.inc(i, "nt", 16)
            n_loads[0] += 1

        lk.on("gpsimd", g_n)

    XI_M = [(0, 259), (259, 512), (771, 512), (1283, 512), (1795, 256)]
    Z_M = [(3, 512), (515, 512), (1027, 512), (1539, 512)]

    for n in (range(32) if ph >= 2 else []):
        stream_n(n)
        is_xi = n < 16
        mlist = XI_M if is_xi else Z_M

        def t_xz(t, n=n, mlist=mlist, nl=n_loads[0], ev_need=xn1_ev):
            lk.wait(t, "ev", ev_need)
            t.wait_ge(lk.sems["nt"], 16 * 8 * nl)
            for gi_, (mo, w) in enumerate(mlist):
                ib, bank = pool.acquire(t)
                for kc in range(NCH):
                    i = lk.nc.tensor.matmul(
                        bank[:, 0:w],
                        nring[:, (n % 2) * (NCH * 128) + kc * 128:(n % 2) * (NCH * 128) + (kc + 1) * 128],
                        xn1[:, kc * NTE + mo:kc * NTE + mo + w],
                        start=(kc == 0), stop=(kc == NCH - 1))
                lk.inc(i, "mm")
                t_xz.banks.append((ib, bank, mo, w, lk.c["mm"]))

        t_xz.banks = []
        lk.on("tensor", t_xz)

        def v_xz(v, n=n, is_xi=is_xi, banks=t_xz.banks, stc=lk.c["st"]):
            if not is_xi:
                lk.wait(v, "st", stc)
            for (ib, bank, mo, w, mmc) in banks:
                lk.wait(v, "mm", mmc)
                if is_xi:
                    i = v.tensor_copy(xi[:, n * NTE + mo:n * NTE + mo + w], bank[:, 0:w])
                else:
                    i = v.tensor_copy(z_t[:, mo - 3:mo - 3 + w], bank[:, 0:w])
                pool.freed(i, ib)
            i = v.tensor_copy(lk.dummy[:1, 0:8], lk.dummy[:1, 8:16])
            i.then_inc(lk.sems["nf"], 1)
            lk.c["nf"] += 1
            lk.mark(v, "ev")

        lk.on("vector", v_xz)

        if not is_xi:
            def st_z(s, n=n):
                lk.wait(s, "ev")
                i = s.dma_start(out=z_out[(n - 16) * 128:(n - 15) * 128, :],
                                in_=z_t[:, 0:TLOC])
                lk.inc(i, "st", 16)

            lk.on("sync", st_z)

    xi_ev = lk.c["ev"]

    for mt, (mo, w) in enumerate(MT512 if ph >= 3 else []):
        for ci in range(NDC):
            def v_conv(v, ci=ci, mo=mo, w=w, xe=xi_ev, stc=lk.c["st"], scc=lk.c["sc"]):
                lk.wait(v, "ev", xe)
                lk.wait(v, "st", stc)
                lk.wait(v, "sc", scc)
                base = ci * NTE + mo
                v.tensor_scalar(csc[:, 0:w], xi[:, base:base + w],
                                cw[:, ci * 4:ci * 4 + 1], 0.0, ALU.mult, ALU.add)
                v.scalar_tensor_tensor(esc[:, 0:w], xi[:, base + 1:base + 1 + w],
                                       cw[:, ci * 4 + 1:ci * 4 + 2], csc[:, 0:w],
                                       ALU.mult, ALU.add)
                v.scalar_tensor_tensor(csc[:, 0:w], xi[:, base + 2:base + 2 + w],
                                       cw[:, ci * 4 + 2:ci * 4 + 3], esc[:, 0:w],
                                       ALU.mult, ALU.add)
                i = v.scalar_tensor_tensor(esc[:, 0:w], xi[:, base + 3:base + 3 + w],
                                           cw[:, ci * 4 + 3:ci * 4 + 4], csc[:, 0:w],
                                           ALU.mult, ALU.add)
                lk.inc(i, "ev")

            lk.on("vector", v_conv)

            def s_u(s, ci=ci, w=w):
                lk.wait(s, "ev")
                i = s.activation(u_t[:, ci * 512:ci * 512 + w], esc[:, 0:w],
                                 AF.Silu, bias=cb[:, ci:ci + 1])
                lk.inc(i, "sc")

            lk.on("scalar", s_u)

        u_sc = lk.c["sc"]
        if ph < 4:
            continue

        def t_xp(t, mo=mo, w=w, us=u_sc):
            lk.wait(t, "sc", us)
            lk.wait(t, "ev")
            for kc in range(NDC):
                lk.nc.tensor.matmul(p_dtr[:, 0:w], xp[:, kc * 96:kc * 96 + 64],
                                    u_t[:, kc * 512:kc * 512 + w],
                                    start=(kc == 0), stop=(kc == NDC - 1))
            for kc in range(NDC):
                lk.nc.tensor.matmul(p_bc[0:16, 0:w], xp[:, kc * 96 + 64:kc * 96 + 80],
                                    u_t[:, kc * 512:kc * 512 + w],
                                    start=(kc == 0), stop=(kc == NDC - 1))
            for kc in range(NDC):
                i = lk.nc.tensor.matmul(p_bc[64:80, 0:w], xp[:, kc * 96 + 80:kc * 96 + 96],
                                        u_t[:, kc * 512:kc * 512 + w],
                                        start=(kc == 0), stop=(kc == NDC - 1))
            lk.inc(i, "mm")

        lk.on("tensor", t_xp)

        def v_xp(v, mo=mo, w=w):
            lk.wait(v, "mm")
            v.tensor_copy(dtr_bf[:, 0:w], p_dtr[:, 0:w])
            v.tensor_copy(bc_sb[:, 0:w], p_bc[0:16, 0:w])
            i = v.tensor_copy(bc_sb[:, 512:512 + w], p_bc[64:80, 0:w])
            lk.mark(v, "ev")

        lk.on("vector", v_xp)

        def st_bc(s, mo=mo, w=w):
            lk.wait(s, "ev")
            i = s.dma_start(out=B_out[:, mo:mo + w], in_=bc_sb[:, 0:w])
            lk.inc(i, "st", 16)
            i = s.dma_start(out=C_out[:, mo:mo + w], in_=bc_sb[:, 512:512 + w])
            lk.inc(i, "st", 16)
            for ci in range(NDC):
                i = s.dma_start(out=u_out[ci * 128:(ci + 1) * 128, mo:mo + w],
                                in_=u_t[:, ci * 512:ci * 512 + w])
                lk.inc(i, "st", 16)

        lk.on("sync", st_bc)

        for ci in range(NDC):
            def t_dt(t, ci=ci, w=w, ec=lk.c["ev"]):
                lk.wait(t, "ev", ec)
                ib, bank = pool.acquire(t)
                i = lk.nc.tensor.matmul(bank[:, 0:w], wdt[:, ci * 128:(ci + 1) * 128],
                                        dtr_bf[:, 0:w], start=True, stop=True)
                lk.inc(i, "mm")
                t_dt.bk = (ib, bank)

            lk.on("tensor", t_dt)
            ib, bank = t_dt.bk

            def v_dt1(v, ci=ci, w=w, ib=ib, bank=bank, scc=lk.c["sc"]):
                lk.wait(v, "mm")
                lk.wait(v, "sc", scc)
                i = v.tensor_scalar(asc[:, 0:w], bank[:, 0:w], bdt[:, ci:ci + 1], 0.0,
                                    ALU.add, ALU.add)
                pool.freed(i, ib)
                lk.mark(v, "ev")

            lk.on("vector", v_dt1)

            def s_dt(s, ci=ci, w=w):
                lk.wait(s, "ev")
                s.activation(csc[:, 0:w], asc[:, 0:w], AF.Abs)
                s.activation(csc[:, 0:w], csc[:, 0:w], AF.Exp, scale=cst[:, 0:1])
                i = s.activation(csc[:, 0:w], csc[:, 0:w], AF.Ln, bias=cst[:, 1:2])
                lk.inc(i, "sc")

            lk.on("scalar", s_dt)

            def v_dt2(v, ci=ci, w=w):
                lk.wait(v, "sc")
                v.tensor_scalar(esc[:, 0:w], asc[:, 0:w], 0.0, 0.0, ALU.max, ALU.add)
                i = v.tensor_tensor(dt_t[:, ci * 512:ci * 512 + w], esc[:, 0:w],
                                    csc[:, 0:w], ALU.add)
                lk.inc(i, "ev")

            lk.on("vector", v_dt2)

        dt_ev = lk.c["ev"]

        def st_dt(s, mo=mo, w=w, de=dt_ev):
            lk.wait(s, "ev", de)
            for ci in range(NDC):
                i = s.dma_start(out=dt_out[ci * 128:(ci + 1) * 128, mo:mo + w],
                                in_=dt_t[:, ci * 512:ci * 512 + w])
                lk.inc(i, "st", 16)

        lk.on("sync", st_dt)

    def fin(s):
        lk.wait(s, "st")

    lk.on("sync", fin)
    return lk


def build_l1front_psums(lk):
    pass


def host_l1front(inputs, x1_list, ph=4):
    lk = build_l1front(ph)
    winh = _f32h(np.asarray(inputs["L1_attn_W_in"]).T)
    cwh = _f32h(np.transpose(np.asarray(inputs["L1_attn_conv_w"]).reshape(NDC, 128, 4),
                             (1, 0, 2)).reshape(128, NDC * 4))
    cbh = host_prep_feat_cols(inputs["L1_attn_conv_b"])
    xph = _f32h(np.asarray(inputs["L1_attn_W_xproj"]).T)
    wdth = _f32h(np.asarray(inputs["L1_attn_W_dt"]).T)
    bdth = host_prep_feat_cols(inputs["L1_attn_b_dt"])
    in_maps = []
    for core in range(NCORES):
        b, j = divmod(core, 2)
        ext = np.zeros((D, TLOC + 3), np.float32)
        x1c = np.asarray(x1_list[core])
        ext[:, 3:] = x1c
        if j == 1:
            ext[:, 0:3] = np.asarray(x1_list[core - 1])[:, TLOC - 3:TLOC]
        in_maps.append({"x1e": np.ascontiguousarray(ext), "winT": winh, "convw": cwh,
                        "convb": cbh, "xprojT": xph, "wdtT": wdth, "bdt": bdth})
    res = _run(lk, in_maps)
    return {k: [np.asarray(r[k]) for r in res]
            for k in ["xn1", "u", "dt", "z", "Bs", "Cs"]}



def build_scan(nseg=8):
    NCB = 8
    NSEG = 8
    SG = T // NSEG
    lk = L("scan")
    dt_in = lk.din("dt", [1024, T], BF16)
    u_in = lk.din("u", [1024, T], BF16)
    b_in = lk.din("Bs", [DS, T])
    c_in = lk.din("Cs", [DS, T])
    al_in = lk.din("Alog", [128, NCB * DS])
    y_out = lk.dout("y", [1024, T], BF16)
    hst_out = lk.dout("hstd", [128, NCB * DS * 2])

    lk.sem("ld", "mm", "sc", "ev", "st", "gp")
    dt_s = lk.sb("s_dt", [128, NCB * SG], BF16)
    u_s = lk.sb("s_u", [128, NCB * SG], BF16)
    dtu = lk.sb("s_dtu", [128, NCB * SG], F32)
    brow = lk.sb("s_brow", [1, DS * 512], F32)
    crow = lk.sb("s_crow", [1, DS * 512], F32)
    asb = lk.sb("s_asb", [128, NCB * DS], F32)
    onesr = lk.sb("s_onesr", [1, 128], F32)
    abuf = lk.sb("s_abuf", [128, 2 * SG], F32)
    bbuf = lk.sb("s_bbuf", [128, 2 * SG], F32)
    hbuf = lk.sb("s_hbuf", [128, 2 * SG], F32)
    tmp = lk.sb("s_tmp", [128, SG], F32)
    brep = lk.sb("s_brep", [128, SG], F32)
    crep = lk.sb("s_crep", [128, SG], F32)
    hst = lk.sb("s_hst", [128, NCB * DS * 2], F32)
    yacc = lk.sb("s_yacc", [128, NCB * SG], F32)
    ybf = lk.sb("s_ybf", [128, NCB * SG], BF16)
    pool = PsumPool(lk, 2)
    lk.block()

    def g0(g):
        g.memset(onesr[:], 1.0)
        i = g.memset(hst[:], 0.0)
        lk.inc(i, "gp")

    lk.on("gpsimd", g0)

    def ld0(s):
        i = s.dma_start(out=asb[:], in_=al_in)
        lk.inc(i, "ld", 16)

    lk.on("sync", ld0)

    def s_a(s):
        lk.wait(s, "ld", 16)
        i = s.activation(asb[:], asb[:], AF.Exp)
        lk.inc(i, "sc")

    lk.on("scalar", s_a)

    def v_a(v):
        lk.wait(v, "sc")
        i = v.tensor_scalar(asb[:], asb[:], -1.0, 0.0, ALU.mult, ALU.add)
        lk.mark(v, "ev")

    lk.on("vector", v_a)
    a_ev = lk.c["ev"]

    it = 0
    ev_after_iter = []

    for seg in range(nseg):
        so = seg * SG

        def ld_seg(s, so=so, evn=lk.c["ev"]):
            lk.wait(s, "ev", evn)
            for cb in range(NCB):
                i = s.dma_start(out=dt_s[:, cb * SG:(cb + 1) * SG],
                                in_=dt_in[cb * 128:(cb + 1) * 128, so:so + SG])
                lk.inc(i, "ld", 16)
                i = s.dma_start(out=u_s[:, cb * SG:(cb + 1) * SG],
                                in_=u_in[cb * 128:(cb + 1) * 128, so:so + SG])
                lk.inc(i, "ld", 16)
            for si in range(DS):
                i = s.dma_start(out=brow[:1, si * SG:(si + 1) * SG],
                                in_=b_in[si:si + 1, so:so + SG])
                lk.inc(i, "ld", 16)
                i = s.dma_start(out=crow[:1, si * SG:(si + 1) * SG],
                                in_=c_in[si:si + 1, so:so + SG])
                lk.inc(i, "ld", 16)

        lk.on("sync", ld_seg)

        def v_dtu(v, stc=lk.c["st"]):
            lk.wait(v, "ld")
            lk.wait(v, "gp", 1)
            lk.wait(v, "st", stc)
            for cb in range(NCB):
                i = v.tensor_tensor(dtu[:, cb * SG:(cb + 1) * SG],
                                    dt_s[:, cb * SG:(cb + 1) * SG],
                                    u_s[:, cb * SG:(cb + 1) * SG], ALU.mult)
            lk.mark(v, "ev")

        lk.on("vector", v_dtu)

        for si in range(DS):
            def t_rep(t, so=so, si=si):
                lk.wait(t, "ld")
                ib0, b0 = pool.acquire(t)
                i = lk.nc.tensor.matmul(b0[:, 0:SG], onesr[:],
                                        brow[:1, si * SG:(si + 1) * SG],
                                        start=True, stop=True)
                lk.inc(i, "mm")
                ib1, b1 = pool.acquire(t)
                i = lk.nc.tensor.matmul(b1[:, 0:SG], onesr[:],
                                        crow[:1, si * SG:(si + 1) * SG],
                                        start=True, stop=True)
                lk.inc(i, "mm")
                t_rep.banks = [(ib0, b0), (ib1, b1)]

            lk.on("tensor", t_rep)
            (ib0, b0), (ib1, b1) = t_rep.banks

            def v_rep(v, ib0=ib0, b0=b0, ib1=ib1, b1=b1):
                lk.wait(v, "mm")
                i = v.tensor_copy(brep[:], b0[:, 0:SG])
                pool.freed(i, ib0)
                i = v.tensor_copy(crep[:], b1[:, 0:SG])
                pool.freed(i, ib1)

            lk.on("vector", v_rep)

            for cb in range(NCB):
                slot = it % 2

                def s_agen(s, cb=cb, si=si, slot=slot, it=it):
                    if it >= 2:
                        lk.wait(s, "ev", ev_after_iter[it - 2])
                    else:
                        lk.wait(s, "ev", a_ev)
                    lk.wait(s, "ld")
                    i = s.activation(abuf[:, slot * SG:(slot + 1) * SG],
                                     dt_s[:, cb * SG:(cb + 1) * SG], AF.Exp,
                                     scale=asb[:, cb * DS + si:cb * DS + si + 1])
                    lk.inc(i, "sc")

                lk.on("scalar", s_agen)

                def v_scan(v, cb=cb, si=si, slot=slot, scc=lk.c["sc"]):
                    lk.wait(v, "sc", scc)
                    a = abuf[:, slot * SG:(slot + 1) * SG]
                    b = bbuf[:, slot * SG:(slot + 1) * SG]
                    hh = hbuf[:, slot * SG:(slot + 1) * SG]
                    v.tensor_tensor(b, dtu[:, cb * SG:(cb + 1) * SG], brep[:], ALU.mult)
                    v.tensor_tensor_scan(hh, a, b,
                                         hst[:, 2 * (cb * DS + si) + 1:2 * (cb * DS + si) + 2],
                                         ALU.mult, ALU.add)
                    if si == 0:
                        v.tensor_tensor(yacc[:, cb * SG:(cb + 1) * SG], hh, crep[:],
                                        ALU.mult)
                    else:
                        v.tensor_tensor(tmp[:], hh, crep[:], ALU.mult)
                        v.tensor_tensor(yacc[:, cb * SG:(cb + 1) * SG],
                                        yacc[:, cb * SG:(cb + 1) * SG], tmp[:],
                                        ALU.add)
                    i = v.tensor_copy(hst[:, 2 * (cb * DS + si):2 * (cb * DS + si) + 2],
                                      hbuf[:, slot * SG + SG - 2:slot * SG + SG])
                    lk.inc(i, "ev")

                lk.on("vector", v_scan)
                ev_after_iter.append(lk.c["ev"])
                it += 1

        def v_y(v, stc=lk.c["st"]):
            lk.wait(v, "st", stc)
            for cb in range(NCB):
                i = v.tensor_copy(ybf[:, cb * SG:(cb + 1) * SG],
                                  yacc[:, cb * SG:(cb + 1) * SG])
            lk.mark(v, "ev")

        lk.on("vector", v_y)

        def st_y(s, so=so):
            lk.wait(s, "ev")
            for cb in range(NCB):
                i = s.dma_start(out=y_out[cb * 128:(cb + 1) * 128, so:so + SG],
                                in_=ybf[:, cb * SG:(cb + 1) * SG])
                lk.inc(i, "st", 16)

        lk.on("sync", st_y)


    def st_h(s):
        lk.wait(s, "ev")
        i = s.dma_start(out=hst_out, in_=hst[:])
        lk.inc(i, "st", 16)

    lk.on("sync", st_h)

    def fin(s):
        lk.wait(s, "st")

    lk.on("sync", fin)
    return lk


def host_scan(inputs, l1f, nseg=8):
    lk = build_scan(nseg)
    al = np.asarray(inputs["L1_attn_A_log"], np.float32)
    in_maps = []
    for core in range(NCORES):
        b, ch = divmod(core, 2)
        r0, r1 = ch * 1024, (ch + 1) * 1024
        dt_cs = np.concatenate([np.asarray(l1f["dt"][2 * b])[r0:r1],
                                np.asarray(l1f["dt"][2 * b + 1])[r0:r1]], axis=1)
        u_cs = np.concatenate([np.asarray(l1f["u"][2 * b])[r0:r1],
                               np.asarray(l1f["u"][2 * b + 1])[r0:r1]], axis=1)
        Bf = np.concatenate([np.asarray(l1f["Bs"][2 * b]),
                             np.asarray(l1f["Bs"][2 * b + 1])], axis=1)
        Cf = np.concatenate([np.asarray(l1f["Cs"][2 * b]),
                             np.asarray(l1f["Cs"][2 * b + 1])], axis=1)
        alc = np.ascontiguousarray(
            np.transpose(al[r0:r1].reshape(8, 128, DS), (1, 0, 2)).reshape(128, 8 * DS))
        in_maps.append({"dt": np.ascontiguousarray(dt_cs), "u": np.ascontiguousarray(u_cs),
                        "Bs": _f32h(Bf), "Cs": _f32h(Cf), "Alog": alc})
    res = _run(lk, in_maps)
    return [np.asarray(r["y"]) for r in res], [np.asarray(r["hstd"]) for r in res]



def build_gate_s1():
    lk = L("gates1")
    y_in = lk.din("y", [DN, TLOC], BF16)
    u_in = lk.din("u", [DN, TLOC], BF16)
    z_in = lk.din("z", [DN, TLOC], BF16)
    dsk_in = lk.din("dsk", [128, NDC])
    wout_in = lk.din("woutT", [DN, D])
    xn1_in = lk.din("xn1", [D, TLOC], BF16)
    gw_in = lk.din("gateWp", [128, NCH * E])
    gb_in = lk.din("gateb", [1, E])
    id_in = lk.din("ident", [128, 128])
    s1_out = lk.dout("s1", [D, TLOC], BF16)
    oh1_out = lk.dout("oh1f", [E, TLOC])
    oh2_out = lk.dout("oh2f", [E, TLOC])
    v1_out = lk.dout("v1f", [1, TLOC])
    v2_out = lk.dout("v2f", [1, TLOC])

    lk.sem("ld", "wt", "mm", "sc", "ev", "st", "gp")
    yt = lk.sb("s_yt", [128, NDC * 512], BF16)
    ut = lk.sb("s_ut", [128, NDC * 512], BF16)
    zt = lk.sb("s_zt", [128, NDC * 512], BF16)
    zs = lk.sb("s_zs", [128, 512], BF16)
    ydt = lk.sb("s_ydt", [128, 512], F32)
    yg = lk.sb("s_yg", [128, NDC * 512], BF16)
    wout = lk.sb("s_wout", [128, NDC * D], BF16)
    dsk = lk.sb("s_dsk", [128, NDC], F32)
    s1t = lk.sb("s_s1t", [128, NCH * 512], BF16)
    xn1 = lk.sb("s_xn1g", [128, NCH * TLOC], BF16)
    gwsb = lk.sb("s_gwp", [128, NCH * E], BF16)
    gbrep = lk.sb("s_gbrep", [128, E], F32)
    gbrow = lk.sb("s_gbrow", [1, E], F32)
    ident = lk.sb("s_ident", [128, 128], F32)
    onesr = lk.sb("s_onesr1", [1, 128], F32)
    gla = lk.sb("s_gla", [128, 16 * E], F32)
    oh1a = lk.sb("s_oh1a", [128, 16 * E], F32)
    oh2a = lk.sb("s_oh2a", [128, 16 * E], F32)
    mska = lk.sb("s_mska", [128, 16 * E], F32)
    glm16 = lk.sb("s_glm16", [128, 16], F32)
    va = lk.sb("s_va", [128, 32], F32)
    ohT1 = lk.sb("s_ohT1", [8, TLOC], F32)
    ohT2 = lk.sb("s_ohT2", [8, TLOC], F32)
    vT = lk.sb("s_vT", [2, TLOC], F32)
    pool = PsumPool(lk, 2)
    p_g = lk.ps("p_g", [128, 8], F32)
    p_t1 = lk.ps("p_t1", [8, 128], F32)
    p_t2 = lk.ps("p_t2", [8, 128], F32)
    p_t3 = lk.ps("p_t3", [2, 128], F32)
    p_gb = lk.ps("p_gb", [128, E], F32)
    lk.block()

    def g0(g):
        i = g.memset(onesr[:], 1.0)
        lk.inc(i, "gp")
        load_weight_bf16(lk, g, wout, wout_in, NDC, D)
        i = g.dma_start(out=gwsb[:], in_=gw_in)
        lk.inc(i, "wt", 16)

    lk.on("gpsimd", g0)

    def ld0(s):
        for t, srcp in [(dsk, dsk_in), (gbrow, gb_in), (ident, id_in)]:
            i = s.dma_start(out=t[:], in_=srcp)
            lk.inc(i, "ld", 16)
        for c in range(NCH):
            i = s.dma_start(out=xn1[:, c * TLOC:(c + 1) * TLOC],
                            in_=xn1_in[c * 128:(c + 1) * 128, :])
            lk.inc(i, "ld", 16)

    lk.on("sync", ld0)

    def t_gb(t):
        lk.wait(t, "ld", 48)
        lk.wait(t, "gp", 1)
        i = lk.nc.tensor.matmul(p_gb[:, 0:E], onesr[:], gbrow[:1, 0:E],
                                start=True, stop=True)
        lk.inc(i, "mm")

    lk.on("tensor", t_gb)

    def v_gb(v):
        lk.wait(v, "mm")
        i = v.tensor_copy(gbrep[:], p_gb[:, 0:E])
        lk.mark(v, "ev")

    lk.on("vector", v_gb)

    for (mo, w) in MT512:
        def ld_t(s, mo=mo, w=w, evc=lk.c["ev"]):
            lk.wait(s, "ev", evc)
            for ci in range(NDC):
                i = s.dma_start(out=yt[:, ci * 512:ci * 512 + w],
                                in_=y_in[ci * 128:(ci + 1) * 128, mo:mo + w])
                lk.inc(i, "ld", 16)
                i = s.dma_start(out=ut[:, ci * 512:ci * 512 + w],
                                in_=u_in[ci * 128:(ci + 1) * 128, mo:mo + w])
                lk.inc(i, "ld", 16)
                i = s.dma_start(out=zt[:, ci * 512:ci * 512 + w],
                                in_=z_in[ci * 128:(ci + 1) * 128, mo:mo + w])
                lk.inc(i, "ld", 16)

        lk.on("sync", ld_t)

        for ci in range(NDC):
            def v_yd(v, ci=ci, w=w, scc=lk.c["sc"]):
                lk.wait(v, "ld")
                lk.wait(v, "sc", scc)
                i = v.scalar_tensor_tensor(ydt[:, 0:w], ut[:, ci * 512:ci * 512 + w],
                                           dsk[:, ci:ci + 1], yt[:, ci * 512:ci * 512 + w],
                                           ALU.mult, ALU.add)
                lk.inc(i, "ev")

            lk.on("vector", v_yd)

            def s_z(s, ci=ci, w=w):
                lk.wait(s, "ev")
                i = s.activation(zs[:, 0:w], zt[:, ci * 512:ci * 512 + w], AF.Silu)
                lk.inc(i, "sc")

            lk.on("scalar", s_z)

            def v_yg(v, ci=ci, w=w):
                lk.wait(v, "sc")
                i = v.tensor_tensor(yg[:, ci * 512:ci * 512 + w], ydt[:, 0:w],
                                    zs[:, 0:w], ALU.mult)
                lk.inc(i, "ev")

            lk.on("vector", v_yg)

        def evac_s1(e, bank, n, mo2, w2):
            return e.tensor_copy(s1t[:, n * 512:n * 512 + w2], bank[:, 0:w2])

        mm_block(lk, pool,
                 get_lhsT=lambda kc, n: wout[:, kc * D + n * 128:kc * D + (n + 1) * 128],
                 get_rhs=lambda kc, mo2, w2: yg[:, kc * 512:kc * 512 + w2],
                 nK=NDC, n_list=range(NCH), m_list=[(0, w)],
                 evac=evac_s1, evac_engine="vector",
                 pre_wait=lambda t, c=lk.c["ev"]: (lk.wait(t, "wt", 16 * 16),
                                                   lk.wait(t, "ev", c)))
        emit_barrier(lk, "vector", "ev")

        def st_s1(s, mo=mo, w=w):
            lk.wait(s, "ev")
            for c in range(NCH):
                i = s.dma_start(out=s1_out[c * 128:(c + 1) * 128, mo:mo + w],
                                in_=s1t[:, c * 512:c * 512 + w])
                lk.inc(i, "st", 16)

        lk.on("sync", st_s1)

    for lc in range(16):
        lo = lc * 128

        def t_gl(t, lo=lo, evc=lk.c["ev"]):
            lk.wait(t, "ev", evc)
            for kc in range(NCH):
                i = lk.nc.tensor.matmul(p_g[:, 0:E],
                                        xn1[:, kc * TLOC + lo:kc * TLOC + lo + 128],
                                        gwsb[:, kc * E:(kc + 1) * E],
                                        start=(kc == 0), stop=(kc == NCH - 1))
            lk.inc(i, "mm")

        lk.on("tensor", t_gl)

        def v_ev(v, lc=lc):
            lk.wait(v, "mm")
            i = v.tensor_tensor(gla[:, lc * E:(lc + 1) * E], p_g[:, 0:E], gbrep[:],
                                ALU.add)
            lk.inc(i, "ev")

        lk.on("vector", v_ev)

    gl3 = gla.ap().rearrange("p (l e) -> p l e", e=E)
    oh13 = oh1a.ap().rearrange("p (l e) -> p l e", e=E)
    oh23 = oh2a.ap().rearrange("p (l e) -> p l e", e=E)
    msk3 = mska.ap().rearrange("p (l e) -> p l e", e=E)
    va3 = va.ap().rearrange("p (l k) -> p l k", k=2)
    m1b = va3[:, :, 0:1].broadcast_to([128, 16, E])
    m2b = va3[:, :, 1:2].broadcast_to([128, 16, E])
    g16b = glm16.ap().unsqueeze(2).broadcast_to([128, 16, E])

    def v_b1(v):
        sp = lambda: v.tensor_copy(ydt[:, 0:512], ydt[:, 0:512])
        v.reduce_max(glm16[:], gl3, axis=AX)
        sp()
        i = v.tensor_tensor(gl3, gl3, g16b, ALU.subtract)
        lk.inc(i, "ev")

    lk.on("vector", v_b1)

    def s_b1(s):
        lk.wait(s, "ev")
        i = s.activation(gla[:], gla[:], AF.Exp)
        lk.inc(i, "sc")

    lk.on("scalar", s_b1)

    def v_b2(v):
        lk.wait(v, "sc")
        sp = lambda: v.tensor_copy(ydt[:, 0:512], ydt[:, 0:512])
        v.reduce_sum(glm16[:], gl3, axis=AX)
        sp()
        v.reciprocal(glm16[:], glm16[:])
        sp()
        v.tensor_tensor(gl3, gl3, g16b, ALU.mult)
        v.reduce_max(va3[:, :, 0:1], gl3, axis=AX)
        sp()
        v.tensor_tensor(oh13, gl3, m1b, ALU.is_ge)
        v.tensor_scalar(mska[:], oh1a[:], -1.0, 1.0, ALU.mult, ALU.add)
        v.tensor_tensor(mska[:], gla[:], mska[:], ALU.mult)
        v.reduce_max(va3[:, :, 1:2], msk3, axis=AX)
        sp()
        i = v.tensor_tensor(oh23, msk3, m2b, ALU.is_ge)
        lk.inc(i, "ev")

    lk.on("vector", v_b2)

    for lc in range(16):
        lo = lc * 128

        def t_tr(t, lc=lc, evc=lk.c["ev"]):
            lk.wait(t, "ev", evc)
            lk.nc.tensor.transpose(p_t1[:, 0:128], oh1a[:, lc * E:(lc + 1) * E], ident[:])
            lk.nc.tensor.transpose(p_t2[:, 0:128], oh2a[:, lc * E:(lc + 1) * E], ident[:])
            i = lk.nc.tensor.transpose(p_t3[:, 0:128], va[:, 2 * lc:2 * lc + 2], ident[:])
            lk.inc(i, "mm")

        lk.on("tensor", t_tr)

        def v_tr(v, lo=lo):
            lk.wait(v, "mm")
            v.tensor_copy(ohT1[:, lo:lo + 128], p_t1[:, 0:128])
            v.tensor_copy(ohT2[:, lo:lo + 128], p_t2[:, 0:128])
            i = v.tensor_copy(vT[:, lo:lo + 128], p_t3[:, 0:128])
            lk.mark(v, "ev")

        lk.on("vector", v_tr)

    def st1(s):
        lk.wait(s, "ev")
        i = s.dma_start(out=oh1_out, in_=ohT1[:])
        lk.inc(i, "st", 16)
        i = s.dma_start(out=oh2_out, in_=ohT2[:])
        lk.inc(i, "st", 16)
        i = s.dma_start(out=v1_out, in_=vT[0:1, :])
        lk.inc(i, "st", 16)
        i = s.dma_start(out=v2_out, in_=vT[1:2, :])
        lk.inc(i, "st", 16)
        s.wait_ge(lk.sems["st"], lk.c["st"])

    lk.on("sync", st1)
    return lk


def host_gate_s1(inputs, l1f, y_scan):
    lk = build_gate_s1()
    dskh = host_prep_feat_cols(inputs["L1_attn_D_skip"])
    wouth = _f32h(np.asarray(inputs["L1_attn_W_out"]).T)
    gwp = _f32h(np.transpose(np.asarray(inputs["L1_ffn_gate_W"]).reshape(NCH, 128, E),
                             (1, 0, 2)).reshape(128, NCH * E))
    gbh = _f32h(np.asarray(inputs["L1_ffn_gate_b"]).reshape(1, E))
    idh = np.eye(128, dtype=np.float32)
    in_maps = []
    for core in range(NCORES):
        b, j = divmod(core, 2)
        yo = np.concatenate([np.asarray(y_scan[2 * b])[:, j * TLOC:(j + 1) * TLOC],
                             np.asarray(y_scan[2 * b + 1])[:, j * TLOC:(j + 1) * TLOC]],
                            axis=0)
        in_maps.append({"y": np.ascontiguousarray(yo),
                        "u": np.ascontiguousarray(l1f["u"][core]),
                        "z": np.ascontiguousarray(l1f["z"][core]),
                        "dsk": dskh, "woutT": wouth,
                        "xn1": np.ascontiguousarray(l1f["xn1"][core]),
                        "gateWp": gwp, "gateb": gbh, "ident": idh})
    res = _run(lk, in_maps)
    return {k: [np.asarray(r[k]) for r in res]
            for k in ["s1", "oh1f", "oh2f", "v1f", "v2f"]}



def build_moe():
    lk = L("moe")
    xn1_in = lk.din("xn1", [D, TLOC], BF16)
    oh1_in = lk.din("oh1f", [E, TLOC])
    oh2_in = lk.din("oh2f", [E, TLOC])
    v1_in = lk.din("v1f", [1, TLOC])
    v2_in = lk.din("v2f", [1, TLOC])
    w1_in = lk.din("W1", [E * D, HMOE])
    w2_in = lk.din("W2", [E * HMOE, D])
    b1_in = lk.din("b1c", [128, E * 16])
    b2_in = lk.din("b2c", [128, E * 8])
    sel_in = lk.din("sel8", [E, E * 128])
    mg_in = lk.din("mlng", [128, NCH])
    mb_in = lk.din("mlnb", [128, NCH])
    g1_out = lk.dout("g1", [D, TLOC], BF16)

    lk.sem("ld", "wt", "mm", "sc", "ev", "st", "gp")
    xn1 = lk.sb("m_xn1", [128, NCH * TLOC], BF16)
    boost = lk.sb("m_boost", [128, NCH * TLOC], BF16)
    g1acc = lk.sb("m_g1acc", [128, NCH * TLOC], BF16)
    w1e = lk.sb("m_w1e", [128, NCH * HMOE], BF16)
    w2e = lk.sb("m_w2e", [128, NDC * D], BF16)
    h_sb = lk.sb("m_h", [128, NDC * 512], BF16)
    ohf = lk.sb("m_ohf", [E, 2 * TLOC], BF16)
    vf = lk.sb("m_vf", [1, 2 * TLOC], BF16)
    selc = lk.sb("m_sel", [E, E * 128], BF16)
    mrep = lk.sb("m_mrep", [128, 512], BF16)
    vrep = lk.sb("m_vrep", [128, 512], F32)
    eot = lk.sb("m_eot", [128, 512], BF16)
    tmpf = lk.sb("m_tmpf", [128, 512], F32)
    b1c = lk.sb("m_b1c", [128, E * 16], F32)
    b2c = lk.sb("m_b2c", [128, E * 8], F32)
    mg = lk.sb("m_mg", [128, NCH], F32)
    mb = lk.sb("m_mb", [128, NCH], F32)
    onesr = lk.sb("m_onesr", [1, 128], BF16)
    kit = LNScratch(lk, 512, name="mln")
    pool = PsumPool(lk, 2)
    lk.block()
    kit.init_consts()

    def g0(g):
        g.memset(onesr[:], 1.0)
        i = g.dma_start(out=ohf[:, 0:TLOC], in_=oh1_in)
        lk.inc(i, "wt", 16)
        i = g.dma_start(out=ohf[:, TLOC:2 * TLOC], in_=oh2_in)
        lk.inc(i, "wt", 16)
        i = g.dma_start(out=vf[:, 0:TLOC], in_=v1_in)
        lk.inc(i, "wt", 16)
        i = g.dma_start(out=vf[:, TLOC:2 * TLOC], in_=v2_in)
        lk.inc(i, "wt", 16)
        i = g.dma_start(out=selc[:], in_=sel_in)
        lk.inc(i, "wt", 16)
        i = g.nop()
        lk.inc(i, "gp")

    lk.on("gpsimd", g0)

    def ld0(s):
        for c in range(NCH):
            i = s.dma_start(out=xn1[:, c * TLOC:(c + 1) * TLOC],
                            in_=xn1_in[c * 128:(c + 1) * 128, :])
            lk.inc(i, "ld", 16)
        i = s.dma_start(out=b1c[:], in_=b1_in)
        lk.inc(i, "ld", 16)
        i = s.dma_start(out=b2c[:], in_=b2_in)
        lk.inc(i, "ld", 16)
        i = s.dma_start(out=mg[:], in_=mg_in)
        lk.inc(i, "ld", 16)
        i = s.dma_start(out=mb[:], in_=mb_in)
        lk.inc(i, "ld", 16)

    lk.on("sync", ld0)

    def v_init(v):
        lk.wait(v, "ld", 16 * NCH)
        for c in range(NCH):
            v.tensor_copy(boost[:, c * TLOC:(c + 1) * TLOC], xn1[:, c * TLOC:(c + 1) * TLOC])
            i = v.tensor_copy(g1acc[:, c * TLOC:(c + 1) * TLOC], xn1[:, c * TLOC:(c + 1) * TLOC])
        lk.mark(v, "ev")

    lk.on("vector", v_init)

    for p2 in range(2):
        rhs_t = xn1 if p2 == 0 else boost
        for e in range(E):
            def g_we(g, e=e, wtc=lk.c["rf"] if False else None, evc=lk.c["ev"]):
                g.wait_ge(lk.sems["mm"], lk.c["mm"])
                for kc in range(NCH):
                    i = g.dma_start(out=w1e[:, kc * HMOE:(kc + 1) * HMOE],
                                    in_=w1_in[e * D + kc * 128:e * D + (kc + 1) * 128, :])
                    lk.inc(i, "wt", 16)
                for kc in range(NDC):
                    i = g.dma_start(out=w2e[:, kc * D:(kc + 1) * D],
                                    in_=w2_in[e * HMOE + kc * 128:e * HMOE + (kc + 1) * 128, :])
                    lk.inc(i, "wt", 16)

            lk.on("gpsimd", g_we)
            wt_done = lk.c["wt"]

            for (mo, w) in MT512:
                def t_rep(t, e=e, p2=p2, mo=mo, w=w, evc=lk.c["ev"]):
                    lk.wait(t, "ev", evc)
                    lk.wait(t, "ld")
                    lk.wait(t, "wt", 16 * 5)
                    lk.wait(t, "gp", 1)
                    ib0, b0 = pool.acquire(t)
                    i = lk.nc.tensor.matmul(b0[:, 0:w], selc[:, e * 128:(e + 1) * 128],
                                            ohf[:, p2 * TLOC + mo:p2 * TLOC + mo + w],
                                            start=True, stop=True)
                    lk.inc(i, "mm")
                    ib1, b1 = pool.acquire(t)
                    i = lk.nc.tensor.matmul(b1[:, 0:w], onesr[:],
                                            vf[:1, p2 * TLOC + mo:p2 * TLOC + mo + w],
                                            start=True, stop=True)
                    lk.inc(i, "mm")
                    t_rep.banks = [(ib0, b0), (ib1, b1)]

                lk.on("tensor", t_rep)
                (ib0, b0), (ib1, b1) = t_rep.banks

                def v_rep(v, ib0=ib0, b0=b0, ib1=ib1, b1=b1, w=w):
                    lk.wait(v, "mm")
                    i = v.tensor_copy(mrep[:, 0:w], b0[:, 0:w])
                    pool.freed(i, ib0)
                    i = v.tensor_copy(vrep[:, 0:w], b1[:, 0:w])
                    pool.freed(i, ib1)
                    lk.mark(v, "ev")

                lk.on("vector", v_rep)

                def evac_h(e_, bank, n, mo2, w2, e=e):
                    return e_.tensor_scalar(h_sb[:, n * 512:n * 512 + w2], bank[:, 0:w2],
                                            b1c[:, e * 16 + n:e * 16 + n + 1], 0.0,
                                            ALU.add, ALU.max)

                mm_block(lk, pool,
                         get_lhsT=lambda kc, n: w1e[:, kc * HMOE + n * 128:kc * HMOE + (n + 1) * 128],
                         get_rhs=lambda kc, mo2, w2, rhs_t=rhs_t, mo=mo: rhs_t[:, kc * TLOC + mo:kc * TLOC + mo + w2],
                         nK=NCH, n_list=range(NDC), m_list=[(0, w)],
                         evac=evac_h, evac_engine="vector",
                         pre_wait=lambda t, wd=wt_done, c=lk.c["ev"]: (lk.wait(t, "wt", wd),
                                                                       lk.wait(t, "ev", c)))

                def v_mask(v, w=w):
                    for n in range(NDC):
                        i = v.tensor_tensor(h_sb[:, n * 512:n * 512 + w],
                                            h_sb[:, n * 512:n * 512 + w],
                                            mrep[:, 0:w], ALU.mult)
                    lk.mark(v, "ev")

                lk.on("vector", v_mask)

                def evac_o(e_, bank, n, mo2, w2, e=e, p2=p2, mo=mo):
                    e_.tensor_scalar(eot[:, 0:w2], bank[:, 0:w2],
                                     b2c[:, e * 8 + n:e * 8 + n + 1], 0.0, ALU.add, ALU.add)
                    e_.tensor_tensor(tmpf[:, 0:w2], eot[:, 0:w2], vrep[:, 0:w2], ALU.mult)
                    i = e_.tensor_tensor(g1acc[:, n * TLOC + mo:n * TLOC + mo + w2],
                                         g1acc[:, n * TLOC + mo:n * TLOC + mo + w2],
                                         tmpf[:, 0:w2], ALU.add)
                    if p2 == 0:
                        i = e_.scalar_tensor_tensor(boost[:, n * TLOC + mo:n * TLOC + mo + w2],
                                                    eot[:, 0:w2], 0.5,
                                                    boost[:, n * TLOC + mo:n * TLOC + mo + w2],
                                                    ALU.mult, ALU.add)
                    return i

                mm_block(lk, pool,
                         get_lhsT=lambda kc, n: w2e[:, kc * D + n * 128:kc * D + (n + 1) * 128],
                         get_rhs=lambda kc, mo2, w2: h_sb[:, kc * 512:kc * 512 + w2],
                         nK=NDC, n_list=range(NCH), m_list=[(0, w)],
                         evac=evac_o, evac_engine="vector",
                         pre_wait=lambda t, c=lk.c["ev"]: lk.wait(t, "ev", c))
                emit_barrier(lk, "vector", "ev")

    sq_full = h_sb.ap().bitcast(F32)
    tsc = tmpf
    g1o = w1e
    for (mo, w) in MT512:
        def get_x(c, mo=mo, w=w):
            return g1acc[:, c * TLOC + mo:c * TLOC + mo + w]

        def put_out(c, v, w=w):
            v.tensor_tensor(tsc[:, 0:w], get_x(c), kit.p_rep[:, 0:w], ALU.subtract)
            v.tensor_tensor(tsc[:, 0:w], tsc[:, 0:w], kit.p_rep[:, 512:512 + w], ALU.mult)
            v.tensor_scalar(g1o[:, c * 512:c * 512 + w], tsc[:, 0:w],
                            mg[:, c:c + 1], mb[:, c:c + 1], ALU.mult, ALU.add)

        emit_ln(lk, kit, get_x, put_out, w, sq_full, out_war_sems=("st",))

        def s_silu(s, mo=mo, w=w):
            lk.wait(s, "ev")
            for c in range(NCH):
                i = s.activation(g1o[:, c * 512:c * 512 + w], g1o[:, c * 512:c * 512 + w],
                                 AF.Silu)
            lk.inc(i, "sc")

        lk.on("scalar", s_silu)

        def st_g(s, mo=mo, w=w):
            lk.wait(s, "sc")
            for c in range(NCH):
                i = s.dma_start(out=g1_out[c * 128:(c + 1) * 128, mo:mo + w],
                                in_=g1o[:, c * 512:c * 512 + w])
                lk.inc(i, "st", 16)

        lk.on("sync", st_g)

    def fin(s):
        lk.wait(s, "st")

    lk.on("sync", fin)
    return lk


def host_moe(inputs, l1f, g1r):
    lk = build_moe()
    w1h = _f32h(np.asarray(inputs["L1_ffn_W1"]).reshape(E * D, HMOE))
    w2h = _f32h(np.asarray(inputs["L1_ffn_W2"]).reshape(E * HMOE, D))
    b1h = _f32h(np.concatenate([np.asarray(inputs["L1_ffn_b1"])[e].reshape(16, 128).T
                                for e in range(E)], axis=1))
    b2h = _f32h(np.concatenate([np.asarray(inputs["L1_ffn_b2"])[e].reshape(8, 128).T
                                for e in range(E)], axis=1))
    selh = _f32h(np.kron(np.eye(E, dtype=np.float32), np.ones((1, 128), np.float32)))
    mgh = host_prep_feat_cols(inputs["L1_ffn_ln_gamma"])
    mbh = host_prep_feat_cols(inputs["L1_ffn_ln_beta"])
    in_maps = []
    for core in range(NCORES):
        in_maps.append({"xn1": np.ascontiguousarray(l1f["xn1"][core]),
                        "oh1f": _f32h(g1r["oh1f"][core]), "oh2f": _f32h(g1r["oh2f"][core]),
                        "v1f": _f32h(g1r["v1f"][core]), "v2f": _f32h(g1r["v2f"][core]),
                        "W1": w1h, "W2": w2h, "b1c": b1h, "b2c": b2h, "sel8": selh,
                        "mlng": mgh, "mlnb": mbh})
    res = _run(lk, in_maps)
    return [np.asarray(r["g1"]) for r in res]



def kernel(**inputs):
    xn, x2 = host_xn(inputs)
    qf, kvp = host_perf_kv(inputs, x2)
    xs = host_attn(inputs, qf, kvp, xn)
    x2b = host_ln2(inputs, xs)
    s0 = host_ffn(inputs, x2b, xs)
    g0 = host_rwkv(inputs, xn)
    x1 = host_combine(inputs, s0, g0)
    l1f = host_l1front(inputs, x1)
    y, _hst = host_scan(inputs, l1f)
    g1r = host_gate_s1(inputs, l1f, y)
    g1 = host_moe(inputs, l1f, g1r)
    out_sh = host_combine(inputs, g1r["s1"], g1, x_list=x1)
    out = np.zeros((B, T, D), np.float32)
    for core in range(NCORES):
        b, j = divmod(core, 2)
        out[b, j * TLOC:(j + 1) * TLOC, :] = np.asarray(out_sh[core]).T
    return out
